# version 06
# baseline (speedup 1.0000x reference)
from contextlib import ExitStack

import numpy as np
import ml_dtypes

import concourse.bass as bass
import concourse.mybir as mybir
from concourse.bass_utils import run_bass_kernel_spmd

F32 = mybir.dt.float32
BF16 = mybir.dt.bfloat16
AF = mybir.ActivationFunctionType
ALU = mybir.AluOpType
AX = mybir.AxisListType

T = 4096
C = 1024
NT = T // 128
PAD = 8
PI = float(np.pi)
INV_2PI = float(1.0 / (2 * np.pi))
MAGIC = 1.5 * 2.0 ** 23
CW1 = 6.25
CW2 = 17.0 / 512.0
CW3 = float(float(np.float64(2 * np.pi)) - 6.25 - 17.0 / 512.0)
NEG_BIG = -3.0e38
CLEAR_SEMS = False
LAG_OPS = False
bf16 = ml_dtypes.bfloat16

ENGINES = ("sync", "tensor", "vector", "scalar", "gpsimd")



def _colmajor(v, p=128):
    v = np.ascontiguousarray(v)
    n = v.shape[0] // p
    return np.ascontiguousarray(v.reshape(n, p).T)


def stage_weights(inp):
    s = {}
    s["L0aa_convw"] = np.ascontiguousarray(
        inp["L0_attn_a_conv_w"].squeeze(1).reshape(8, 128, 4).transpose(1, 0, 2).reshape(128, 32))
    s["L0aa_convb"] = _colmajor(inp["L0_attn_a_conv_b"])
    s["L0aa_qkw"] = np.ascontiguousarray(inp["L0_attn_a_qk_w"].T)
    s["L0aa_vw"] = np.ascontiguousarray(inp["L0_attn_a_v_w"].T)
    s["L0aa_gw"] = np.ascontiguousarray(inp["L0_attn_a_g_w"].T)
    s["L0aa_gb"] = _colmajor(inp["L0_attn_a_g_b"])
    s["L0aa_ow"] = np.ascontiguousarray(inp["L0_attn_a_o_w"].T)
    s["L0aa_lng"] = inp["L0_attn_a_ln_g"].reshape(1, C)
    s["L0aa_lnb"] = inp["L0_attn_a_ln_b"].reshape(1, C)
    s["L0aa_freqs"] = np.ascontiguousarray(inp["L0_attn_a_freqs"].T)
    s["L0aa_gamma"] = inp["L0_attn_a_gamma"].reshape(1, 16)

    s["L0f_g1"] = inp["L0_ffn_g1"].reshape(1, C)
    s["L0f_g2"] = inp["L0_ffn_g2"].reshape(1, C)
    s["L0f_wq"] = inp["L0_ffn_wq"]
    s["L0f_wk"] = inp["L0_ffn_wk"]
    s["L0f_wv"] = inp["L0_ffn_wv"]
    s["L0f_wo"] = inp["L0_ffn_wo"]
    gp = np.zeros((1024, 128), np.float32); gp[:, 0:4] = inp["L0_ffn_gate"]
    s["L0f_gate"] = gp
    s["L0f_w1"] = inp["L0_ffn_w1"]
    s["L0f_w2"] = inp["L0_ffn_w2"]
    s["L0f_w3"] = inp["L0_ffn_w3"]

    s["L0ab_r1w"] = inp["L0_attn_b_r1_w"]
    s["L0ab_r1b"] = _colmajor(inp["L0_attn_b_r1_b"])
    s["L0ab_lng"] = _colmajor(inp["L0_attn_b_ln_g"])
    s["L0ab_lnb"] = _colmajor(inp["L0_attn_b_ln_b"])
    r2p = np.zeros((512, 128), np.float32); r2p[:, 0:6] = inp["L0_attn_b_r2_w"]
    s["L0ab_r2w"] = r2p
    s["L0ab_r2b"] = inp["L0_attn_b_r2_b"].reshape(1, 6)
    s["L0ab_temp"] = inp["L0_attn_b_temp"].reshape(1, 1)
    s["L0ab_rbias"] = inp["L0_attn_b_route_bias"].reshape(1, 6)
    s["L0ab_W1"] = inp["L0_attn_b_W1"]
    s["L0ab_b1"] = np.concatenate([_colmajor(inp["L0_attn_b_b1"][e]) for e in range(6)], axis=1)
    s["L0ab_W2"] = inp["L0_attn_b_W2"]
    s["L0ab_b2"] = inp["L0_attn_b_b2"].reshape(1, 6 * 1024)

    s["L1aa_kvd"] = inp["L1_attn_a_w_kv_down"]
    s["L1aa_kvu"] = inp["L1_attn_a_w_kv_up"]
    s["L1aa_wq"] = inp["L1_attn_a_w_q"]
    s["L1aa_wo"] = inp["L1_attn_a_w_out"]

    wq = inp["L1_attn_b_wq"]; wk = inp["L1_attn_b_wk"]; wv = inp["L1_attn_b_wv"]
    s["L1ab_wq"] = np.stack([np.concatenate([wq[2 * i], wq[2 * i + 1]], axis=1) for i in range(8)])
    s["L1ab_wk"] = np.stack([np.concatenate([wk[2 * i], wk[2 * i + 1]], axis=1) for i in range(8)])
    s["L1ab_wv"] = np.stack([np.concatenate([wv[2 * i], wv[2 * i + 1]], axis=1) for i in range(8)])
    def bias2(b):
        return np.stack([np.concatenate([b[2 * i, 0], b[2 * i + 1, 0]]) for i in range(8)], axis=1)
    s["L1ab_bq"] = bias2(inp["L1_attn_b_bq"])
    s["L1ab_bk"] = bias2(inp["L1_attn_b_bk"])
    s["L1ab_bv"] = bias2(inp["L1_attn_b_bv"])
    s["L1ab_wo"] = np.ascontiguousarray(inp["L1_attn_b_wo"].T)
    s["L1ab_bo"] = inp["L1_attn_b_bo"].reshape(1, C)

    s["L1f_wqkv"] = inp["L1_ffn_wqkv"]
    s["L1f_qs"] = np.ascontiguousarray(np.concatenate([inp["L1_ffn_q_scale"]] * 2, axis=1).T)
    s["L1f_ks"] = np.ascontiguousarray(np.concatenate([inp["L1_ffn_k_scale"]] * 2, axis=1).T)
    s["L1f_wo"] = inp["L1_ffn_wo"]
    s["L1f_lng"] = inp["L1_ffn_ln_g"]
    s["L1f_lnb"] = inp["L1_ffn_ln_b"]
    HID, HIDP = 2730, 2816
    w1 = inp["L1_ffn_w1"]
    w1a = np.zeros((2, 1024, HIDP), np.float32); w1a[:, :, :HID] = w1[:, :, :HID]
    w1g = np.zeros((2, 1024, HIDP), np.float32); w1g[:, :, :HID] = w1[:, :, HID:]
    s["L1f_w1a"] = w1a; s["L1f_w1g"] = w1g
    b1 = inp["L1_ffn_b1"]
    b1a = np.zeros((2, HIDP), np.float32); b1a[:, :HID] = b1[:, :HID]
    b1g = np.zeros((2, HIDP), np.float32); b1g[:, :HID] = b1[:, HID:]
    s["L1f_b1a"] = np.concatenate([_colmajor(b1a[l]) for l in range(2)], axis=1)
    s["L1f_b1g"] = np.concatenate([_colmajor(b1g[l]) for l in range(2)], axis=1)
    w2p = np.zeros((2, HIDP, 1024), np.float32); w2p[:, :HID, :] = inp["L1_ffn_w2"]
    s["L1f_w2"] = w2p
    s["L1f_b2"] = inp["L1_ffn_b2"]

    out = {}
    for k, v in s.items():
        v = np.ascontiguousarray(v, dtype=np.float32)
        if k in WCAST:
            v = v.reshape(-1, v.shape[-1])
        out[k] = v
    return out


def stage_consts():
    c = {}
    hd = 64
    t = np.arange(T, dtype=np.float32)
    inv_freq = (1.0 / 10000.0 ** (np.arange(0, hd, 2, dtype=np.float32) / np.float32(hd))).astype(np.float32)
    fr = (t[:, None] * inv_freq[None, :]).astype(np.float32)
    emb = np.concatenate([fr, fr], axis=-1)
    cos = np.cos(emb.astype(np.float64)).astype(np.float32)
    sin = np.sin(emb.astype(np.float64)).astype(np.float32)
    c["ropecos"] = np.ascontiguousarray(np.vstack([cos.T, cos.T]))
    c["ropesin"] = np.ascontiguousarray(np.vstack([sin.T, sin.T]))

    half = np.arange(0, hd, 2, dtype=np.float32)
    inv2 = (1.0 / 10000.0 ** (half / np.float32(hd))).astype(np.float32)
    fr2 = (t[:, None] * inv2[None, :]).astype(np.float32)
    fr2 = np.concatenate([fr2, fr2], axis=-1)
    base = ((half + np.float32(0.4 * hd)) / np.float32(1.4 * hd)).astype(np.float32)
    power = ((t - T // 2) / np.float32(256.0)).astype(np.float32)
    scale = (base[None, :] ** power[:, None]).astype(np.float32)
    scale = np.concatenate([scale, scale], axis=-1)
    xcos = np.cos(fr2.astype(np.float64)).astype(np.float32)
    xsin = np.sin(fr2.astype(np.float64)).astype(np.float32)
    for nm, tab in [("xposcsA", xcos * scale), ("xpossnA", xsin * scale),
                    ("xposcsB", xcos / scale), ("xpossnB", xsin / scale)]:
        tab = tab.astype(np.float32)
        c[nm] = np.ascontiguousarray(np.vstack([tab.T, tab.T]))

    R = np.zeros((hd, hd), np.float32)
    for d in range(32):
        R[d + 32, d] = -1.0
        R[d, d + 32] = 1.0
    R2 = np.zeros((128, 128), np.float32)
    R2[:64, :64] = R; R2[64:, 64:] = R
    c["rot2"] = R2

    ii = np.arange(128)
    c["madd"] = np.where(ii[None, :] <= ii[:, None], 0.0, NEG_BIG).astype(np.float32)
    cc = np.arange(640)
    ok = (cc[None, :] >= ii[:, None]) & (cc[None, :] <= ii[:, None] + 512)
    c["bandmask"] = np.where(ok, 0.0, NEG_BIG).astype(np.float32)
    c["trow"] = t.reshape(1, T).copy()
    rr = np.arange(128)[:, None]
    ccc = np.arange(128)[None, :]
    tri_le = np.where(rr <= ccc, 0.0, NEG_BIG).astype(np.float32)
    tri_ge = np.where(rr >= ccc, 0.0, NEG_BIG).astype(np.float32)
    zz = np.zeros((128, 128), np.float32)
    bb = np.full((128, 128), NEG_BIG, np.float32)
    for d in range(4):
        blocks = []
        for s in range(4):
            if s > d:
                blocks.append(zz)
            elif s == d:
                blocks.append(tri_le)
            else:
                blocks.append(bb)
        c[f"cmaskT_{d}"] = np.concatenate(blocks, axis=1)
    for k in range(8):
        dd = k - 4
        blocks = []
        for s in range(4):
            dl = dd - s
            if dl == 0:
                blocks.append(tri_le)
            elif dl == -4:
                blocks.append(tri_ge)
            elif -3 <= dl <= -1:
                blocks.append(zz)
            else:
                blocks.append(bb)
        c[f"lmaskT_{k}"] = np.concatenate(blocks, axis=1)
    ones2 = np.zeros((128, 2), np.float32); ones2[:64, 0] = 1.0; ones2[64:, 1] = 1.0
    c["ones2h"] = ones2
    sel2 = np.zeros((2, 128), np.float32); sel2[0, :64] = 1.0; sel2[1, 64:] = 1.0
    c["sel2h"] = sel2
    return c


WCAST = {
    "L0aa_qkw": (1024, 2048), "L0aa_vw": (1024, 2048), "L0aa_gw": (1024, 2048),
    "L0aa_ow": (2048, 1024),
    "L0f_wq": (1024, 1024), "L0f_wk": (1024, 256), "L0f_wv": (1024, 256),
    "L0f_wo": (1024, 1024), "L0f_gate": (1024, 128),
    "L0f_w1": (4, 1024, 3072), "L0f_w2": (4, 3072, 1024), "L0f_w3": (4, 1024, 3072),
    "L0ab_r1w": (1024, 512), "L0ab_r2w": (512, 128),
    "L0ab_W1": (6, 1024, 1536), "L0ab_W2": (6, 1536, 1024),
    "L1aa_kvd": (1024, 128), "L1aa_kvu": (128, 2048), "L1aa_wq": (1024, 1024),
    "L1aa_wo": (1024, 1024),
    "L1ab_wq": (8, 1024, 128), "L1ab_wk": (8, 1024, 128), "L1ab_wv": (8, 1024, 128),
    "L1ab_wo": (1024, 1024),
    "L1f_wqkv": (2, 1024, 1536), "L1f_wo": (2, 512, 1024),
    "L1f_w1a": (2, 1024, 2816), "L1f_w1g": (2, 1024, 2816), "L1f_w2": (2, 2816, 1024),
    "rot2": (128, 128), "ropecos": (128, T), "ropesin": (128, T),
    "xposcsA": (128, T), "xpossnA": (128, T), "xposcsB": (128, T), "xpossnB": (128, T),
}
WCAST["L0ab_r2w"] = (512, 128)



def mk_tiles(nc, stack):
    def sb(name, shape, dt=F32):
        return stack.enter_context(nc.sbuf_tensor(name, shape, dt))
    def ps(name, shape, dt=F32):
        return stack.enter_context(nc.psum_tensor(name, shape, dt))
    return sb, ps


class KB:
    def __init__(self, nc):
        self.nc = nc
        self.dram = {}
        self._sems = {}
        self._retcnt = {}
        self.pgscr = nc.alloc_sbuf_tensor("kb_pgscr", [128, 8], F32)

    def ret(self, e, ins):
        key = str(e.engine)
        sem = self.sem("_ret_" + key)
        self._retcnt[key] = self._retcnt.get(key, 0) + 1
        ins.then_inc(sem, 1)
        e.wait_ge(sem, self._retcnt[key])
        return ins

    def pg(self, e, n=3):
        scr = self.pgscr
        key = str(e.engine)
        if "DVE" in key:
            for _ in range(n):
                self.ret(e, e.tensor_copy(scr[:, 0:1], scr[:, 1:2]))
        elif "Activation" in key:
            for _ in range(n):
                self.ret(e, e.copy(scr[:, 2:3], scr[:, 3:4]))
        else:
            for _ in range(n):
                self.ret(e, e.memset(scr[:, 4:5], 0.0))

    def lagop(self, e):
        scr = self.pgscr
        key = str(e.engine)
        if "DVE" in key:
            return e.tensor_copy(scr[:, 0:1], scr[:, 1:2])
        if "Activation" in key:
            return e.copy(scr[:, 2:3], scr[:, 3:4])
        return e.memset(scr[:, 4:5], 0.0)

    def sem(self, name):
        if name not in self._sems:
            self._sems[name] = self.nc.ctx.enter_context(self.nc.semaphore(name))
        return self._sems[name]

    def dram_t(self, name, shape, dt=F32, kind="Internal"):
        ap = self.nc.dram_tensor(name, list(shape), dt, kind=kind).ap()
        self.dram[name] = ap
        return ap


class Sched:

    def __init__(self, kb, tag):
        self.kb = kb
        self.tag = tag
        self.ops = {e: [] for e in ENGINES}
        self.base = {}
        self.pe_sems = set()

    def s(self, name):
        if name not in self.base:
            if not hasattr(self.kb, "gcnt"):
                self.kb.gcnt = {}
            phys = f"p{len(self.base):02d}_{id(self) % 7}" if False else f"p{len(self.base):02d}"
            self.base[name] = (phys, self.kb.gcnt.get(phys, 0))
        return self.base[name][0]

    def cur(self, sem):
        self.s(sem)
        phys, b = self.base[sem]
        return self.kb.gcnt.get(phys, 0) - b

    def add(self, eng, fn, waits=(), inc=None, inc_by=1, dma=False):
        if eng == "tensor" and inc is not None:
            self.pe_sems.add(inc)
        if (not dma and eng in ("vector", "scalar", "gpsimd")
                and any(sm in self.pe_sems for sm, v in waits if v > 0)):
            inner = fn
            def fn(e, inner=inner):
                self.kb.pg(e)
                return inner(e)
        if LAG_OPS and not dma and inc is not None and eng in ("vector", "scalar", "gpsimd"):
            inner2 = fn
            def fn(e, inner2=inner2):
                ins = inner2(e)
                self.kb.ret(e, ins)
                return self.kb.lagop(e)
        w = []
        for sm, v in waits:
            if v > 0:
                self.s(sm)
                phys, b = self.base[sm]
                w.append((phys, b + v))
        w = tuple(w)
        tgt = None
        if inc is not None:
            self.s(inc)
            phys, b = self.base[inc]
            self.kb.gcnt[phys] = self.kb.gcnt.get(phys, 0) + inc_by
            tgt = (phys, inc_by)
        self.ops[eng].append((fn, w, tgt))

    def dma(self, eng, sem, dst, src, waits=(), **kw):
        def fn(e):
            return e.dma_start(dst, src, **kw)
        self.add(eng, fn, waits, inc=sem, inc_by=16, dma=True)

    def emit(self):
        kb = self.kb
        active = [e for e in ENGINES if self.ops[e]]
        with kb.nc.Block() as blk:
            for engname in active:
                ops = self.ops[engname]

                def body(e, ops=ops):
                    for fn, waits, tgt in ops:
                        for sname, v in waits:
                            e.wait_ge(kb.sem(sname), v)
                        ins = fn(e)
                        if tgt is not None:
                            sname, by = tgt
                            ins.then_inc(kb.sem(sname), by)
                getattr(blk, engname)(body)
        if CLEAR_SEMS:
            used = sorted({phys for (phys, b) in self.base.values()})
            if used:
                with kb.nc.Block() as blk2:
                    @blk2.gpsimd
                    def _(e):
                        for p in used:
                            e.sem_clear(kb.sem(p))
                for p in used:
                    kb.gcnt[p] = 0

    def fence(self, eng, sem):
        tgt = self.cur(sem)
        self.add(eng, lambda e: e.nop(), waits=[(sem, tgt)])

    def sw(self, sem, slot):
        nm = f"{sem}{slot}"
        return (nm, self.cur(nm))

    def mm(self, fn, waits=(), inc=None):
        self.add("tensor", fn, waits, inc)

    def dve(self, fn, waits=(), inc=None):
        self.add("vector", fn, waits, inc)

    def act(self, fn, waits=(), inc=None):
        self.add("scalar", fn, waits, inc)

    def pool(self, fn, waits=(), inc=None):
        self.add("gpsimd", fn, waits, inc)



def emit_init(kb, g):
    nc = kb.nc
    sc = Sched(kb, "ini")
    sc.pool(lambda e: e.memset(kb.pgscr[:], 0.0))
    def mk_ident(e, t="ident"):
        kb.ret(e, e.memset(g[t][:], 1.0))
        return e.affine_select(g[t][:], g[t][:], [[1, 128]],
                               ALU.is_equal, 0.0, channel_multiplier=-1)
    sc.pool(mk_ident)
    sc.pool(lambda e: mk_ident(e, "identf"))
    sc.pool(lambda e: e.memset(g["ones_r"][:], 1.0))
    sc.pool(lambda e: e.memset(g["ones_rb"][:], 1.0))
    sc.pool(lambda e: e.memset(g["ones_c"][:], 1.0))
    sc.pool(lambda e: e.memset(g["pihalf"][:], PI / 2))
    sc.pool(lambda e: e.memset(g["onesT"][:], 1.0))
    sc.pool(lambda e: e.memset(g["eps_ln"][:], 1e-5))
    sc.pool(lambda e: e.memset(g["eps_rms"][:], float(2.0 ** -23)))
    sc.pool(lambda e: e.memset(g["zero_c"][:], 0.0))
    sc.emit()


def alloc_globals(nc):
    g = {}
    g["ident"] = nc.alloc_sbuf_tensor("g_ident", [128, 128], BF16)
    g["identf"] = nc.alloc_sbuf_tensor("g_identf", [128, 128], F32)
    g["ones_r"] = nc.alloc_sbuf_tensor("g_ones_r", [1, 128], F32)
    g["ones_rb"] = nc.alloc_sbuf_tensor("g_ones_rb", [1, 128], BF16)
    g["ones_c"] = nc.alloc_sbuf_tensor("g_ones_c", [128, 1], F32)
    g["pihalf"] = nc.alloc_sbuf_tensor("g_pihalf", [128, 1], F32)
    g["onesT"] = nc.alloc_sbuf_tensor("g_onesT", [128, 2048], F32)
    g["eps_ln"] = nc.alloc_sbuf_tensor("g_eps_ln", [128, 1], F32)
    g["eps_rms"] = nc.alloc_sbuf_tensor("g_eps_rms", [128, 1], F32)
    g["zero_c"] = nc.alloc_sbuf_tensor("g_zero_c", [128, 1], F32)
    return g



def declare_bf_weights(kb, names):
    for nm in names:
        shp = WCAST[nm]
        flat = (int(np.prod(shp[:-1])), shp[-1])
        kb.dram_t("bf_" + nm, flat, BF16)


def emit_wcast(kb, names):
    nc = kb.nc
    jobs = []
    for nm in names:
        shp = WCAST[nm]
        rows = int(np.prod(shp[:-1]))
        cols = shp[-1]
        assert rows % 128 == 0
        for r in range(rows // 128):
            c0 = 0
            while c0 < cols:
                cw = min(4096, cols - c0)
                jobs.append((nm, r, cw, c0))
                c0 += cw
    with (
        nc.sbuf_tensor("wc_f", [128, 2 * 4096], F32) as tf,
        nc.sbuf_tensor("wc_b", [128, 2 * 4096], BF16) as tb,
    ):
        sc = Sched(kb, "wc")
        for i, (nm, r, cw, c0) in enumerate(jobs):
            b = (i % 2) * 4096
            sl = i % 2
            src = kb.dram[nm].rearrange("(n p) m -> n p m", p=128)[r, :, c0:c0 + cw]
            dst = kb.dram["bf_" + nm].rearrange("(n p) m -> n p m", p=128)[r, :, c0:c0 + cw]
            sc.dma("sync", f"ld{sl}", tf[:, b:b + cw], src,
                   waits=[("cp", i - 1)])
            stw = sc.sw("st", sl)
            sc.dve(lambda e, tb=tb, tf=tf, b=b, cw=cw: e.tensor_copy(tb[:, b:b + cw], tf[:, b:b + cw]),
                   waits=[sc.sw("ld", sl), stw], inc="cp")
            sc.dma("gpsimd", f"st{sl}", dst, tb[:, b:b + cw], waits=[("cp", i + 1)])
        sc.fence("gpsimd", "st0")
        sc.fence("gpsimd", "st1")
        sc.emit()
    return len(jobs)



def emit_norm(kb, g, tag, src_name, dst_fm=None, dst_tm=None, kind="ln",
              gain_row=None):
    nc = kb.nc
    src = kb.dram[src_name]
    eps = g["eps_ln"] if kind == "ln" else g["eps_rms"]
    with (
        nc.sbuf_tensor(f"{tag}_x", [128, 2 * C], F32) as xt,
        nc.sbuf_tensor(f"{tag}_c", [128, C], F32) as cent,
        nc.sbuf_tensor(f"{tag}_b", [128, 2 * C], BF16) as xbf,
        nc.sbuf_tensor(f"{tag}_q", [128, C], F32) as sq,
        nc.sbuf_tensor(f"{tag}_s", [128, 8], F32) as st,
        nc.sbuf_tensor(f"{tag}_gr", [1, C], F32) as grow,
        nc.sbuf_tensor(f"{tag}_gb", [128, C], F32) as gaint,
        nc.sbuf_tensor(f"{tag}_fs", [128, 2 * 4096], BF16) as fms,
        nc.sbuf_tensor(f"{tag}_ts", [128, 2 * C], F32) as tms,
        nc.psum_tensor(f"{tag}_pt", [128, 2048], BF16) as pst,
        nc.psum_tensor(f"{tag}_pg", [128, C], F32) as psg,
    ):
        do_fm = dst_fm is not None
        do_tm = dst_tm is not None

        if gain_row is not None:
            sc0 = Sched(kb, tag + "0")
            sc0.dma("sync", "ld", grow[:], kb.dram[gain_row])
            def gbc(e):
                e.matmul(psg[:, 0:512], g["ones_r"][:], grow[0:1, 0:512],
                         start=True, stop=True)
                return e.matmul(psg[:, 512:1024], g["ones_r"][:], grow[0:1, 512:1024],
                                start=True, stop=True)
            sc0.mm(gbc, waits=[("ld", 16)], inc="bc")
            sc0.act(lambda e: e.copy(gaint[:], psg[:]), waits=[("bc", 1)])
            sc0.emit()

        sc = Sched(kb, tag)
        for i in range(NT):
            b = (i % 2) * C
            sl = i % 2
            xv = xt[:, b:b + C]
            sc.dma("sync", f"ld{sl}", xv, src[i * 128:(i + 1) * 128, :],
                   waits=[("dv", i - 1)])
            def stats(e, xv=xv, i=i, kind=kind):
                R = kb.ret
                if kind == "ln":
                    R(e, e.reduce_sum(st[:, 0:1], xv, AX.X))
                    R(e, e.tensor_scalar(st[:, 1:2], st[:, 0:1], 1.0 / C, None, ALU.mult))
                    e.tensor_scalar(cent[:], xv, st[:, 1:2], None, ALU.subtract)
                    e.tensor_mul(sq[:], cent[:], cent[:])
                    R(e, e.reduce_sum(st[:, 2:3], sq[:], AX.X))
                    return e.tensor_scalar(st[:, 3:4], st[:, 2:3], 1.0 / C, None, ALU.mult)
                else:
                    e.tensor_mul(sq[:], xv, xv)
                    R(e, e.reduce_sum(st[:, 2:3], sq[:], AX.X))
                    return e.tensor_scalar(st[:, 3:4], st[:, 2:3], 1.0 / C, None, ALU.mult)
            sc.dve(stats, waits=[sc.sw("ld", sl)], inc="va")
            def a_rstd(e):
                kb.ret(e, e.activation(st[:, 4:5], st[:, 3:4], AF.Ln,
                                       bias=eps[:], scale=1.0))
                return e.activation(st[:, 5:6], st[:, 4:5], AF.Exp, scale=-0.5)
            sc.act(a_rstd, waits=[("va", i + 1)], inc="sq")

            def norm(e, i=i, b=b, xv=xv):
                srcv = cent[:] if kind == "ln" else xv
                if gain_row is not None:
                    e.tensor_scalar(sq[:], srcv, st[:, 5:6], None, ALU.mult)
                    ins = e.tensor_tensor(xbf[:, b:b + C], sq[:], gaint[:], ALU.mult)
                    if do_tm:
                        ins = e.tensor_tensor(tms[:, b:b + C], sq[:], gaint[:], ALU.mult)
                else:
                    ins = e.tensor_scalar(xbf[:, b:b + C], srcv, st[:, 5:6], None, ALU.mult)
                    if do_tm:
                        ins = e.tensor_scalar(tms[:, b:b + C], srcv, st[:, 5:6], None, ALU.mult)
                return ins
            nwaits = [("sq", i + 1)]
            if do_tm:
                nwaits.append(sc.sw("tm", sl))
            if do_fm:
                nwaits.append(("pe", i - 1))
            sc.dve(norm, waits=nwaits, inc="dv")
            if do_tm:
                sc.dma("gpsimd", f"tm{sl}", kb.dram[dst_tm][i * 128:(i + 1) * 128, :],
                       tms[:, b:b + C], waits=[("dv", i + 1)])
            if do_fm:
                gi = i // 4
                ti = i % 4
                pb = (i % 2) * 1024
                def trs(e, b=b, pb=pb):
                    for j in range(8):
                        mm = e.transpose(pst[:, pb + j * 128: pb + j * 128 + 128],
                                         xbf[:, b + j * 128: b + j * 128 + 128],
                                         g["ident"][:])
                    return mm
                sc.mm(trs, waits=[("dv", i + 1), ("ev", i - 1)], inc="pe")
                half = (gi % 2) * 4096
                def evac(e, half=half, ti=ti, pb=pb):
                    dst = fms[:, half:half + 4096].rearrange(
                        "p (j f) -> p j f", f=512)[:, :, ti * 128:ti * 128 + 128]
                    return e.tensor_copy(
                        dst, pst[:, pb:pb + 1024].rearrange("p (j f) -> p j f", f=128))
                sc.dve(evac, waits=[("pe", i + 1), sc.sw("fm", gi % 2)], inc="ev")
                if ti == 3:
                    for j in range(8):
                        sc.dma("gpsimd", f"fm{gi % 2}",
                               kb.dram[dst_fm][j * 128:(j + 1) * 128,
                                               PAD + gi * 512: PAD + (gi + 1) * 512],
                               fms[:, half + j * 512: half + j * 512 + 512],
                               waits=[("ev", i + 1)])
        if do_fm:
            sc.fence("gpsimd", "fm0")
            sc.fence("gpsimd", "fm1")
        if do_tm:
            sc.fence("gpsimd", "tm0")
            sc.fence("gpsimd", "tm1")
        sc.emit()



def emit_zero_pads(kb, fm_names):
    nc = kb.nc
    with nc.sbuf_tensor("zp", [128, PAD], BF16) as zp:
        sc = Sched(kb, "zp")
        sc.pool(lambda e: e.memset(zp[:], 0.0), inc="ms")
        for nm in fm_names:
            rows = kb.dram[nm].shape[0]
            for j in range(rows // 128):
                sc.dma("sync", "st", kb.dram[nm][j * 128:(j + 1) * 128, 0:PAD], zp[:],
                       waits=[("ms", 1)])
        sc.fence("sync", "st")
        sc.emit()



def emit_l0aa_conv(kb, g, src_fm, dst_fm):
    nc = kb.nc
    with (
        nc.sbuf_tensor("cv_x", [128, 2 * 1027], BF16) as xt,
        nc.sbuf_tensor("cv_a", [128, 1024], F32) as acc,
        nc.sbuf_tensor("cv_o", [128, 2 * 1024], BF16) as oc,
        nc.sbuf_tensor("cv_w", [128, 32], F32) as wt,
        nc.sbuf_tensor("cv_b", [128, 8], F32) as bt,
    ):
        sc = Sched(kb, "cv")
        sc.dma("sync", "ld", wt[:], kb.dram["L0aa_convw"])
        sc.dma("sync", "ld", bt[:], kb.dram["L0aa_convb"])
        u = 0
        for j in range(8):
            for q in range(4):
                b = (u % 2) * 1027
                ob = (u % 2) * 1024
                sc.dma("sync", f"ldt{u % 2}",
                       xt[:, b:b + 1027],
                       kb.dram[src_fm][j * 128:(j + 1) * 128,
                                       PAD + q * 1024 - 3: PAD + q * 1024 + 1024],
                       waits=[("dv", u - 1)])

                def comp(e, b=b, ob=ob, j=j, u=u):
                    e.tensor_scalar(acc[:], xt[:, b + 3:b + 1027],
                                    wt[:, j * 4 + 3:j * 4 + 4], bt[:, j:j + 1],
                                    ALU.mult, ALU.add)
                    for tap in (2, 1, 0):
                        e.scalar_tensor_tensor(acc[:], xt[:, b + tap:b + tap + 1024],
                                               wt[:, j * 4 + tap:j * 4 + tap + 1],
                                               acc[:], ALU.mult, ALU.add)
                    return e.tensor_copy(oc[:, ob:ob + 1024], acc[:])
                sc.dve(comp, waits=[("ld", 32), sc.sw("ldt", u % 2),
                                    sc.sw("st", u % 2)], inc="dv")
                sc.dma("gpsimd", f"st{u % 2}",
                       kb.dram[dst_fm][j * 128:(j + 1) * 128,
                                       PAD + q * 1024: PAD + (q + 1) * 1024],
                       oc[:, ob:ob + 1024], waits=[("dv", u + 1)])
                u += 1
        sc.fence("gpsimd", "st0")
        sc.fence("gpsimd", "st1")
        sc.emit()



def emit_l0aa_retnet(kb, g, xc_fm, attout):
    nc = kb.nc
    with ExitStack() as _st:
        sb, ps = mk_tiles(nc, _st)
        xcs = sb("rn_xc", [128, 8 * 4096], BF16)
        whd = sb("rn_w", [128, 3072], BF16)
        t_b = sb("rn_tb", [128, 2 * 2048])
        trow = sb("rn_tr", [1, T])
        qk = sb("rn_qk", [128, 2048])
        vv = sb("rn_v", [128, 2048])
        gg = sb("rn_g", [128, 2048])
        sqb = sb("rn_sq", [128, 2048])
        cosb = sb("rn_cos", [128, 2048])
        tA = sb("rn_tA", [128, 2048])
        tB = sb("rn_tB", [128, 2048])
        decb = sb("rn_de", [128, 2048])
        outst = sb("rn_os", [128, 2 * 2048], BF16)
        freqs = sb("rn_fq", [128, 16])
        gbs = sb("rn_gb", [128, 16])
        gmrow = sb("rn_gm", [1, 16])
        lngr = sb("rn_ln", [1, 16])
        rowA = sb("rn_rA", [1, 2048])
        rowB = sb("rn_rB", [1, 2048])
        qcar = sb("rn_qc", [128, 1])
        ccar = sb("rn_cc", [128, 1])
        psA = ps("rn_pA", [128, 2048])
        psB = ps("rn_pB", [128, 2048])

        s0 = Sched(kb, "rn0")
        for c in range(8):
            s0.dma("sync", "ld", xcs[:, c * 4096:(c + 1) * 4096],
                   kb.dram[xc_fm][c * 128:(c + 1) * 128, PAD:PAD + T])
        s0.dma("sync", "ld", freqs[:], kb.dram["L0aa_freqs"])
        s0.dma("sync", "ld", gbs[:], kb.dram["L0aa_gb"])
        s0.dma("sync", "ld", gmrow[:], kb.dram["L0aa_gamma"])
        s0.dma("sync", "ld", trow[:], kb.dram["trow"])
        nld = s0.cur("ld")
        s0.act(lambda e: e.activation(lngr[:], gmrow[:], AF.Ln), waits=[("ld", nld)])
        for hf in range(2):
            def tbmm(e, hf=hf):
                for tc in range(4):
                    mm = e.matmul(psB[:, tc * 512:(tc + 1) * 512], g["ones_r"][:],
                                  trow[0:1, hf * 2048 + tc * 512: hf * 2048 + (tc + 1) * 512],
                                  start=True, stop=True)
                return mm
            s0.mm(tbmm, waits=[("ld", nld), ("tb", hf)], inc="pe")
            s0.act(lambda e, hf=hf: e.copy(t_b[:, hf * 2048:(hf + 1) * 2048], psB[:]),
                   waits=[("pe", hf + 1)], inc="tb")
        s0.emit()

        sc = Sched(kb, "rn")
        HD = 128

        for h in range(16):
            for wi, wnm in enumerate(["bf_L0aa_qkw", "bf_L0aa_vw", "bf_L0aa_gw"]):
                src = kb.dram[wnm].rearrange("(c p) f -> p c f", p=128)[:, :, h * 128:(h + 1) * 128]
                sc.dma("sync", "wld", whd[:, wi * 1024:(wi + 1) * 1024].rearrange(
                    "p (c f) -> p c f", c=8), src,
                    waits=[("peA", 3 * (2 * h - 1) + 3 if h > 0 else 0)])
            for hf in range(2):
                u = 2 * h + hf
                tcol = slice(hf * 2048, (hf + 1) * 2048)

                def p_qk(e, u=u, hf=hf):
                    for tc in range(4):
                        for cch in range(8):
                            mm = e.matmul(
                                psA[:, tc * 512:(tc + 1) * 512],
                                whd[:, 0 + cch * 128: 0 + cch * 128 + 128],
                                xcs[:, cch * 4096 + hf * 2048 + tc * 512:
                                        cch * 4096 + hf * 2048 + (tc + 1) * 512],
                                start=(cch == 0), stop=(cch == 7))
                    return mm
                sc.mm(p_qk, waits=[("acA", 3 * u), ("wld", 48 * (h + 1))], inc="peA")
                sc.act(lambda e: e.copy(qk[:], psA[:]), waits=[("peA", 3 * u + 1), ("dvG", u)], inc="acA")
                sc.dve(lambda e: e.tensor_mul(sqb[:], qk[:], qk[:]),
                       waits=[("acA", 3 * u + 1)], inc="dvA")
                def p_rs(e):
                    for tc in range(4):
                        mm = e.matmul(psB[0:1, tc * 512:(tc + 1) * 512], g["ones_c"][:],
                                      sqb[:, tc * 512:(tc + 1) * 512], start=True, stop=True)
                    return mm
                sc.mm(p_rs, waits=[("dvA", u + 1), ("dvG", u)], inc="peB")
                def a_rnrm(e):
                    kb.ret(e, e.activation(rowA[:], psB[0:1, :], AF.Ln))
                    return e.activation(rowA[:], rowA[:], AF.Exp, scale=-0.5)
                sc.act(a_rnrm, waits=[("peB", 4 * u + 1), ("dvG", u)], inc="acB")
                def d_nrm(e):
                    return e.tensor_scalar(rowA[:], rowA[:], 1e12, None, ALU.min)
                sc.dve(d_nrm, waits=[("acB", u + 1)], inc="dvB")
                def p_bc1(e):
                    for tc in range(4):
                        mm = e.matmul(psB[:, tc * 512:(tc + 1) * 512], g["ones_r"][:],
                                      rowA[0:1, tc * 512:(tc + 1) * 512], start=True, stop=True)
                    return mm
                sc.mm(p_bc1, waits=[("dvB", u + 1), ("acB", u + 1)], inc="peB")
                import os as _os
                if _os.environ.get("RETNET_DUMP", "") == "qkraw":
                    sc.dve(lambda e: e.tensor_copy(tA[:], psB[:]),
                           waits=[("peB", 4 * u + 2)], inc="dvC")
                else:
                    sc.dve(lambda e: e.tensor_tensor(qk[:], qk[:], psB[:], ALU.mult),
                           waits=[("peB", 4 * u + 2)], inc="dvC")
                def p_v(e, hf=hf):
                    for tc in range(4):
                        for cch in range(8):
                            mm = e.matmul(
                                psA[:, tc * 512:(tc + 1) * 512],
                                whd[:, 1024 + cch * 128: 1024 + cch * 128 + 128],
                                xcs[:, cch * 4096 + hf * 2048 + tc * 512:
                                        cch * 4096 + hf * 2048 + (tc + 1) * 512],
                                start=(cch == 0), stop=(cch == 7))
                    return mm
                sc.mm(p_v, waits=[("acA", 3 * u + 1)], inc="peA")
                sc.act(lambda e: e.copy(vv[:], psA[:]), waits=[("peA", 3 * u + 2), ("dvG", u)], inc="acA")
                def p_g(e, hf=hf):
                    for tc in range(4):
                        for cch in range(8):
                            mm = e.matmul(
                                psA[:, tc * 512:(tc + 1) * 512],
                                whd[:, 2048 + cch * 128: 2048 + cch * 128 + 128],
                                xcs[:, cch * 4096 + hf * 2048 + tc * 512:
                                        cch * 4096 + hf * 2048 + (tc + 1) * 512],
                                start=(cch == 0), stop=(cch == 7))
                    return mm
                sc.mm(p_g, waits=[("acA", 3 * u + 2)], inc="peA")
                sc.act(lambda e, h=h: e.activation(gg[:], psA[:], AF.Sigmoid,
                                                   bias=gbs[:, h:h + 1]),
                       waits=[("peA", 3 * u + 3), ("dvG", u)], inc="acA")
                def d_ang(e, h=h, hf=hf):
                    e.tensor_scalar(tA[:], t_b[:, hf * 2048:(hf + 1) * 2048],
                                    freqs[:, h:h + 1], None, ALU.mult)
                    e.tensor_scalar(tB[:], tA[:], INV_2PI, MAGIC, ALU.mult, ALU.add)
                    e.tensor_scalar(tB[:], tB[:], -MAGIC, None, ALU.add)
                    e.scalar_tensor_tensor(tA[:], tB[:], -CW1, tA[:], ALU.mult, ALU.add)
                    e.scalar_tensor_tensor(tA[:], tB[:], -CW2, tA[:], ALU.mult, ALU.add)
                    e.scalar_tensor_tensor(tA[:], tB[:], -CW3, tA[:], ALU.mult, ALU.add)
                    e.tensor_scalar(tB[:], tA[:], -1.0, None, ALU.mult)
                    return e.tensor_max(tB[:], tB[:], tA[:])
                sc.dve(d_ang, waits=[("acC", u)], inc="dvE")
                sc.act(lambda e: e.activation(cosb[:], tB[:], AF.Sin,
                                              bias=g["pihalf"][:], scale=-1.0),
                       waits=[("dvE", u + 1), ("dvG", u)], inc="acC")
                def d_t1(e, h=h, hf=hf):
                    return e.tensor_scalar(rowB[:], trow[0:1, hf * 2048:(hf + 1) * 2048],
                                           lngr[0:1, h:h + 1], None, ALU.mult)
                sc.dve(d_t1, waits=[("acD", u), ("peB", 4 * u - 1)], inc="dvF")
                sc.act(lambda e: e.activation(rowB[:], rowB[:], AF.Exp),
                       waits=[("dvF", u + 1), ("dvD", u)], inc="acD")
                def d_rd(e):
                    kb.ret(e, e.tensor_scalar(rowA[:], rowB[:], 1e-6, None, ALU.add))
                    return e.reciprocal(rowA[:], rowA[:])
                sc.dve(d_rd, waits=[("acD", u + 1), ("peB", 4 * u + 2)], inc="dvD")
                def p_bc2(e):
                    for tc in range(4):
                        mm = e.matmul(psB[:, tc * 512:(tc + 1) * 512], g["ones_r"][:],
                                      rowB[0:1, tc * 512:(tc + 1) * 512], start=True, stop=True)
                    return mm
                sc.mm(p_bc2, waits=[("acD", u + 1), ("dvC", u + 1)], inc="peB")
                sc.act(lambda e: e.copy(decb[:], psB[:]), waits=[("peB", 4 * u + 3), ("dvG", u)], inc="acE")
                def p_bc3(e):
                    for tc in range(4):
                        mm = e.matmul(psB[:, tc * 512:(tc + 1) * 512], g["ones_r"][:],
                                      rowA[0:1, tc * 512:(tc + 1) * 512], start=True, stop=True)
                    return mm
                sc.mm(p_bc3, waits=[("dvD", u + 1), ("acE", u + 1)], inc="peB")
                def d_ksh(e, hf=hf):
                    e.tensor_copy(tB[:, 1:2048], qk[:, 0:2047])
                    if hf == 0:
                        kb.ret(e, e.tensor_copy(tB[:, 0:1], g["zero_c"][:]))
                    else:
                        kb.ret(e, e.tensor_copy(tB[:, 0:1], qcar[:]))
                    return kb.ret(e, e.tensor_copy(qcar[:], qk[:, 2047:2048]))
                sc.dve(d_ksh, waits=[("acC", u + 1)], inc=None)
                def d_state(e):
                    e.tensor_add(tA[:], cosb[:], tB[:])
                    e.tensor_mul(tA[:], vv[:], tA[:])
                    e.tensor_mul(tA[:], tA[:], gg[:])
                    return e.tensor_mul(tA[:], tA[:], decb[:])
                sc.dve(d_state, waits=[("acC", u + 1), ("acA", 3 * u + 3), ("acE", u + 1)],
                       inc=None)
                def d_scan(e, hf=hf):
                    init = g["zero_c"][:] if hf == 0 else ccar[:]
                    kb.ret(e, e.tensor_tensor_scan(sqb[:], g["onesT"][:], tA[:], init,
                                                   ALU.mult, ALU.add))
                    return kb.ret(e, e.tensor_copy(ccar[:], sqb[:, 2047:2048]))
                sc.dve(d_scan, waits=[], inc=None)
                ob = (u % 2) * 2048
                import os as _os
                _dump = _os.environ.get("RETNET_DUMP", "")
                def d_out(e, ob=ob, _dump=_dump):
                    e.tensor_tensor(tB[:], sqb[:], psB[:], ALU.mult)
                    if _dump:
                        srcmap = {"q": qk, "qkraw": qk, "rnrmb": tA, "v": vv, "g": gg, "cos": cosb,
                                  "sd": tA, "cs": sqb, "dec": decb}
                        return e.tensor_copy(outst[:, ob:ob + 2048], srcmap[_dump][:])
                    return e.tensor_tensor(outst[:, ob:ob + 2048], tB[:], qk[:], ALU.mult)
                sc.dve(d_out, waits=[("peB", 4 * u + 4), sc.sw("st", u % 2)], inc="dvG")
                sc.dma("gpsimd", f"st{u % 2}",
                       kb.dram[attout][h * 128:(h + 1) * 128, hf * 2048:(hf + 1) * 2048],
                       outst[:, ob:ob + 2048], waits=[("dvG", u + 1)])
        sc.fence("gpsimd", "st0")
        sc.fence("gpsimd", "st1")
        sc.emit()



def emit_l0aa_proj(kb, g, attout, src, dst):
    nc = kb.nc
    with (
        nc.sbuf_tensor("pj_w", [128, 16 * 1024], BF16) as ows,
        nc.sbuf_tensor("pj_a", [128, 2 * 2048], BF16) as att,
        nc.sbuf_tensor("pj_y", [128, 2 * 1024], F32) as ysb,
        nc.sbuf_tensor("pj_x", [128, 2 * 1024], F32) as xsb,
        nc.sbuf_tensor("pj_o", [128, 2 * 1024], F32) as osb,
        nc.sbuf_tensor("pj_q", [128, 1024], F32) as sq,
        nc.sbuf_tensor("pj_s", [128, 8], F32) as st,
        nc.sbuf_tensor("pj_gr", [1, 2 * C], F32) as grow,
        nc.sbuf_tensor("pj_gb", [128, 2 * C], F32) as gbt,
        nc.psum_tensor("pj_p", [128, 2 * 1024], F32) as psY,
        nc.psum_tensor("pj_pg", [128, 2 * 1024], F32) as psG,
    ):
        s0 = Sched(kb, "pj0")
        s0.dma("sync", "ld", ows[:].rearrange("p (c f) -> p c f", c=16),
               kb.dram["bf_L0aa_ow"].rearrange("(c p) f -> p c f", p=128))
        s0.dma("sync", "ld", grow[0:1, 0:C], kb.dram["L0aa_lng"])
        s0.dma("sync", "ld", grow[0:1, C:2 * C], kb.dram["L0aa_lnb"])
        def bc(e):
            e.matmul(psG[:, 0:512], g["ones_r"][:], grow[0:1, 0:512], start=True, stop=True)
            e.matmul(psG[:, 512:1024], g["ones_r"][:], grow[0:1, 512:1024], start=True, stop=True)
            e.matmul(psG[:, 1024:1536], g["ones_r"][:], grow[0:1, 1024:1536], start=True, stop=True)
            return e.matmul(psG[:, 1536:2048], g["ones_r"][:], grow[0:1, 1536:2048], start=True, stop=True)
        s0.mm(bc, waits=[("ld", 48)], inc="pe")
        s0.act(lambda e: e.copy(gbt[:], psG[:]), waits=[("pe", 1)])
        s0.emit()

        sc = Sched(kb, "pj")
        for i in range(NT):
            ab = (i % 2) * 2048
            yb = (i % 2) * 1024
            sl = i % 2
            sc.dma("sync", f"ld{sl}", att[:, ab:ab + 2048].rearrange("p (c f) -> p c f", c=16),
                   kb.dram[attout][:, i * 128:(i + 1) * 128].rearrange("(c p) f -> p c f", p=128),
                   waits=[("pe", i - 1)])
            sc.dma("sync", f"ldx{sl}", xsb[:, yb:yb + 1024], kb.dram[src][i * 128:(i + 1) * 128, :],
                   waits=[("dv", i - 1)])

            def p_mm(e, ab=ab, yb=yb):
                for half in range(2):
                    for cch in range(16):
                        mm = e.matmul(psY[:, yb + half * 512: yb + half * 512 + 512],
                                      att[:, ab + cch * 128: ab + cch * 128 + 128],
                                      ows[:, cch * 1024 + half * 512: cch * 1024 + half * 512 + 512],
                                      start=(cch == 0), stop=(cch == 15))
                return mm
            sc.mm(p_mm, waits=[sc.sw("ld", sl), ("ac", 2 * i - 3)], inc="pe")
            sc.act(lambda e, yb=yb: e.copy(ysb[:, yb:yb + 1024], psY[:, yb:yb + 1024]),
                   waits=[("pe", i + 1)], inc="ac")

            def d_ln(e, yb=yb, i=i):
                R = kb.ret
                yv = ysb[:, yb:yb + 1024]
                R(e, e.reduce_sum(st[:, 0:1], yv, AX.X))
                R(e, e.tensor_scalar(st[:, 1:2], st[:, 0:1], 1.0 / C, None, ALU.mult))
                e.tensor_scalar(osb[:, yb:yb + 1024], yv, st[:, 1:2], None, ALU.subtract)
                e.tensor_mul(sq[:], osb[:, yb:yb + 1024], osb[:, yb:yb + 1024])
                R(e, e.reduce_sum(st[:, 2:3], sq[:], AX.X))
                return e.tensor_scalar(st[:, 3:4], st[:, 2:3], 1.0 / C, None, ALU.mult)
            sc.dve(d_ln, waits=[("ac", 2 * i + 1)], inc="dvv")
            def a_rstd2(e):
                kb.ret(e, e.activation(st[:, 4:5], st[:, 3:4], AF.Ln,
                                       bias=g["eps_ln"][:], scale=1.0))
                return e.activation(st[:, 5:6], st[:, 4:5], AF.Exp, scale=-0.5)
            sc.act(a_rstd2, waits=[("dvv", i + 1)], inc="ac")

            def d_fin(e, yb=yb, i=i):
                ov = osb[:, yb:yb + 1024]
                e.tensor_scalar(ov, ov, st[:, 5:6], None, ALU.mult)
                e.tensor_tensor(ov, ov, gbt[:, 0:1024], ALU.mult)
                e.tensor_tensor(ov, ov, gbt[:, 1024:2048], ALU.add)
                return e.tensor_tensor(ov, ov, xsb[:, yb:yb + 1024], ALU.add)
            sc.dve(d_fin, waits=[("ac", 2 * i + 2), sc.sw("ldx", sl),
                                 sc.sw("stx", sl)], inc="dv")
            sc.dma("gpsimd", f"stx{sl}", kb.dram[dst][i * 128:(i + 1) * 128, :],
                   osb[:, yb:yb + 1024], waits=[("dv", i + 1)])
        sc.fence("gpsimd", "stx0")
        sc.fence("gpsimd", "stx1")
        sc.emit()


def emit_L0_attn_a(kb, g, src, dst):
    emit_norm(kb, g, "aaln", src, dst_fm="xnbuf", kind="ln")
    emit_l0aa_conv(kb, g, "xnbuf", "fm2")
    emit_l0aa_retnet(kb, g, "fm2", "attout")
    emit_l0aa_proj(kb, g, "attout", src, dst)



SMALL_SHAPES = {
    "L0aa_convw": (128, 32), "L0aa_convb": (128, 8), "L0aa_gb": (128, 16),
    "L0aa_lng": (1, 1024), "L0aa_lnb": (1, 1024), "L0aa_freqs": (128, 16),
    "L0aa_gamma": (1, 16),
    "L0f_g1": (1, 1024), "L0f_g2": (1, 1024),
    "L0ab_r1b": (128, 4), "L0ab_lng": (128, 4), "L0ab_lnb": (128, 4),
    "L0ab_r2b": (1, 6), "L0ab_temp": (1, 1), "L0ab_rbias": (1, 6),
    "L0ab_b1": (128, 72), "L0ab_b2": (1, 6144),
    "L1ab_bq": (128, 8), "L1ab_bk": (128, 8), "L1ab_bv": (128, 8),
    "L1ab_bo": (1, 1024),
    "L1f_qs": (128, 2), "L1f_ks": (128, 2), "L1f_lng": (2, 1024),
    "L1f_lnb": (2, 1024), "L1f_b1a": (128, 44), "L1f_b1g": (128, 44),
    "L1f_b2": (2, 1024),
    "madd": (128, 128), "bandmask": (128, 640), "trow": (1, 4096),
    "cmaskT_0": (128, 512), "cmaskT_1": (128, 512), "cmaskT_2": (128, 512),
    "cmaskT_3": (128, 512),
    "lmaskT_0": (128, 512), "lmaskT_1": (128, 512), "lmaskT_2": (128, 512),
    "lmaskT_3": (128, 512), "lmaskT_4": (128, 512), "lmaskT_5": (128, 512),
    "lmaskT_6": (128, 512), "lmaskT_7": (128, 512),
    "ones2h": (128, 2), "sel2h": (2, 128),
}

SUB_WCAST = {
    "L0aa": ["L0aa_qkw", "L0aa_vw", "L0aa_gw", "L0aa_ow"],
    "L0f": ["L0f_wq", "L0f_wk", "L0f_wv", "L0f_wo", "L0f_gate", "L0f_w1",
            "L0f_w2", "L0f_w3", "ropecos", "ropesin", "rot2"],
    "L0ab": ["L0ab_r1w", "L0ab_r2w", "L0ab_W1", "L0ab_W2"],
    "L1aa": ["L1aa_kvd", "L1aa_kvu", "L1aa_wq", "L1aa_wo"],
    "L1ab": ["L1ab_wq", "L1ab_wk", "L1ab_wv", "L1ab_wo"],
    "L1f": ["L1f_wqkv", "L1f_wo", "L1f_w1a", "L1f_w1g", "L1f_w2",
            "xposcsA", "xpossnA", "xposcsB", "xpossnB", "rot2"],
}

FULL_ORDER = ["L0aa", "L0f", "L0ab", "L0f", "L1aa", "L1f", "L1ab", "L1f"]


def build_program(sublayers, dbg=()):
    nc = bass.Bass("TRN2", target_bir_lowering=False)
    kb = KB(nc)
    g = alloc_globals(nc)

    kb.dram_t("x", (T, C), F32, kind="ExternalInput")
    kb.dram_t("out", (T, C), F32, kind="ExternalOutput")

    def scr(name, shape, dt):
        kind = "ExternalOutput" if name in dbg else "Internal"
        kb.dram_t(name, shape, dt, kind=kind)

    scr("xres", (T, C), F32)
    scr("ybuf", (T, C), F32)
    scr("xnbuf", (C, PAD + T), BF16)
    scr("fm2", (C, PAD + T), BF16)
    scr("attout", (2048, T), BF16)
    scr("comb", (T, 8), F32)
    scr("qfm", (C, T), BF16)
    scr("kfm", (C, T), BF16)
    scr("vtm", (T, C), BF16)
    scr("yfm", (C, PAD + T), BF16)

    need_w = []
    for s in set(sublayers):
        need_w += SUB_WCAST[s]
    need_w = sorted(set(need_w))
    for nm in need_w:
        shp = WCAST[nm]
        kb.dram_t(nm, (int(np.prod(shp[:-1])), shp[-1]), F32, kind="ExternalInput")
    for nm, shp in SMALL_SHAPES.items():
        kb.dram_t(nm, shp, F32, kind="ExternalInput")

    declare_bf_weights(kb, need_w)
    emit_init(kb, g)
    emit_zero_pads(kb, ["xnbuf", "fm2", "yfm"])
    emit_wcast(kb, need_w)

    n = len(sublayers)
    for i, s in enumerate(sublayers):
        src = "x" if i == 0 else "xres"
        dst = "out" if i == n - 1 else "xres"
        if s == "L0aa":
            emit_L0_attn_a(kb, g, src, dst)
        elif s == "L0f":
            emit_L0_ffn(kb, g, src, dst, tag=f"f{i}")
        elif s == "L0ab":
            emit_L0_attn_b(kb, g, src, dst, tag=f"b{i}")
        elif s == "L1aa":
            emit_L1_attn_a(kb, g, src, dst)
        elif s == "L1ab":
            emit_L1_attn_b(kb, g, src, dst)
        elif s == "L1f":
            emit_L1_ffn(kb, g, src, dst, tag=f"lf{i}")
        else:
            raise ValueError(s)
    nc.finalize()
    return nc, kb


def kernel(**inputs):
    staged = stage_weights(inputs)
    staged.update(stage_consts())
    x_all = np.ascontiguousarray(inputs["L0_attn_a_x"], dtype=np.float32)
    nc, kb = build_program(FULL_ORDER)
    in_maps = []
    for b in range(8):
        m = dict(staged)
        m["x"] = np.ascontiguousarray(x_all[b])
        in_maps.append(m)
    res = run_bass_kernel_spmd(nc, in_maps, core_ids=list(range(8)))
    return np.stack([np.asarray(res.results[i]["out"], dtype=np.float32)
                     for i in range(8)])



def emit_proj_residual(kb, g, tag, fm_src, nrows, w_rhs, src, dst,
                       ln=None, bias_row=None, w_row0=0):
    nc = kb.nc
    nch = nrows // 128
    with ExitStack() as _st:
        sb, ps = mk_tiles(nc, _st)
        ows = sb(f"{tag}_w", [128, nch * 1024], BF16)
        att = sb(f"{tag}_a", [128, 2 * nch * 128], BF16)
        ysb = sb(f"{tag}_y", [128, 2 * 1024])
        xsb = sb(f"{tag}_x", [128, 2 * 1024])
        osb = sb(f"{tag}_o", [128, 2 * 1024])
        sq = sb(f"{tag}_q", [128, 1024])
        st = sb(f"{tag}_s", [128, 8])
        grow = sb(f"{tag}_gr", [1, 2 * C])
        gbt = sb(f"{tag}_gb", [128, 2 * C])
        psY = ps(f"{tag}_p", [128, 2 * 1024])
        psG = ps(f"{tag}_pg", [128, 1024])

        s0 = Sched(kb, tag + "0")
        s0.dma("sync", "ld", ows[:, 0:nch * 1024].rearrange("p (c f) -> p c f", c=nch),
               kb.dram[w_rhs][w_row0:w_row0 + nrows, :].rearrange(
                   "(c p) f -> p c f", p=128))
        if ln is not None:
            s0.dma("sync", "ld", grow[0:1, 0:C], kb.dram[ln[0]])
            s0.dma("sync", "ld", grow[0:1, C:2 * C], kb.dram[ln[1]])
        elif bias_row is not None:
            s0.dma("sync", "ld", grow[0:1, 0:C], kb.dram[bias_row])
        if ln is not None or bias_row is not None:
            wide = 2 * C if ln is not None else C
            for k0 in range(0, wide, 512):
                s0.mm(lambda e, k0=k0: e.matmul(psG[:, 0:512], g["ones_r"][:],
                                                grow[0:1, k0:k0 + 512], start=True, stop=True),
                      waits=[("ld", s0.cur("ld")), ("bce", s0.cur("bce"))], inc="bcm")
                s0.act(lambda e, k0=k0: e.copy(gbt[:, k0:k0 + 512], psG[:, 0:512]),
                       waits=[("bcm", s0.cur("bcm"))], inc="bce")
        s0.fence("sync", "ld")
        s0.emit()

        sc = Sched(kb, tag)
        for i in range(NT):
            ab = (i % 2) * nch * 128
            yb = (i % 2) * 1024
            sl = i % 2
            sc.dma("sync", f"ld{sl}", att[:, ab:ab + nch * 128].rearrange("p (c f) -> p c f", c=nch),
                   kb.dram[fm_src][0:nrows, i * 128:(i + 1) * 128].rearrange(
                       "(c p) f -> p c f", p=128),
                   waits=[("pe", i - 1)])
            sc.dma("sync", f"ldx{sl}", xsb[:, yb:yb + 1024], kb.dram[src][i * 128:(i + 1) * 128, :],
                   waits=[("dv", i - 1)])

            def p_mm(e, ab=ab, yb=yb):
                for half in range(2):
                    for cch in range(nch):
                        mm = e.matmul(psY[:, yb + half * 512: yb + half * 512 + 512],
                                      att[:, ab + cch * 128: ab + cch * 128 + 128],
                                      ows[:, cch * 1024 + half * 512: cch * 1024 + half * 512 + 512],
                                      start=(cch == 0), stop=(cch == nch - 1))
                return mm
            sc.mm(p_mm, waits=[sc.sw("ld", sl),
                               ("acv", max(0, i - 1))], inc="pe")
            sc.act(lambda e, yb=yb: e.copy(ysb[:, yb:yb + 1024], psY[:, yb:yb + 1024]),
                   waits=[("pe", i + 1)], inc="acv")

            if ln is not None:
                def d_ln(e, yb=yb):
                    R = kb.ret
                    yv = ysb[:, yb:yb + 1024]
                    R(e, e.reduce_sum(st[:, 0:1], yv, AX.X))
                    R(e, e.tensor_scalar(st[:, 1:2], st[:, 0:1], 1.0 / C, None, ALU.mult))
                    e.tensor_scalar(osb[:, yb:yb + 1024], yv, st[:, 1:2], None, ALU.subtract)
                    e.tensor_mul(sq[:], osb[:, yb:yb + 1024], osb[:, yb:yb + 1024])
                    R(e, e.reduce_sum(st[:, 2:3], sq[:], AX.X))
                    return e.tensor_scalar(st[:, 3:4], st[:, 2:3], 1.0 / C, None, ALU.mult)
                sc.dve(d_ln, waits=[("acv", i + 1)], inc="dvv")
                def a_rstd(e):
                    kb.ret(e, e.activation(st[:, 4:5], st[:, 3:4], AF.Ln,
                                           bias=g["eps_ln"][:], scale=1.0))
                    return e.activation(st[:, 5:6], st[:, 4:5], AF.Exp, scale=-0.5)
                sc.act(a_rstd, waits=[("dvv", sc.cur("dvv"))], inc="acs")

                def d_fin(e, yb=yb):
                    ov = osb[:, yb:yb + 1024]
                    e.tensor_scalar(ov, ov, st[:, 5:6], None, ALU.mult)
                    e.tensor_tensor(ov, ov, gbt[:, 0:1024], ALU.mult)
                    e.tensor_tensor(ov, ov, gbt[:, 1024:2048], ALU.add)
                    return e.tensor_tensor(ov, ov, xsb[:, yb:yb + 1024], ALU.add)
                sc.dve(d_fin, waits=[("acs", sc.cur("acs")), sc.sw("ldx", sl),
                                     sc.sw("stx", sl)], inc="dv")
            else:
                def d_fin(e, yb=yb):
                    ov = osb[:, yb:yb + 1024]
                    if bias_row is not None:
                        e.tensor_add(ov, ysb[:, yb:yb + 1024], gbt[:, 0:1024])
                        return e.tensor_tensor(ov, ov, xsb[:, yb:yb + 1024], ALU.add)
                    return e.tensor_tensor(ov, ysb[:, yb:yb + 1024],
                                           xsb[:, yb:yb + 1024], ALU.add)
                sc.dve(d_fin, waits=[("acv", i + 1), sc.sw("ldx", sl),
                                     sc.sw("stx", sl)], inc="dv")
            sc.dma("gpsimd", f"stx{sl}", kb.dram[dst][i * 128:(i + 1) * 128, :],
                   osb[:, yb:yb + 1024], waits=[("dv", i + 1)])
        sc.fence("gpsimd", "stx0")
        sc.fence("gpsimd", "stx1")
        sc.emit()



def emit_flash(kb, g, tag, qfm, kfm, vtm, attout, nheads=16, vcols=1024,
               causal=True, scale=0.125, kv_of_head=None, local_tiles=None):
    nc = kb.nc
    if kv_of_head is None:
        kv_of_head = lambda h: h
    npair = nheads // 2
    kv_rows = max(kv_of_head(h) for h in range(nheads)) * 64 + 64
    half_pairs = npair // 2 if kv_rows * 2 // 128 * T * 2 > 40 * 1024 * 128 else npair
    with ExitStack() as _st:
        sb, ps = mk_tiles(nc, _st)
        max_kch = 4
        kres = sb(f"{tag}_k", [128, max_kch * T], BF16)
        vres = sb(f"{tag}_v", [128, NT * 512], BF16)
        qt_sb = sb(f"{tag}_qt", [128, 2 * 128], BF16)
        S_sb = sb(f"{tag}_S", [128, T])
        P_sb = sb(f"{tag}_P", [128, T], BF16)
        Pt_sb = sb(f"{tag}_Pt", [128, 1024], BF16)
        madd = sb(f"{tag}_md", [128, 128])
        outst = sb(f"{tag}_os", [128, T], BF16)
        rows = sb(f"{tag}_r", [128, 8])
        psS = ps(f"{tag}_pS", [128, 1024])
        psT = ps(f"{tag}_pT", [128, 1024], BF16)
        ps_e = ps(f"{tag}_pe", [64, 128])
        ps_o = ps(f"{tag}_po", [64, 128])

        bmask = sb(f"{tag}_bm", [128, 640]) if local_tiles else None
        s0 = Sched(kb, tag + "0")
        if local_tiles:
            s0.dma("sync", "ld", bmask[:], kb.dram["bandmask"])
        elif causal:
            s0.dma("sync", "ld", madd[:], kb.dram["madd"])
        s0.fence("sync", "ld")
        s0.emit()

        kv_all = sorted(set(kv_of_head(h) // 2 for h in range(nheads)))
        groups = []
        cur = []
        cur_kv = set()
        for p in range(npair):
            kvs = {kv_of_head(2 * p) // 2, kv_of_head(2 * p + 1) // 2}
            if len(cur_kv | kvs) > max_kch and cur:
                groups.append((cur, sorted(cur_kv)))
                cur, cur_kv = [], set()
            cur.append(p)
            cur_kv |= kvs
        groups.append((cur, sorted(cur_kv)))

        sc = Sched(kb, tag)
        uidx = 0
        for pairs, kvchunks in groups:
            kvpos = {kc: j for j, kc in enumerate(kvchunks)}
            assert all(kvchunks[j + 1] == kvchunks[j] + 1 for j in range(len(kvchunks) - 1))
            gate = [("pvE", sc.cur("pvE")), ("mmS", sc.cur("mmS"))]
            for j, kc in enumerate(kvchunks):
                sc.dma("sync", "kld", kres[:, j * T:(j + 1) * T],
                       kb.dram[kfm][kc * 128:(kc + 1) * 128, :], waits=gate)
            nvc = len(kvchunks) * 128
            sc.dma("sync", "kld",
                   vres[:, 0:NT * nvc].rearrange("p (i f) -> p i f", f=nvc),
                   kb.dram[vtm].rearrange("(i p) f -> p i f", p=128)[
                       :, :, kvchunks[0] * 128: kvchunks[0] * 128 + nvc],
                   waits=gate)
            kld_target = sc.cur("kld")

            for p in pairs:
                for qt in range(NT):
                    qb = (uidx % 2) * 128
                    sc.dma("sync", "qld", qt_sb[:, qb:qb + 128],
                           kb.dram[qfm][p * 128:(p + 1) * 128, qt * 128:(qt + 1) * 128],
                           waits=[("mmS", sc.cur("mmS"))])
                    qld_t = sc.cur("qld")
                    if local_tiles:
                        ks = max(0, qt - local_tiles)
                        ntile = qt - ks + 1
                    else:
                        ks = 0
                        ntile = (qt + 1) if causal else NT
                    strip = ntile * 128
                    nchunk = (strip + 511) // 512
                    for h2 in range(2):
                        h = 2 * p + h2
                        kv = kv_of_head(h)
                        kj = kvpos[kv // 2]
                        kbase = (kv % 2) * 64
                        qbase = h2 * 64
                        for cči in range(nchunk):
                            c0 = cči * 512
                            w = min(512, strip - c0)
                            pb = (cči % 2) * 512
                            def s_mm(e, c0=c0, w=w, pb=pb, qb=qb, qbase=qbase,
                                     kj=kj, kbase=kbase, ks=ks):
                                return e.matmul(
                                    psS[:, pb:pb + w],
                                    qt_sb[qbase:qbase + 64, qb:qb + 128],
                                    kres[kbase:kbase + 64,
                                         kj * T + ks * 128 + c0: kj * T + ks * 128 + c0 + w],
                                    start=True, stop=True)
                            sc.mm(s_mm, waits=[("kld", kld_target), ("qld", qld_t),
                                               ("evS", sc.cur("evS") - 1)], inc="mmS")
                            def e_ev(e, c0=c0, w=w, pb=pb):
                                return e.activation(S_sb[:, c0:c0 + w], psS[:, pb:pb + w],
                                                    AF.Copy, bias=0.0, scale=scale)
                            sc.act(e_ev, waits=[("mmS", sc.cur("mmS")),
                                                ("dvX", sc.cur("dvX"))], inc="evS")
                        if local_tiles:
                            def d_mask(e, strip=strip, ntile=ntile):
                                off = 640 - ntile * 128
                                e.tensor_tensor(S_sb[:, 0:strip], S_sb[:, 0:strip],
                                                bmask[:, off:640], ALU.add)
                                return e.tensor_reduce(rows[:, 0:1], S_sb[:, 0:strip],
                                                       AX.X, ALU.max, negate=True)
                        elif causal:
                            def d_mask(e, strip=strip):
                                e.tensor_tensor(S_sb[:, strip - 128:strip],
                                                S_sb[:, strip - 128:strip], madd[:], ALU.add)
                                return e.tensor_reduce(rows[:, 0:1], S_sb[:, 0:strip],
                                                       AX.X, ALU.max, negate=True)
                        else:
                            def d_mask(e, strip=strip):
                                return e.tensor_reduce(rows[:, 0:1], S_sb[:, 0:strip],
                                                       AX.X, ALU.max, negate=True)
                        sc.dve(d_mask, waits=[("evS", sc.cur("evS")),
                                              ("evE", sc.cur("evE"))], inc="dvM")
                        sc.act(lambda e, strip=strip: e.activation(
                            P_sb[:, 0:strip], S_sb[:, 0:strip], AF.Exp,
                            bias=rows[:, 0:1], scale=1.0, accum_out=rows[:, 1:2]),
                            waits=[("dvM", sc.cur("dvM")), ("mmT", sc.cur("mmT"))],
                            inc="evE")
                        def d_scale(e, strip=strip):
                            kb.ret(e, e.reciprocal(rows[:, 2:3], rows[:, 1:2]))
                            return e.tensor_scalar(P_sb[:, 0:strip], P_sb[:, 0:strip],
                                                   rows[:, 2:3], None, ALU.mult)
                        sc.dve(d_scale, waits=[("evE", sc.cur("evE"))], inc="dvX")
                        psO = ps_e if h2 == 0 else ps_o
                        for b0 in range(0, ntile, 8):
                            bn = min(8, ntile - b0)
                            def p_tr(e, b0=b0, bn=bn):
                                for j in range(bn):
                                    mm = e.transpose(psT[:, j * 128:(j + 1) * 128],
                                                     P_sb[:, (b0 + j) * 128:(b0 + j + 1) * 128],
                                                     g["ident"][:])
                                return mm
                            sc.mm(p_tr, waits=[("dvX", sc.cur("dvX")),
                                               ("dvT", sc.cur("dvT"))], inc="mmT")
                            def d_evT(e, bn=bn):
                                return e.tensor_copy(Pt_sb[:, 0:bn * 128], psT[:, 0:bn * 128])
                            sc.dve(d_evT, waits=[("mmT", sc.cur("mmT")),
                                                 ("pvB", sc.cur("pvB"))], inc="dvT")
                            def p_pv(e, b0=b0, bn=bn, kj=kj, kbase=kbase, psO=psO,
                                     ntile=ntile, nvc=nvc, ks=ks):
                                for j in range(bn):
                                    tk = b0 + j
                                    mm = e.matmul(
                                        psO[:],
                                        vres[:, (ks + tk) * nvc + kj * 128 + kbase:
                                                (ks + tk) * nvc + kj * 128 + kbase + 64],
                                        Pt_sb[:, j * 128:(j + 1) * 128],
                                        start=(tk == 0), stop=(tk == ntile - 1))
                                return mm
                            sc.mm(p_pv, waits=[("dvT", sc.cur("dvT")),
                                               ("pvE", sc.cur("pvE"))], inc="pvB")
                        def a_out(e, h2=h2, qt=qt, psO=psO, qbase=qbase):
                            return e.copy(outst[qbase:qbase + 64, qt * 128:(qt + 1) * 128],
                                          psO[:])
                        sc.act(a_out, waits=[("pvB", sc.cur("pvB")),
                                             ("osf", sc.cur("osf"))], inc="pvE")
                    uidx += 1
                sc.dma("gpsimd", "osf", kb.dram[attout][p * 128:(p + 1) * 128, :],
                       outst[:], waits=[("pvE", sc.cur("pvE"))])
        sc.fence("gpsimd", "osf")
        sc.emit()



def emit_l1aa_qkv(kb, g, src_fm="xnbuf"):
    nc = kb.nc
    with ExitStack() as _st:
        sb, ps = mk_tiles(nc, _st)
        xnr = sb("l1q_x", [128, 8 * T], BF16)
        latf = sb("l1q_l", [128, T], BF16)
        wkvd = sb("l1q_wd", [128, 8 * 128], BF16)
        wkvu = sb("l1q_wu", [128, 2048], BF16)
        wqr = sb("l1q_wq", [128, 8 * 1024], BF16)
        est = sb("l1q_e", [128, 2 * 512], BF16)
        vst = sb("l1q_v", [128, 2 * 512], BF16)
        psA = ps("l1q_pA", [128, 1024])
        psT = ps("l1q_pT", [128, 512], BF16)

        s0 = Sched(kb, "l1q0")
        for c in range(8):
            s0.dma("sync", "ld", xnr[:, c * T:(c + 1) * T],
                   kb.dram[src_fm][c * 128:(c + 1) * 128, PAD:PAD + T])
        s0.dma("sync", "ld", wkvd[:].rearrange("p (c f) -> p c f", c=8),
               kb.dram["bf_L1aa_kvd"].rearrange("(c p) f -> p c f", p=128))
        s0.dma("sync", "ld", wkvu[:], kb.dram["bf_L1aa_kvu"])
        s0.dma("sync", "ld", wqr[:].rearrange("p (c f) -> p c f", c=8),
               kb.dram["bf_L1aa_wq"].rearrange("(c p) f -> p c f", p=128))
        s0.fence("sync", "ld")
        s0.emit()

        sc = Sched(kb, "l1q")
        for tc in range(8):
            pb = (tc % 2) * 512
            def mmA(e, tc=tc, pb=pb):
                for c in range(8):
                    mm = e.matmul(psA[:, pb:pb + 512], wkvd[:, c * 128:(c + 1) * 128],
                                  xnr[:, c * T + tc * 512: c * T + (tc + 1) * 512],
                                  start=(c == 0), stop=(c == 7))
                return mm
            sc.mm(mmA, waits=[("evA", sc.cur("evA") - 1)], inc="mmA")
            sc.act(lambda e, tc=tc, pb=pb: e.copy(latf[:, tc * 512:(tc + 1) * 512],
                                                  psA[:, pb:pb + 512]),
                   waits=[("mmA", sc.cur("mmA"))], inc="evA")
        vidx = 0
        for fch in range(16):
            for tc in range(8):
                pb = (tc % 2) * 512
                eb = (tc % 2) * 512
                def mmB(e, fch=fch, tc=tc, pb=pb):
                    return e.matmul(psA[:, pb:pb + 512],
                                    wkvu[:, fch * 128:(fch + 1) * 128],
                                    latf[:, tc * 512:(tc + 1) * 512],
                                    start=True, stop=True)
                sc.mm(mmB, waits=[("evA", 8), ("evB", sc.cur("evB") - 1)], inc="mmB")
                if fch < 8:
                    sc.act(lambda e, pb=pb, eb=eb: e.copy(est[:, eb:eb + 512], psA[:, pb:pb + 512]),
                           waits=[("mmB", sc.cur("mmB")), sc.sw("stk", tc % 2)],
                           inc="evB")
                    sc.dma("gpsimd", f"stk{tc % 2}",
                           kb.dram["kfm"][fch * 128:(fch + 1) * 128, tc * 512:(tc + 1) * 512],
                           est[:, eb:eb + 512], waits=[("evB", sc.cur("evB"))])
                else:
                    sc.act(lambda e, pb=pb, eb=eb: e.copy(est[:, eb:eb + 512], psA[:, pb:pb + 512]),
                           waits=[("mmB", sc.cur("mmB")), ("mmT", sc.cur("mmT") - 1),
                                  ("stk0", 16 * 32), ("stk1", 16 * 32)],
                           inc="evB")
                    def trB(e, eb=eb):
                        for j in range(4):
                            mm = e.transpose(psT[:, j * 128:(j + 1) * 128],
                                             est[:, eb + j * 128: eb + j * 128 + 128],
                                             g["ident"][:])
                        return mm
                    sc.mm(trB, waits=[("evB", sc.cur("evB")),
                                      ("evT", sc.cur("evT"))], inc="mmT")
                    vb = (vidx % 2) * 512
                    def evT(e, vb=vb):
                        return e.tensor_copy(vst[:, vb:vb + 512], psT[:])
                    sc.dve(evT, waits=[("mmT", sc.cur("mmT")),
                                       sc.sw("stv", vidx % 2)], inc="evT")
                    for j in range(4):
                        sc.dma("gpsimd", f"stv{vidx % 2}",
                               kb.dram["vtm"][(tc * 4 + j) * 128:(tc * 4 + j + 1) * 128,
                                              (fch - 8) * 128:(fch - 7) * 128],
                               vst[:, vb + j * 128: vb + j * 128 + 128],
                               waits=[("evT", sc.cur("evT"))])
                    vidx += 1
        for fch in range(8):
            for tc in range(8):
                pb = (tc % 2) * 512
                eb = (tc % 2) * 512
                def mmC(e, fch=fch, tc=tc, pb=pb):
                    for c in range(8):
                        mm = e.matmul(psA[:, pb:pb + 512],
                                      wqr[:, c * 1024 + fch * 128: c * 1024 + (fch + 1) * 128],
                                      xnr[:, c * T + tc * 512: c * T + (tc + 1) * 512],
                                      start=(c == 0), stop=(c == 7))
                    return mm
                sc.mm(mmC, waits=[("evC", sc.cur("evC") - 1), ("evB", 128)], inc="mmC")
                sc.act(lambda e, pb=pb, eb=eb: e.copy(est[:, eb:eb + 512], psA[:, pb:pb + 512]),
                       waits=[("mmC", sc.cur("mmC")), sc.sw("stq", tc % 2),
                              ("stk0", 16 * 32), ("stk1", 16 * 32)],
                       inc="evC")
                sc.dma("gpsimd", f"stq{tc % 2}",
                       kb.dram["qfm"][fch * 128:(fch + 1) * 128, tc * 512:(tc + 1) * 512],
                       est[:, eb:eb + 512], waits=[("evC", sc.cur("evC"))])
        sc.fence("gpsimd", "stk0")
        sc.fence("gpsimd", "stk1")
        sc.fence("gpsimd", "stv0")
        sc.fence("gpsimd", "stv1")
        sc.fence("gpsimd", "stq0")
        sc.fence("gpsimd", "stq1")
        sc.emit()


def emit_L1_attn_a(kb, g, src, dst):
    emit_norm(kb, g, "n1a", src, dst_fm="xnbuf", kind="ln")
    emit_l1aa_qkv(kb, g)
    emit_flash2(kb, g, "f1a", "qfm", "kfm", "vtm", "attout",
                nheads=16, causal=True, scale=0.125)
    emit_proj_residual(kb, g, "p1a", "attout", 1024, "bf_L1aa_wo", src, dst)



def emit_l1ab_qkv(kb, g, src_fm="xnbuf"):
    nc = kb.nc
    with ExitStack() as _st:
        sb, ps = mk_tiles(nc, _st)
        xnr = sb("abq_x", [128, 8 * T], BF16)
        wres = sb("abq_w", [128, 2 * 8 * 128], BF16)
        bres = sb("abq_b", [128, 3 * 8])
        est = sb("abq_e", [128, 2 * 512], BF16)
        vst = sb("abq_v", [128, 2 * 512], BF16)
        psA = ps("abq_pA", [128, 1024])
        psT = ps("abq_pT", [128, 512], BF16)

        s0 = Sched(kb, "abq0")
        for c in range(8):
            s0.dma("sync", "ld", xnr[:, c * T:(c + 1) * T],
                   kb.dram[src_fm][c * 128:(c + 1) * 128, PAD:PAD + T])
        s0.dma("sync", "ld", bres[:, 0:8], kb.dram["L1ab_bq"])
        s0.dma("sync", "ld", bres[:, 8:16], kb.dram["L1ab_bk"])
        s0.dma("sync", "ld", bres[:, 16:24], kb.dram["L1ab_bv"])
        s0.fence("sync", "ld")
        s0.emit()

        sc = Sched(kb, "abq")
        vidx = 0
        widx = 0
        for wi, wnm in enumerate(["bf_L1ab_wq", "bf_L1ab_wk", "bf_L1ab_wv"]):
            for pr in range(8):
                wb = (widx % 2) * 1024
                sc.dma("sync", "wld",
                       wres[:, wb:wb + 1024].rearrange("p (c f) -> p c f", c=8),
                       kb.dram[wnm].rearrange("(n c p) f -> n p c f", c=8, p=128)[pr],
                       waits=[("mmP", sc.cur("mmP"))])
                wld_t = sc.cur("wld")
                widx += 1
                for tc in range(8):
                    pb = (tc % 2) * 512
                    eb = (tc % 2) * 512
                    def mmP(e, wb=wb, pb=pb, tc=tc):
                        for c in range(8):
                            mm = e.matmul(psA[:, pb:pb + 512],
                                          wres[:, wb + c * 128: wb + (c + 1) * 128],
                                          xnr[:, c * T + tc * 512: c * T + (tc + 1) * 512],
                                          start=(c == 0), stop=(c == 7))
                        return mm
                    sc.mm(mmP, waits=[("wld", wld_t), ("evP", sc.cur("evP") - 1)],
                          inc="mmP")
                    if wi < 2:
                        dstn = "qfm" if wi == 0 else "kfm"
                        sc.dve(lambda e, pb=pb, eb=eb, wi=wi, pr=pr: e.tensor_scalar(
                            est[:, eb:eb + 512], psA[:, pb:pb + 512],
                            bres[:, wi * 8 + pr: wi * 8 + pr + 1], None, ALU.add),
                            waits=[("mmP", sc.cur("mmP")),
                                   sc.sw("stk", tc % 2)], inc="evP")
                        sc.dma("gpsimd", f"stk{tc % 2}",
                               kb.dram[dstn][pr * 128:(pr + 1) * 128, tc * 512:(tc + 1) * 512],
                               est[:, eb:eb + 512], waits=[("evP", sc.cur("evP"))])
                    else:
                        sc.dve(lambda e, pb=pb, eb=eb, pr=pr: e.tensor_scalar(
                            est[:, eb:eb + 512], psA[:, pb:pb + 512],
                            bres[:, 16 + pr:16 + pr + 1], None, ALU.add),
                            waits=[("mmP", sc.cur("mmP")), ("mmT", sc.cur("mmT") - 1),
                                   ("stk0", 16 * 64), ("stk1", 16 * 64)], inc="evP")
                        def trB(e, eb=eb):
                            for j in range(4):
                                mm = e.transpose(psT[:, j * 128:(j + 1) * 128],
                                                 est[:, eb + j * 128: eb + j * 128 + 128],
                                                 g["ident"][:])
                            return mm
                        sc.mm(trB, waits=[("evP", sc.cur("evP")),
                                          ("evT", sc.cur("evT"))], inc="mmT")
                        vb = (vidx % 2) * 512
                        def evT(e, vb=vb):
                            return e.tensor_copy(vst[:, vb:vb + 512], psT[:])
                        sc.dve(evT, waits=[("mmT", sc.cur("mmT")),
                                           sc.sw("stv", vidx % 2)], inc="evT")
                        for j in range(4):
                            sc.dma("gpsimd", f"stv{vidx % 2}",
                                   kb.dram["vtm"][(tc * 4 + j) * 128:(tc * 4 + j + 1) * 128,
                                                  pr * 128:(pr + 1) * 128],
                                   vst[:, vb + j * 128: vb + j * 128 + 128],
                                   waits=[("evT", sc.cur("evT"))])
                        vidx += 1
        sc.fence("gpsimd", "stk0")
        sc.fence("gpsimd", "stk1")
        sc.fence("gpsimd", "stv0")
        sc.fence("gpsimd", "stv1")
        sc.emit()


def emit_L1_attn_b(kb, g, src, dst):
    emit_norm(kb, g, "n1b", src, dst_fm="xnbuf", kind="ln")
    emit_l1ab_qkv(kb, g)
    emit_flash2(kb, g, "f1b", "qfm", "kfm", "vtm", "attout",
                nheads=16, causal=False, scale=0.125)
    emit_proj_residual(kb, g, "p1b", "attout", 1024, "bf_L1ab_wo", src, dst,
                       bias_row="L1ab_bo")



def emit_l0f_qkv(kb, g, tag, src_fm="xnbuf"):
    nc = kb.nc
    with ExitStack() as _st:
        sb, ps = mk_tiles(nc, _st)
        xnr = sb(f"{tag}_x", [128, 8 * T], BF16)
        wq = sb(f"{tag}_wq", [128, 8 * 1024], BF16)
        wkv = sb(f"{tag}_wkv", [128, 8 * 512], BF16)
        rot = sb(f"{tag}_rot", [128, 128], BF16)
        tcos = sb(f"{tag}_tc", [128, T], BF16)
        tsin = sb(f"{tag}_ts", [128, T], BF16)
        qsb = sb(f"{tag}_q", [128, 2 * 512], BF16)
        tmp1 = sb(f"{tag}_t1", [128, 512])
        tmp2 = sb(f"{tag}_t2", [128, 512])
        est = sb(f"{tag}_e", [128, 2 * 512], BF16)
        vst = sb(f"{tag}_v", [128, 2 * 512], BF16)
        psA = ps(f"{tag}_pA", [128, 1024])
        psR = ps(f"{tag}_pR", [128, 512])
        psT = ps(f"{tag}_pT", [128, 512], BF16)

        s0 = Sched(kb, tag + "0")
        for c in range(8):
            s0.dma("sync", "ld", xnr[:, c * T:(c + 1) * T],
                   kb.dram[src_fm][c * 128:(c + 1) * 128, PAD:PAD + T])
        s0.dma("sync", "ld", wq[:].rearrange("p (c f) -> p c f", c=8),
               kb.dram["bf_L0f_wq"].rearrange("(c p) f -> p c f", p=128))
        s0.dma("sync", "ld", wkv[:, 0:8 * 256].rearrange("p (c f) -> p c f", c=8),
               kb.dram["bf_L0f_wk"].rearrange("(c p) f -> p c f", p=128))
        s0.dma("sync", "ld", wkv[:, 8 * 256:8 * 512].rearrange("p (c f) -> p c f", c=8),
               kb.dram["bf_L0f_wv"].rearrange("(c p) f -> p c f", p=128))
        s0.dma("sync", "ld", rot[:], kb.dram["bf_rot2"])
        s0.dma("sync", "ld", tcos[:], kb.dram["bf_ropecos"])
        s0.dma("sync", "ld", tsin[:], kb.dram["bf_ropesin"])
        s0.fence("sync", "ld")
        s0.emit()

        sc = Sched(kb, tag)

        def rope_block(fch, tc, wsl, dst_kind):
            pb = (sc.cur("mmQ") % 2) * 512
            qb = (sc.cur("mmQ") % 2) * 512
            buf, off = wsl
            def mmQ(e, pb=pb, off=off, tc=tc, buf=buf):
                for c in range(8):
                    mm = e.matmul(psA[:, pb:pb + 512],
                                  buf[:, off + c * (1024 if buf is wq else 256):
                                      off + c * (1024 if buf is wq else 256) + 128],
                                  xnr[:, c * T + tc * 512: c * T + (tc + 1) * 512],
                                  start=(c == 0), stop=(c == 7))
                return mm
            sc.mm(mmQ, waits=[("evQ", sc.cur("evQ") - 1)], inc="mmQ")
            sc.act(lambda e, pb=pb, qb=qb: e.copy(qsb[:, qb:qb + 512], psA[:, pb:pb + 512]),
                   waits=[("mmQ", sc.cur("mmQ")), ("dvQ", sc.cur("dvQ") - 1)], inc="evQ")
            def mmR(e, qb=qb):
                return e.matmul(psR[:], rot[:], qsb[:, qb:qb + 512], start=True, stop=True)
            sc.mm(mmR, waits=[("evQ", sc.cur("evQ")), ("dvQ", sc.cur("dvQ"))], inc="mmR")
            ri = sc.cur("dvQ")
            eb = (ri % 2) * 512
            def d_rope(e, qb=qb, tc=tc, eb=eb):
                e.tensor_tensor(tmp1[:], qsb[:, qb:qb + 512],
                                tcos[:, tc * 512:(tc + 1) * 512], ALU.mult)
                e.tensor_tensor(tmp2[:], psR[:], tsin[:, tc * 512:(tc + 1) * 512], ALU.mult)
                return e.tensor_tensor(est[:, eb:eb + 512], tmp1[:], tmp2[:], ALU.add)
            wsem = sc.sw("stq" if dst_kind == "q" else "stk", ri % 2)
            sc.dve(d_rope, waits=[("mmR", sc.cur("mmR")), wsem], inc="dvQ")
            return eb, ri % 2

        for fch in range(8):
            for tc in range(8):
                eb, ssl = rope_block(fch, tc, (wq, fch * 128), "q")
                sc.dma("gpsimd", f"stq{ssl}",
                       kb.dram["qfm"][fch * 128:(fch + 1) * 128, tc * 512:(tc + 1) * 512],
                       est[:, eb:eb + 512], waits=[("dvQ", sc.cur("dvQ"))])
        for fch in range(2):
            for tc in range(8):
                eb, ssl = rope_block(fch, tc, (wkv, fch * 128), "k")
                sc.dma("gpsimd", f"stk{ssl}",
                       kb.dram["kfm"][fch * 128:(fch + 1) * 128, tc * 512:(tc + 1) * 512],
                       est[:, eb:eb + 512], waits=[("dvQ", sc.cur("dvQ"))])
        vidx = 0
        for fch in range(2):
            for tc in range(8):
                pb = (tc % 2) * 512
                eb = (tc % 2) * 512
                def mmV(e, fch=fch, tc=tc, pb=pb):
                    for c in range(8):
                        mm = e.matmul(psA[:, pb:pb + 512],
                                      wkv[:, 8 * 256 + c * 256 + fch * 128:
                                          8 * 256 + c * 256 + (fch + 1) * 128],
                                      xnr[:, c * T + tc * 512: c * T + (tc + 1) * 512],
                                      start=(c == 0), stop=(c == 7))
                    return mm
                sc.mm(mmV, waits=[("evV", sc.cur("evV") - 1), ("dvQ", sc.cur("dvQ"))],
                      inc="mmV")
                sc.act(lambda e, pb=pb, eb=eb: e.copy(est[:, eb:eb + 512], psA[:, pb:pb + 512]),
                       waits=[("mmV", sc.cur("mmV")), ("mmT", sc.cur("mmT") - 1),
                              ("stq0", 16 * 32), ("stq1", 16 * 32),
                              sc.sw("stk", 0), sc.sw("stk", 1)], inc="evV")
                def trV(e, eb=eb):
                    for j in range(4):
                        mm = e.transpose(psT[:, j * 128:(j + 1) * 128],
                                         est[:, eb + j * 128: eb + j * 128 + 128],
                                         g["ident"][:])
                    return mm
                sc.mm(trV, waits=[("evV", sc.cur("evV")), ("evT", sc.cur("evT"))], inc="mmT")
                vb = (vidx % 2) * 512
                def evT(e, vb=vb):
                    return e.tensor_copy(vst[:, vb:vb + 512], psT[:])
                sc.dve(evT, waits=[("mmT", sc.cur("mmT")), sc.sw("stv", vidx % 2)],
                       inc="evT")
                for j in range(4):
                    sc.dma("gpsimd", f"stv{vidx % 2}",
                           kb.dram["vtm"][(tc * 4 + j) * 128:(tc * 4 + j + 1) * 128,
                                          fch * 128:(fch + 1) * 128],
                           vst[:, vb + j * 128: vb + j * 128 + 128],
                           waits=[("evT", sc.cur("evT"))])
                vidx += 1
        sc.fence("gpsimd", "stq0")
        sc.fence("gpsimd", "stq1")
        sc.fence("gpsimd", "stk0")
        sc.fence("gpsimd", "stk1")
        sc.fence("gpsimd", "stv0")
        sc.fence("gpsimd", "stv1")
        sc.emit()


def emit_l0f_router(kb, g, tag, src_fm="xnbuf", n_exp=4, gate_w="bf_L0f_gate",
                    comb_dst="comb", renorm=True, dbg_dst=None):
    nc = kb.nc
    with ExitStack() as _st:
        sb, ps = mk_tiles(nc, _st)
        xnr = sb(f"{tag}_x", [128, 2 * 1024], BF16)
        gw = sb(f"{tag}_g", [128, 8 * 128], BF16)
        lgb = sb(f"{tag}_l", [128, 2 * n_exp])
        zzb = sb(f"{tag}_z", [128, 2 * n_exp])
        kpb = sb(f"{tag}_k", [128, 2 * n_exp])
        rowsb = sb(f"{tag}_r", [128, 16])
        cmb = sb(f"{tag}_c", [128, 2 * n_exp])
        dbgt = sb(f"{tag}_d", [128, 16])
        psL = ps(f"{tag}_p", [128, 512])

        s0 = Sched(kb, tag + "0")
        s0.dma("sync", "ld", gw[:].rearrange("p (c f) -> p c f", c=8),
               kb.dram[gate_w].rearrange("(c p) f -> p c f", p=128))
        s0.fence("sync", "ld")
        s0.emit()

        sc = Sched(kb, tag)
        for i in range(NT):
            lg = lgb[:, 0:n_exp]
            zz = zzb[:, 0:n_exp]
            kp = kpb[:, 0:n_exp]
            rows = rowsb[:, 0:8]
            xb = (i % 2) * 1024
            sl = i % 2
            sc.dma("sync", f"ld{sl}",
                   xnr[:, xb:xb + 1024].rearrange("p (c f) -> p c f", c=8),
                   kb.dram[src_fm][:, PAD + i * 128:PAD + (i + 1) * 128].rearrange(
                       "(c p) f -> p c f", p=128),
                   waits=[("mmL", sc.cur("mmL") - 1)])
            def mmL(e, xb=xb):
                for c in range(8):
                    mm = e.matmul(psL[:, 0:128], xnr[:, xb + c * 128: xb + (c + 1) * 128],
                                  gw[:, c * 128:(c + 1) * 128],
                                  start=(c == 0), stop=(c == 7))
                return mm
            sc.mm(mmL, waits=[sc.sw("ld", sl), ("dvL", sc.cur("dvL"))], inc="mmL")

            def d_rt(e, lg=lg, rows=rows):
                kb.ret(e, e.tensor_copy(lg[:], psL[:, 0:n_exp]))
                return e.tensor_reduce(rows[:, 0:1], lg[:], AX.X, ALU.max, negate=True)
            sc.dve(d_rt, waits=[("mmL", sc.cur("mmL")), ("evL", sc.cur("evL"))], inc="dvL")
            sc.act(lambda e, lg=lg, zz=zz, rows=rows: e.activation(
                zz[:], lg[:], AF.Exp, bias=rows[:, 0:1], scale=1.0,
                accum_out=rows[:, 1:2]),
                   waits=[("dvL", sc.cur("dvL")), ("dvC", sc.cur("dvC"))], inc="evL")

            cb = (i % 2) * n_exp
            def d_cmb(e, cb=cb, lg=lg, zz=zz, kp=kp, rows=rows):
                R = kb.ret
                R(e, e.tensor_scalar(rows[:, 3:4], rows[:, 0:1], -1.0, None, ALU.mult))
                R(e, e.tensor_scalar(kp[:], lg[:], rows[:, 3:4], None, ALU.is_ge))
                R(e, e.scalar_tensor_tensor(kp[:], kp[:], NEG_BIG, lg[:], ALU.mult, ALU.add))
                R(e, e.tensor_reduce(rows[:, 4:5], kp[:], AX.X, ALU.max))
                R(e, e.tensor_scalar(kp[:], lg[:], rows[:, 4:5], None, ALU.is_ge))
                R(e, e.tensor_tensor(zz[:], zz[:], kp[:], ALU.mult))
                if renorm:
                    R(e, e.reduce_sum(rows[:, 6:7], zz[:], AX.X))
                    R(e, e.reciprocal(rows[:, 5:6], rows[:, 6:7]))
                else:
                    R(e, e.reciprocal(rows[:, 5:6], rows[:, 1:2]))
                return e.tensor_scalar(cmb[:, cb:cb + n_exp], zz[:], rows[:, 5:6],
                                       None, ALU.mult)
            sc.dve(d_cmb, waits=[("evL", sc.cur("evL")), sc.sw("stc", sl)], inc="dvC")
            sc.dma("gpsimd", f"stc{sl}", kb.dram[comb_dst][i * 128:(i + 1) * 128, 0:n_exp],
                   cmb[:, cb:cb + n_exp], waits=[("dvC", sc.cur("dvC"))])
            if dbg_dst is not None:
                def d_dbg(e, lg=lg, zz=zz, kp=kp, rows=rows):
                    e.tensor_copy(dbgt[:, 0:n_exp], lg[:])
                    e.tensor_copy(dbgt[:, 4:4 + n_exp], zz[:])
                    return e.tensor_copy(dbgt[:, 8:16], rows[:])
                sc.dve(d_dbg, waits=[sc.sw("stdb", sl)], inc="dvD")
                sc.dma("gpsimd", f"stdb{sl}", kb.dram[dbg_dst][i * 128:(i + 1) * 128, :],
                       dbgt[:], waits=[("dvD", sc.cur("dvD"))])
        sc.fence("gpsimd", "stc0")
        sc.fence("gpsimd", "stc1")
        sc.emit()


def emit_l0f_moe(kb, g, tag, src_fm="xnbuf", acc_dst="ybuf"):
    nc = kb.nc
    TC = 512
    NFC = 24
    with ExitStack() as _st:
        sb, ps = mk_tiles(nc, _st)
        w1r = sb(f"{tag}_w1", [128, 8 * 3072], BF16)
        w3r = sb(f"{tag}_w3", [128, 8 * 3072], BF16)
        w2r = sb(f"{tag}_w2", [128, NFC * 1024], BF16)
        xnr = sb(f"{tag}_x", [128, 8 * TC], BF16)
        asil = sb(f"{tag}_a", [128, NFC * TC], BF16)
        sil = sb(f"{tag}_s", [128, 2 * 512], BF16)
        cmb = sb(f"{tag}_c", [128, 16])
        acc = sb(f"{tag}_o", [128, 2 * 1024])
        psA = ps(f"{tag}_pA", [128, 1024])
        psB = ps(f"{tag}_pB", [128, 1024])
        psO = ps(f"{tag}_pO", [128, 2 * 1024])

        sc = Sched(kb, tag)
        for e_i in range(4):
            for wnm, wbuf in (("bf_L0f_w1", w1r), ("bf_L0f_w3", w3r)):
                sc.dma("sync", "wld",
                       wbuf[:].rearrange("p (c f) -> p c f", c=8),
                       kb.dram[wnm].rearrange("(n c p) f -> n p c f", c=8, p=128)[e_i],
                       waits=[("mmH", sc.cur("mmH"))])
            wld_t = sc.cur("wld")
            for tc in range(T // TC):
                sc.dma("sync", "xld",
                       xnr[:].rearrange("p (c f) -> p c f", c=8),
                       kb.dram[src_fm][:, PAD + tc * TC:PAD + (tc + 1) * TC].rearrange(
                           "(c p) f -> p c f", p=128),
                       waits=[("mmH", sc.cur("mmH") - 8), ("mmO", sc.cur("mmO"))])
                sc.dma("sync", "xld",
                       cmb[:].rearrange("p (i f) -> p i f", f=4),
                       kb.dram["comb"][tc * TC:(tc + 1) * TC, 0:4].rearrange(
                           "(i p) f -> p i f", p=128),
                       waits=[("dvO", sc.cur("dvO"))])
                xld_t = sc.cur("xld")
                for j4 in range(4):
                    sc.dma("sync", "w2ld",
                           w2r[:, j4 * 6 * 1024:(j4 + 1) * 6 * 1024].rearrange(
                               "p (c f) -> p c f", c=6),
                           kb.dram["bf_L0f_w2"].rearrange(
                               "(n c p) f -> n p c f", c=NFC, p=128)[e_i][:, j4 * 6:(j4 + 1) * 6],
                           waits=[("mmO", sc.cur("mmO"))])
                w2ld_t = sc.cur("w2ld")
                for fc in range(NFC):
                    pb = (fc % 2) * 512
                    def mmH1(e, fc=fc, pb=pb):
                        for c in range(8):
                            mm = e.matmul(psA[:, pb:pb + 512],
                                          w1r[:, c * 3072 + fc * 128: c * 3072 + (fc + 1) * 128],
                                          xnr[:, c * TC: c * TC + TC],
                                          start=(c == 0), stop=(c == 7))
                        return mm
                    sc.mm(mmH1, waits=[("wld", wld_t), ("xld", xld_t),
                                       ("evA", sc.cur("evA") - 1)], inc="mmH")
                    def mmH3(e, fc=fc, pb=pb):
                        for c in range(8):
                            mm = e.matmul(psB[:, pb:pb + 512],
                                          w3r[:, c * 3072 + fc * 128: c * 3072 + (fc + 1) * 128],
                                          xnr[:, c * TC: c * TC + TC],
                                          start=(c == 0), stop=(c == 7))
                        return mm
                    sc.mm(mmH3, waits=[("dvA", sc.cur("dvA") - 1)], inc="mmH")
                    sc.act(lambda e, pb=pb: e.activation(sil[:, pb:pb + 512],
                                                         psA[:, pb:pb + 512], AF.Silu),
                           waits=[("mmH", sc.cur("mmH"))], inc="evA")
                    sc.dve(lambda e, fc=fc, pb=pb: e.tensor_tensor(
                        asil[:, fc * TC: fc * TC + TC], sil[:, pb:pb + 512],
                        psB[:, pb:pb + 512], ALU.mult),
                        waits=[("evA", sc.cur("evA"))], inc="dvA")
                for tt in range(TC // 128):
                    ob = (tt % 2) * 1024
                    def mmO(e, tt=tt, ob=ob):
                        for fc in range(NFC):
                            for half in range(2):
                                mm = e.matmul(
                                    psO[:, ob + half * 512: ob + half * 512 + 512],
                                    asil[:, fc * TC + tt * 128: fc * TC + (tt + 1) * 128],
                                    w2r[:, fc * 1024 + half * 512: fc * 1024 + half * 512 + 512],
                                    start=(fc == 0), stop=(fc == NFC - 1))
                        return mm
                    sc.mm(mmO, waits=[("dvA", sc.cur("dvA")), ("w2ld", w2ld_t),
                                      ("dvO", sc.cur("dvO") - 1)], inc="mmO")
                    def d_out(e, tt=tt, ob=ob, e_i=e_i):
                        return e.tensor_scalar(acc[:, ob:ob + 1024], psO[:, ob:ob + 1024],
                                               cmb[:, tt * 4 + e_i: tt * 4 + e_i + 1],
                                               None, ALU.mult)
                    sc.dve(d_out, waits=[("mmO", sc.cur("mmO")),
                                         sc.sw("accst", tt % 2)],
                           inc="dvO")
                    sc.dma("gpsimd", f"accst{tt % 2}",
                           kb.dram[acc_dst][tc * TC + tt * 128: tc * TC + (tt + 1) * 128, :],
                           acc[:, ob:ob + 1024], accum_op=ALU.add,
                           waits=[("dvO", sc.cur("dvO")),
                                  ("accst0", 16 * 16 * e_i), ("accst1", 16 * 16 * e_i)])
        sc.fence("gpsimd", "accst0")
        sc.fence("gpsimd", "accst1")
        sc.emit()


def emit_stream_add(kb, g, tag, a, b, dst):
    nc = kb.nc
    with ExitStack() as _st:
        sb, ps = mk_tiles(nc, _st)
        at = sb(f"{tag}_a", [128, 2 * 1024])
        bt = sb(f"{tag}_b", [128, 2 * 1024])
        ot = sb(f"{tag}_o", [128, 2 * 1024])
        sc = Sched(kb, tag)
        for i in range(NT):
            ab = (i % 2) * 1024
            sl = i % 2
            sc.dma("sync", f"ld{sl}", at[:, ab:ab + 1024], kb.dram[a][i * 128:(i + 1) * 128, :],
                   waits=[("dv", i - 1)])
            sc.dma("sync", f"ld{sl}", bt[:, ab:ab + 1024], kb.dram[b][i * 128:(i + 1) * 128, :],
                   waits=[("dv", i - 1)])
            stw = sc.sw("st", sl)
            sc.dve(lambda e, ab=ab: e.tensor_tensor(ot[:, ab:ab + 1024], at[:, ab:ab + 1024],
                                                    bt[:, ab:ab + 1024], ALU.add),
                   waits=[sc.sw("ld", sl), stw], inc="dv")
            sc.dma("gpsimd", f"st{sl}", kb.dram[dst][i * 128:(i + 1) * 128, :],
                   ot[:, ab:ab + 1024], waits=[("dv", i + 1)])
        sc.fence("gpsimd", "st0")
        sc.fence("gpsimd", "st1")
        sc.emit()


def emit_L0_ffn(kb, g, src, dst, tag):
    emit_norm(kb, g, tag + "ln", src, dst_tm="ybuf", kind="ln")
    emit_norm(kb, g, tag + "r1", "ybuf", dst_fm="xnbuf", kind="rms", gain_row="L0f_g1")
    emit_l0f_qkv(kb, g, tag + "qk")
    emit_flash2(kb, g, tag + "fl", "qfm", "kfm", "vtm", "attout",
                nheads=16, causal=True, scale=0.125,
                kv_of_head=lambda h: h // 4)
    emit_proj_residual(kb, g, tag + "pj", "attout", 1024, "bf_L0f_wo",
                       "ybuf", "ybuf")
    emit_norm(kb, g, tag + "r2", "ybuf", dst_fm="xnbuf", kind="rms", gain_row="L0f_g2")
    emit_l0f_router(kb, g, tag + "rt")
    emit_l0f_moe(kb, g, tag + "mo")
    emit_stream_add(kb, g, tag + "sa", src, "ybuf", dst)



def emit_l0ab_hg(kb, g, tag, src_fm="xnbuf", dst="kfm"):
    nc = kb.nc
    with ExitStack() as _st:
        sb, ps = mk_tiles(nc, _st)
        xnr = sb(f"{tag}_x", [128, 8 * T], BF16)
        w1 = sb(f"{tag}_w", [128, 8 * 512], BF16)
        bcols = sb(f"{tag}_b", [128, 12])
        hsb = sb(f"{tag}_h", [128, 4 * 512])
        cent = sb(f"{tag}_ce", [128, 4 * 512])
        sq = sb(f"{tag}_sq", [128, 512])
        mrow = sb(f"{tag}_m", [1, 3 * 512])
        bc = sb(f"{tag}_bc", [128, 2 * 512])
        est = sb(f"{tag}_e", [128, 2 * 512], BF16)
        psA = ps(f"{tag}_pA", [128, 1024])
        psR = ps(f"{tag}_pR", [128, 512])

        s0 = Sched(kb, tag + "0")
        for c in range(8):
            s0.dma("sync", "ld", xnr[:, c * T:(c + 1) * T],
                   kb.dram[src_fm][c * 128:(c + 1) * 128, PAD:PAD + T])
        s0.dma("sync", "ld", w1[:].rearrange("p (c f) -> p c f", c=8),
               kb.dram["bf_L0ab_r1w"].rearrange("(c p) f -> p c f", p=128))
        s0.dma("sync", "ld", bcols[:, 0:4], kb.dram["L0ab_r1b"])
        s0.dma("sync", "ld", bcols[:, 4:8], kb.dram["L0ab_lng"])
        s0.dma("sync", "ld", bcols[:, 8:12], kb.dram["L0ab_lnb"])
        s0.fence("sync", "ld")
        s0.emit()

        sc = Sched(kb, tag)
        for tc in range(8):
            for fc in range(4):
                pb = (fc % 2) * 512
                def mmh(e, fc=fc, tc=tc, pb=pb):
                    for c in range(8):
                        mm = e.matmul(psA[:, pb:pb + 512],
                                      w1[:, c * 512 + fc * 128: c * 512 + (fc + 1) * 128],
                                      xnr[:, c * T + tc * 512: c * T + (tc + 1) * 512],
                                      start=(c == 0), stop=(c == 7))
                    return mm
                sc.mm(mmh, waits=[("evh", sc.cur("evh") - 1)], inc="mmh")
                sc.dve(lambda e, fc=fc, pb=pb: e.tensor_scalar(
                    hsb[:, fc * 512:(fc + 1) * 512], psA[:, pb:pb + 512],
                    bcols[:, fc:fc + 1], None, ALU.add),
                    waits=[("mmh", sc.cur("mmh")), ("evg", sc.cur("evg") - 1)], inc="evh")
            def mms(e):
                for fc in range(4):
                    mm = e.matmul(psR[0:1, 0:512], g["ones_c"][:],
                                  hsb[:, fc * 512:(fc + 1) * 512],
                                  start=(fc == 0), stop=(fc == 3))
                return mm
            sc.mm(mms, waits=[("evh", sc.cur("evh")), ("dvv", sc.cur("dvv"))], inc="mms")
            def d_mu(e):
                return e.tensor_scalar(mrow[0:1, 0:512], psR[0:1, 0:512],
                                       1.0 / 512.0, None, ALU.mult)
            sc.dve(d_mu, waits=[("mms", sc.cur("mms"))], inc="dmu")
            def mmb(e):
                return e.matmul(psR[:, 0:512], g["ones_r"][:], mrow[0:1, 0:512],
                                start=True, stop=True)
            sc.mm(mmb, waits=[("dmu", sc.cur("dmu"))], inc="mmb")
            sc.act(lambda e: e.copy(bc[:, 0:512], psR[:]),
                   waits=[("mmb", sc.cur("mmb"))], inc="evb")
            for fc in range(4):
                sc.dve(lambda e, fc=fc: e.tensor_tensor(
                    cent[:, fc * 512:(fc + 1) * 512], hsb[:, fc * 512:(fc + 1) * 512],
                    bc[:, 0:512], ALU.subtract),
                    waits=[("evb", sc.cur("evb"))], inc="dvc")
            for fc in range(4):
                sc.dve(lambda e, fc=fc: e.tensor_mul(sq[:], cent[:, fc * 512:(fc + 1) * 512],
                                                     cent[:, fc * 512:(fc + 1) * 512]),
                       waits=[("dvc", sc.cur("dvc")), ("mmv", sc.cur("mmv"))], inc="evq")
                def mmv(e, fc=fc):
                    return e.matmul(psR[0:1, 0:512], g["ones_c"][:], sq[:, 0:512],
                                    start=(fc == 0), stop=(fc == 3))
                sc.mm(mmv, waits=[("evq", sc.cur("evq"))], inc="mmv")
            def d_var(e):
                R = kb.ret
                R(e, e.tensor_scalar(mrow[0:1, 512:1024], psR[0:1, 0:512],
                                     1.0 / 512.0, None, ALU.mult))
                return e.tensor_copy(mrow[0:1, 1024:1536], mrow[0:1, 512:1024])
            sc.dve(d_var, waits=[("mmv", sc.cur("mmv"))], inc="dva")
            def a_rstd3(e):
                kb.ret(e, e.activation(mrow[0:1, 512:1024], mrow[0:1, 1024:1536],
                                       AF.Ln, bias=g["eps_ln"][0:1, :], scale=1.0))
                return e.activation(mrow[0:1, 512:1024], mrow[0:1, 512:1024],
                                    AF.Exp, scale=-0.5)
            sc.act(a_rstd3, waits=[("dva", sc.cur("dva"))], inc="eva")
            def mmb2(e):
                return e.matmul(psR[:, 0:512], g["ones_r"][:], mrow[0:1, 512:1024],
                                start=True, stop=True)
            sc.mm(mmb2, waits=[("eva", sc.cur("eva")), ("evb", sc.cur("evb"))], inc="mmb2")
            sc.act(lambda e: e.copy(bc[:, 512:1024], psR[:]),
                   waits=[("mmb2", sc.cur("mmb2"))], inc="evb2")
            for fc in range(4):
                hsl = sc.cur("evg") % 2
                eb = hsl * 512
                def d_hn(e, fc=fc):
                    e.tensor_tensor(cent[:, fc * 512:(fc + 1) * 512],
                                    cent[:, fc * 512:(fc + 1) * 512], bc[:, 512:1024],
                                    ALU.mult)
                    return e.tensor_scalar(cent[:, fc * 512:(fc + 1) * 512],
                                           cent[:, fc * 512:(fc + 1) * 512],
                                           bcols[:, 4 + fc:5 + fc], bcols[:, 8 + fc:9 + fc],
                                           ALU.mult, ALU.add)
                sc.dve(d_hn, waits=[("evb2", sc.cur("evb2"))], inc="dvv")
                sc.act(lambda e, fc=fc, eb=eb: e.activation(
                    est[:, eb:eb + 512], cent[:, fc * 512:(fc + 1) * 512], AF.Gelu),
                    waits=[("dvv", sc.cur("dvv")), sc.sw("sth", hsl)],
                    inc="evg")
                sc.dma("gpsimd", f"sth{hsl}",
                       kb.dram[dst][fc * 128:(fc + 1) * 128, tc * 512:(tc + 1) * 512],
                       est[:, eb:eb + 512], waits=[("evg", sc.cur("evg"))])
        sc.fence("gpsimd", "sth0")
        sc.fence("gpsimd", "sth1")
        sc.emit()


def emit_l0ab_router(kb, g, tag, hg_src="kfm"):
    nc = kb.nc
    n_exp = 6
    with ExitStack() as _st:
        sb, ps = mk_tiles(nc, _st)
        hgr = sb(f"{tag}_x", [128, 2 * 512], BF16)
        gw = sb(f"{tag}_g", [128, 4 * 128], BF16)
        lgb = sb(f"{tag}_l", [128, 2 * n_exp])
        zzb = sb(f"{tag}_z", [128, 2 * n_exp])
        kpb = sb(f"{tag}_k", [128, 2 * n_exp])
        rowsb = sb(f"{tag}_r", [128, 16])
        cmb = sb(f"{tag}_c", [128, 2 * n_exp])
        brow = sb(f"{tag}_br", [1, 16])
        bbc = sb(f"{tag}_bb", [128, 2 * n_exp + 2])
        psL = ps(f"{tag}_p", [128, 512])

        s0 = Sched(kb, tag + "0")
        s0.dma("sync", "ld", gw[:].rearrange("p (c f) -> p c f", c=4),
               kb.dram["bf_L0ab_r2w"].rearrange("(c p) f -> p c f", p=128))
        s0.dma("sync", "ld", brow[0:1, 0:6], kb.dram["L0ab_r2b"])
        s0.dma("sync", "ld", brow[0:1, 6:12], kb.dram["L0ab_rbias"])
        s0.dma("sync", "ld", brow[0:1, 12:13], kb.dram["L0ab_temp"])
        nld = s0.cur("ld")
        def d_t(e):
            return e.reciprocal(brow[0:1, 13:14], brow[0:1, 12:13])
        s0.dve(d_t, waits=[("ld", nld)], inc="dvt")
        def bcast(e):
            e.matmul(psL[:, 0:6], g["ones_r"][:], brow[0:1, 0:6], start=True, stop=True)
            e.matmul(psL[:, 6:12], g["ones_r"][:], brow[0:1, 6:12], start=True, stop=True)
            return e.matmul(psL[:, 12:13], g["ones_r"][:], brow[0:1, 13:14],
                            start=True, stop=True)
        s0.mm(bcast, waits=[("dvt", 1)], inc="mmb")
        s0.act(lambda e: e.copy(bbc[:, 0:13], psL[:, 0:13]), waits=[("mmb", 1)], inc="evb")
        s0.emit()

        sc = Sched(kb, tag)
        for i in range(NT):
            lg = lgb[:, (i % 2) * n_exp:(i % 2 + 1) * n_exp]
            zz = zzb[:, (i % 2) * n_exp:(i % 2 + 1) * n_exp]
            kp = kpb[:, (i % 2) * n_exp:(i % 2 + 1) * n_exp]
            rows = rowsb[:, (i % 2) * 8:(i % 2) * 8 + 8]
            xb = (i % 2) * 512
            sl = i % 2
            sc.dma("sync", f"ld{sl}",
                   hgr[:, xb:xb + 512].rearrange("p (c f) -> p c f", c=4),
                   kb.dram[hg_src][0:512, i * 128:(i + 1) * 128].rearrange(
                       "(c p) f -> p c f", p=128),
                   waits=[("mmL", sc.cur("mmL") - 1)])
            def mmL(e, xb=xb):
                for c in range(4):
                    mm = e.matmul(psL[:, 0:128], hgr[:, xb + c * 128: xb + (c + 1) * 128],
                                  gw[:, c * 128:(c + 1) * 128],
                                  start=(c == 0), stop=(c == 3))
                return mm
            sc.mm(mmL, waits=[sc.sw("ld", sl), ("dvL", sc.cur("dvL"))], inc="mmL")

            def d_rt(e, lg=lg, rows=rows):
                R = kb.ret
                R(e, e.tensor_copy(lg[:], psL[:, 0:n_exp]))
                R(e, e.tensor_tensor(lg[:], lg[:], bbc[:, 0:6], ALU.add))
                R(e, e.tensor_scalar(lg[:], lg[:], bbc[:, 12:13], None, ALU.mult))
                R(e, e.tensor_tensor(lg[:], lg[:], bbc[:, 6:12], ALU.add))
                return e.tensor_reduce(rows[:, 0:1], lg[:], AX.X, ALU.max, negate=True)
            sc.dve(d_rt, waits=[("mmL", sc.cur("mmL")), ("evL", sc.cur("evL"))],
                   inc="dvL")
            sc.act(lambda e, lg=lg, zz=zz, rows=rows: e.activation(
                zz[:], lg[:], AF.Exp, bias=rows[:, 0:1], scale=1.0,
                accum_out=rows[:, 1:2]),
                waits=[("dvL", sc.cur("dvL")), ("dvC", sc.cur("dvC"))], inc="evL")
            cb = (i % 2) * n_exp
            def d_cmb(e, cb=cb, lg=lg, zz=zz, kp=kp, rows=rows):
                R = kb.ret
                R(e, e.tensor_scalar(rows[:, 3:4], rows[:, 0:1], -1.0, None, ALU.mult))
                R(e, e.tensor_scalar(kp[:], lg[:], rows[:, 3:4], None, ALU.is_ge))
                R(e, e.scalar_tensor_tensor(kp[:], kp[:], NEG_BIG, lg[:], ALU.mult, ALU.add))
                R(e, e.tensor_reduce(rows[:, 4:5], kp[:], AX.X, ALU.max))
                R(e, e.tensor_scalar(kp[:], lg[:], rows[:, 4:5], None, ALU.is_ge))
                R(e, e.tensor_tensor(zz[:], zz[:], kp[:], ALU.mult))
                R(e, e.reciprocal(rows[:, 5:6], rows[:, 1:2]))
                return e.tensor_scalar(cmb[:, cb:cb + n_exp], zz[:], rows[:, 5:6],
                                       None, ALU.mult)
            sc.dve(d_cmb, waits=[("evL", sc.cur("evL")), sc.sw("stc", sl)], inc="dvC")
            sc.dma("gpsimd", f"stc{sl}", kb.dram["comb"][i * 128:(i + 1) * 128, 0:n_exp],
                   cmb[:, cb:cb + n_exp], waits=[("dvC", sc.cur("dvC"))])
        sc.fence("gpsimd", "stc0")
        sc.fence("gpsimd", "stc1")
        sc.emit()


def emit_l0ab_experts(kb, g, tag, src_fm="xnbuf", acc_dst="xres"):
    nc = kb.nc
    TC = 512
    NFC = 12
    with ExitStack() as _st:
        sb, ps = mk_tiles(nc, _st)
        w1r = sb(f"{tag}_w1", [128, 8 * 1536], BF16)
        w2r = sb(f"{tag}_w2", [128, NFC * 1024], BF16)
        b1c = sb(f"{tag}_b1", [128, 72])
        b2r = sb(f"{tag}_b2", [1, 1024], BF16)
        xnr = sb(f"{tag}_x", [128, 8 * TC], BF16)
        asil = sb(f"{tag}_a", [128, NFC * TC], BF16)
        cmb = sb(f"{tag}_c", [128, 4 * 8])
        acc = sb(f"{tag}_o", [128, 2 * 1024])
        psA = ps(f"{tag}_pA", [128, 1024])
        psO = ps(f"{tag}_pO", [128, 2 * 1024])

        s00 = Sched(kb, tag + "00")
        s00.dma("sync", "ld", b1c[:], kb.dram["L0ab_b1"])
        s00.fence("sync", "ld")
        s00.emit()

        sc = Sched(kb, tag)
        for e_i in range(6):
            act_fn = AF.Gelu if e_i % 2 == 0 else AF.Silu
            sc.dma("sync", "wld",
                   w1r[:].rearrange("p (c f) -> p c f", c=8),
                   kb.dram["bf_L0ab_W1"].rearrange("(n c p) f -> n p c f", c=8, p=128)[e_i],
                   waits=[("mmH", sc.cur("mmH"))])
            sc.dma("gpsimd", "bld", b2r[:],
                   kb.dram["L0ab_b2"][0:1, e_i * 1024:(e_i + 1) * 1024],
                   waits=[("mmO", sc.cur("mmO"))])
            wld_t = sc.cur("wld")
            bld_t = sc.cur("bld")
            for tc in range(T // TC):
                sc.dma("sync", "xld",
                       xnr[:].rearrange("p (c f) -> p c f", c=8),
                       kb.dram[src_fm][:, PAD + tc * TC:PAD + (tc + 1) * TC].rearrange(
                           "(c p) f -> p c f", p=128),
                       waits=[("mmH", sc.cur("mmH") - 8), ("mmO", sc.cur("mmO"))])
                sc.dma("sync", "xld",
                       cmb[:].rearrange("p (i f) -> p i f", f=8),
                       kb.dram["comb"][tc * TC:(tc + 1) * TC, :].rearrange(
                           "(i p) f -> p i f", p=128),
                       waits=[("dvO", sc.cur("dvO"))])
                xld_t = sc.cur("xld")
                for j2 in range(2):
                    sc.dma("sync", "w2ld",
                           w2r[:, j2 * 6 * 1024:(j2 + 1) * 6 * 1024].rearrange(
                               "p (c f) -> p c f", c=6),
                           kb.dram["bf_L0ab_W2"].rearrange(
                               "(n c p) f -> n p c f", c=NFC, p=128)[e_i][:, j2 * 6:(j2 + 1) * 6],
                           waits=[("mmO", sc.cur("mmO"))])
                w2ld_t = sc.cur("w2ld")
                for fc in range(NFC):
                    pb = (fc % 2) * 512
                    def mmH1(e, fc=fc, pb=pb):
                        for c in range(8):
                            mm = e.matmul(psA[:, pb:pb + 512],
                                          w1r[:, c * 1536 + fc * 128: c * 1536 + (fc + 1) * 128],
                                          xnr[:, c * TC: c * TC + TC],
                                          start=(c == 0), stop=(c == 7))
                        return mm
                    sc.mm(mmH1, waits=[("wld", wld_t), ("xld", xld_t),
                                       ("evA", sc.cur("evA") - 1)], inc="mmH")
                    sc.act(lambda e, fc=fc, pb=pb, e_i=e_i, act_fn=act_fn: e.activation(
                        asil[:, fc * TC: fc * TC + TC], psA[:, pb:pb + 512], act_fn,
                        bias=b1c[:, e_i * 12 + fc: e_i * 12 + fc + 1], scale=1.0),
                        waits=[("mmH", sc.cur("mmH")), ("mmO", sc.cur("mmO"))], inc="evA")
                for tt in range(TC // 128):
                    ob = (tt % 2) * 1024
                    def mmO(e, tt=tt, ob=ob):
                        for fc in range(NFC):
                            for half in range(2):
                                mm = e.matmul(
                                    psO[:, ob + half * 512: ob + half * 512 + 512],
                                    asil[:, fc * TC + tt * 128: fc * TC + (tt + 1) * 128],
                                    w2r[:, fc * 1024 + half * 512: fc * 1024 + half * 512 + 512],
                                    start=(fc == 0), stop=False)
                        for half in range(2):
                            mm = e.matmul(psO[:, ob + half * 512: ob + half * 512 + 512],
                                          g["ones_rb"][:],
                                          b2r[0:1, half * 512:(half + 1) * 512],
                                          start=False, stop=True)
                        return mm
                    sc.mm(mmO, waits=[("evA", sc.cur("evA")), ("w2ld", w2ld_t),
                                      ("bld", bld_t), ("dvO", sc.cur("dvO") - 1)],
                          inc="mmO")
                    def d_out(e, tt=tt, ob=ob, e_i=e_i):
                        return e.tensor_scalar(acc[:, ob:ob + 1024], psO[:, ob:ob + 1024],
                                               cmb[:, tt * 8 + e_i: tt * 8 + e_i + 1],
                                               None, ALU.mult)
                    sc.dve(d_out, waits=[("mmO", sc.cur("mmO")),
                                         sc.sw("accst", tt % 2)],
                           inc="dvO")
                    sc.dma("gpsimd", f"accst{tt % 2}",
                           kb.dram[acc_dst][tc * TC + tt * 128: tc * TC + (tt + 1) * 128, :],
                           acc[:, ob:ob + 1024], accum_op=ALU.add,
                           waits=[("dvO", sc.cur("dvO")),
                                  ("accst0", 16 * 16 * e_i), ("accst1", 16 * 16 * e_i)])
        sc.fence("gpsimd", "accst0")
        sc.fence("gpsimd", "accst1")
        sc.emit()


def emit_copy_stream(kb, g, tag, src, dst):
    sc = Sched(kb, tag)
    sc.dma("sync", "cp", kb.dram[dst], kb.dram[src])
    sc.fence("sync", "cp")
    sc.emit()


def emit_L0_attn_b(kb, g, src, dst, tag):
    emit_norm(kb, g, tag + "ln", src, dst_fm="xnbuf", kind="ln")
    emit_l0ab_hg(kb, g, tag + "hg")
    emit_l0ab_router(kb, g, tag + "rt")
    if src != dst:
        emit_copy_stream(kb, g, tag + "cp", src, dst)
    emit_l0ab_experts(kb, g, tag + "ex", acc_dst=dst)



def emit_tm_to_fm(kb, g, tag, src, dst_fm):
    nc = kb.nc
    with ExitStack() as _st:
        sb, ps = mk_tiles(nc, _st)
        xt = sb(f"{tag}_x", [128, 2 * C])
        xbf = sb(f"{tag}_b", [128, 2 * C], BF16)
        fms = sb(f"{tag}_f", [128, 2 * 4096], BF16)
        pst = ps(f"{tag}_p", [128, 2048], BF16)
        sc = Sched(kb, tag)
        for i in range(NT):
            b = (i % 2) * C
            sl = i % 2
            sc.dma("sync", f"ld{sl}", xt[:, b:b + C], kb.dram[src][i * 128:(i + 1) * 128, :],
                   waits=[("dv", i - 1)])
            sc.dve(lambda e, b=b: e.tensor_copy(xbf[:, b:b + C], xt[:, b:b + C]),
                   waits=[sc.sw("ld", sl), ("pe", i - 1)], inc="dv")
            gi = i // 4
            ti = i % 4
            pb = (i % 2) * 1024
            def trs(e, b=b, pb=pb):
                for j in range(8):
                    mm = e.transpose(pst[:, pb + j * 128: pb + j * 128 + 128],
                                     xbf[:, b + j * 128: b + j * 128 + 128],
                                     g["ident"][:])
                return mm
            sc.mm(trs, waits=[("dv", i + 1), ("ev", i - 1)], inc="pe")
            half = (gi % 2) * 4096
            def evac(e, half=half, ti=ti, pb=pb):
                dst = fms[:, half:half + 4096].rearrange(
                    "p (j f) -> p j f", f=512)[:, :, ti * 128:ti * 128 + 128]
                return e.tensor_copy(dst, pst[:, pb:pb + 1024].rearrange(
                    "p (j f) -> p j f", f=128))
            sc.dve(evac, waits=[("pe", i + 1), sc.sw("fm", gi % 2)], inc="ev")
            if ti == 3:
                for j in range(8):
                    sc.dma("gpsimd", f"fm{gi % 2}",
                           kb.dram[dst_fm][j * 128:(j + 1) * 128,
                                           PAD + gi * 512: PAD + (gi + 1) * 512],
                           fms[:, half + j * 512: half + j * 512 + 512],
                           waits=[("ev", i + 1)])
        sc.fence("gpsimd", "fm0")
        sc.fence("gpsimd", "fm1")
        sc.emit()


def emit_l1f_qkv(kb, g, tag, lay):
    nc = kb.nc
    with ExitStack() as _st:
        sb, ps = mk_tiles(nc, _st)
        yf = sb(f"{tag}_y", [128, 8 * 4104], BF16)
        wr = sb(f"{tag}_w", [128, 8 * 1536], BF16)
        tabs = sb(f"{tag}_t", [128, 4 * T], BF16)
        rot = sb(f"{tag}_ro", [128, 128], BF16)
        qscol = sb(f"{tag}_qs", [128, 2])
        qsb = sb(f"{tag}_q", [128, 512])
        sqsb = sb(f"{tag}_sq", [128, 512])
        qbf = sb(f"{tag}_qb", [128, 512], BF16)
        tmp1 = sb(f"{tag}_t1", [128, 512])
        tmp2 = sb(f"{tag}_t2", [128, 512])
        est = sb(f"{tag}_e", [128, 2 * 512], BF16)
        vst = sb(f"{tag}_v", [128, 2 * 512], BF16)
        rrow = sb(f"{tag}_r", [2, 512])
        psA = ps(f"{tag}_pA", [128, 1024])
        psR = ps(f"{tag}_pR", [128, 512])
        psB = ps(f"{tag}_pB", [128, 512])
        psT = ps(f"{tag}_pT", [128, 512], BF16)
        o2h = sb(f"{tag}_o2", [128, 2])
        s2h = sb(f"{tag}_s2", [2, 128])

        s0 = Sched(kb, tag + "0")
        for c in range(8):
            s0.dma("sync", "ld", yf[:, c * 4104:(c + 1) * 4104],
                   kb.dram["yfm"][c * 128:(c + 1) * 128, :])
        s0.dma("sync", "ld", wr[:].rearrange("p (c f) -> p c f", c=8),
               kb.dram["bf_L1f_wqkv"].rearrange("(n c p) f -> n p c f", c=8, p=128)[lay])
        for ti, tn in enumerate(["bf_xposcsA", "bf_xpossnA", "bf_xposcsB", "bf_xpossnB"]):
            s0.dma("sync", "ld", tabs[:, ti * T:(ti + 1) * T], kb.dram[tn])
        s0.dma("sync", "ld", rot[:], kb.dram["bf_rot2"])
        s0.dma("sync", "ld", qscol[:, 0:1], kb.dram["L1f_qs"][:, lay:lay + 1],
               allow_slow_non_contiguous=True)
        s0.dma("sync", "ld", qscol[:, 1:2], kb.dram["L1f_ks"][:, lay:lay + 1],
               allow_slow_non_contiguous=True)
        s0.dma("sync", "ld", o2h[:], kb.dram["ones2h"])
        s0.dma("sync", "ld", s2h[:], kb.dram["sel2h"])
        s0.fence("sync", "ld")
        s0.emit()

        sc = Sched(kb, tag)

        def qkv_mm(e, fch, tc):
            for c in range(8):
                off = PAD + tc * 512 - (1 if c >= 4 else 0)
                mm = e.matmul(psA[:, 0:512],
                              wr[:, c * 1536 + fch * 128: c * 1536 + (fch + 1) * 128],
                              yf[:, c * 4104 + off: c * 4104 + off + 512],
                              start=(c == 0), stop=(c == 7))
            return mm

        for fch in range(8):
            is_q = fch < 4
            dstn = "qfm" if is_q else "kfm"
            cs_off = (0 if is_q else 2) * T
            sn_off = (1 if is_q else 3) * T
            qcol = 0 if is_q else 1
            for tc in range(8):
                sc.mm(lambda e, fch=fch, tc=tc: qkv_mm(e, fch, tc),
                      waits=[("evq", sc.cur("evq"))], inc="mmq")
                sc.act(lambda e: e.copy(qsb[:], psA[:, 0:512]),
                       waits=[("mmq", sc.cur("mmq")), ("dvn", sc.cur("dvn"))], inc="evq")
                sc.dve(lambda e: e.tensor_mul(sqsb[:], qsb[:], qsb[:]),
                       waits=[("dvs", sc.cur("dvs")), ("evq", sc.cur("evq"))], inc="evsq")
                def mm_ss(e):
                    return e.matmul(psR[0:2, 0:512], o2h[:], sqsb[:], start=True, stop=True)
                sc.mm(mm_ss, waits=[("evsq", sc.cur("evsq")), ("dvn", sc.cur("dvn"))],
                      inc="mmss")
                def a_rn(e):
                    kb.ret(e, e.activation(rrow[0:2, :], psR[0:2, 0:512], AF.Ln))
                    return e.activation(rrow[0:2, :], rrow[0:2, :], AF.Exp, scale=-0.5)
                sc.act(a_rn, waits=[("mmss", sc.cur("mmss"))], inc="evr")
                def d_rn(e):
                    return e.tensor_scalar(rrow[0:2, :], rrow[0:2, :], 1e12, None, ALU.min)
                sc.dve(d_rn, waits=[("evr", sc.cur("evr"))], inc="dvr")
                def mm_bc(e):
                    return e.matmul(psB[:, 0:512], s2h[:], rrow[0:2, :],
                                    start=True, stop=True)
                sc.mm(mm_bc, waits=[("dvr", sc.cur("dvr")), ("dvs", sc.cur("dvs"))],
                      inc="mmbc")
                def d_qn(e, qcol=qcol):
                    return e.scalar_tensor_tensor(qbf[:], qsb[:], qscol[:, qcol:qcol + 1],
                                                  psB[:, 0:512], ALU.mult, ALU.mult)
                sc.dve(d_qn, waits=[("mmbc", sc.cur("mmbc"))], inc="dvs")
                def mm_rot(e):
                    return e.matmul(psR[:, 0:512], rot[:], qbf[:], start=True, stop=True)
                sc.mm(mm_rot, waits=[("dvs", sc.cur("dvs")), ("dvn", sc.cur("dvn"))],
                      inc="mmrot")
                qsl = sc.cur("dvn") % 2
                eb = qsl * 512
                def d_rope(e, tc=tc, eb=eb, cs_off=cs_off, sn_off=sn_off):
                    e.tensor_tensor(tmp1[:], qbf[:],
                                    tabs[:, cs_off + tc * 512: cs_off + (tc + 1) * 512],
                                    ALU.mult)
                    e.tensor_tensor(tmp2[:], psR[:, 0:512],
                                    tabs[:, sn_off + tc * 512: sn_off + (tc + 1) * 512],
                                    ALU.mult)
                    return e.tensor_tensor(est[:, eb:eb + 512], tmp1[:], tmp2[:], ALU.add)
                sc.dve(d_rope, waits=[("mmrot", sc.cur("mmrot")),
                                      sc.sw("stq", qsl)], inc="dvn")
                sc.dma("gpsimd", f"stq{qsl}",
                       kb.dram[dstn][(fch % 4) * 128:(fch % 4 + 1) * 128,
                                     tc * 512:(tc + 1) * 512],
                       est[:, eb:eb + 512], waits=[("dvn", sc.cur("dvn"))])
        vidx = 0
        for fch in range(8, 12):
            for tc in range(8):
                pb2 = (vidx % 2) * 512
                sc.mm(lambda e, fch=fch, tc=tc: qkv_mm(e, fch, tc),
                      waits=[("evv", sc.cur("evv")), ("dvn", sc.cur("dvn"))], inc="mmq")
                sc.act(lambda e, pb2=pb2: e.copy(est[:, pb2:pb2 + 512], psA[:, 0:512]),
                       waits=[("mmq", sc.cur("mmq")), ("mmtv", sc.cur("mmtv") - 1),
                              ("stq0", 16 * 32), ("stq1", 16 * 32)], inc="evv")
                def trv(e, pb2=pb2):
                    for j in range(4):
                        mm = e.transpose(psT[:, j * 128:(j + 1) * 128],
                                         est[:, pb2 + j * 128: pb2 + j * 128 + 128],
                                         g["ident"][:])
                    return mm
                sc.mm(trv, waits=[("evv", sc.cur("evv")), ("evtv", sc.cur("evtv"))],
                      inc="mmtv")
                vb = (vidx % 2) * 512
                def evtv(e, vb=vb):
                    return e.tensor_copy(vst[:, vb:vb + 512], psT[:])
                sc.dve(evtv, waits=[("mmtv", sc.cur("mmtv")), sc.sw("stv", vidx % 2)],
                       inc="evtv")
                for j in range(4):
                    sc.dma("gpsimd", f"stv{vidx % 2}",
                           kb.dram["vtm"][(tc * 4 + j) * 128:(tc * 4 + j + 1) * 128,
                                          (fch - 8) * 128:(fch - 7) * 128],
                           vst[:, vb + j * 128: vb + j * 128 + 128],
                           waits=[("evtv", sc.cur("evtv"))])
                vidx += 1
        sc.fence("gpsimd", "stq0")
        sc.fence("gpsimd", "stq1")
        sc.fence("gpsimd", "stv0")
        sc.fence("gpsimd", "stv1")
        sc.emit()


def emit_l1f_ffn(kb, g, tag, lay, stream="ybuf"):
    nc = kb.nc
    NFC = 22
    with ExitStack() as _st:
        sb, ps = mk_tiles(nc, _st)
        w1a = sb(f"{tag}_wa", [128, 8 * 2816], BF16)
        w1g = sb(f"{tag}_wg", [128, 8 * 2816], BF16)
        w2r = sb(f"{tag}_w2", [128, NFC * 1024], BF16)
        b1a = sb(f"{tag}_ba", [128, 22])
        b1g = sb(f"{tag}_bg", [128, 22])
        b2r = sb(f"{tag}_b2", [1, 1024], BF16)
        gbt = sb(f"{tag}_gb", [128, 2 * C])
        grow = sb(f"{tag}_gr", [1, 2 * C])
        ht = sb(f"{tag}_h", [128, C])
        hbf = sb(f"{tag}_hb", [128, C], BF16)
        sq = sb(f"{tag}_q", [128, C])
        st = sb(f"{tag}_s", [128, 8])
        hnf = sb(f"{tag}_hf", [128, 8 * 512], BF16)
        gsb = sb(f"{tag}_g", [128, 2 * 512], BF16)
        asil = sb(f"{tag}_a", [128, NFC * 512], BF16)
        ysb = sb(f"{tag}_y", [128, 1024])
        psA = ps(f"{tag}_pA", [128, 1024])
        psB = ps(f"{tag}_pB", [128, 1024])
        psT = ps(f"{tag}_pT", [128, 1024], BF16)
        psO = ps(f"{tag}_pO", [128, 1024])
        psG = psO

        s0 = Sched(kb, tag + "0")
        s0.dma("sync", "ld", w1a[:].rearrange("p (c f) -> p c f", c=8),
               kb.dram["bf_L1f_w1a"].rearrange("(n c p) f -> n p c f", c=8, p=128)[lay])
        s0.dma("sync", "ld", w1g[:].rearrange("p (c f) -> p c f", c=8),
               kb.dram["bf_L1f_w1g"].rearrange("(n c p) f -> n p c f", c=8, p=128)[lay])
        s0.dma("sync", "ld", b1a[:], kb.dram["L1f_b1a"][:, lay * 22:(lay + 1) * 22])
        s0.dma("sync", "ld", b1g[:], kb.dram["L1f_b1g"][:, lay * 22:(lay + 1) * 22])
        s0.dma("gpsimd", "ldb", b2r[:], kb.dram["L1f_b2"][lay:lay + 1, :])
        s0.fence("gpsimd", "ldb")
        s0.dma("sync", "ld", grow[0:1, 0:C], kb.dram["L1f_lng"][lay:lay + 1, :])
        s0.dma("sync", "ld", grow[0:1, C:2 * C], kb.dram["L1f_lnb"][lay:lay + 1, :])
        nld = s0.cur("ld")
        s0.mm(lambda e: (e.matmul(psG[:, 0:512], g["ones_r"][:], grow[0:1, 0:512],
                                  start=True, stop=True),
                         e.matmul(psG[:, 512:1024], g["ones_r"][:], grow[0:1, 512:1024],
                                  start=True, stop=True))[-1],
              waits=[("ld", nld)], inc="bcm")
        s0.act(lambda e: e.copy(gbt[:, 0:1024], psG[:]), waits=[("bcm", 1)], inc="bce")
        s0.mm(lambda e: (e.matmul(psG[:, 0:512], g["ones_r"][:], grow[0:1, 1024:1536],
                                  start=True, stop=True),
                         e.matmul(psG[:, 512:1024], g["ones_r"][:], grow[0:1, 1536:2048],
                                  start=True, stop=True))[-1],
              waits=[("bce", 1)], inc="bcm")
        s0.act(lambda e: e.copy(gbt[:, 1024:2048], psG[:]), waits=[("bcm", 2)], inc="bce")
        s0.emit()

        sc = Sched(kb, tag)
        for tc in range(8):
            for ti in range(4):
                i = tc * 4 + ti
                hb = 0
                sc.dma("sync", "hld", ht[:, hb:hb + 512],
                       kb.dram[stream][i * 128:(i + 1) * 128, 0:512],
                       waits=[("dvh", sc.cur("dvh"))])
                if i == 0:
                    sc.dma("sync", "hld", ht[1:128, hb + 512:hb + 1024],
                           kb.dram[stream][0:127, 512:1024], waits=[])
                else:
                    sc.dma("sync", "hld", ht[:, hb + 512:hb + 1024],
                           kb.dram[stream][i * 128 - 1:(i + 1) * 128 - 1, 512:1024],
                           waits=[])
                hld_t = sc.cur("hld")

                def d_ln2(e, hb=hb, i=i):
                    R = kb.ret
                    if i == 0:
                        kb.ret(e, e.memset(ht[0:1, hb + 512:hb + 1024], 0.0))
                    hv = ht[:, hb:hb + 1024]
                    R(e, e.reduce_sum(st[:, 0:1], hv, AX.X))
                    R(e, e.tensor_scalar(st[:, 1:2], st[:, 0:1], 1.0 / C, None, ALU.mult))
                    e.tensor_scalar(hv, hv, st[:, 1:2], None, ALU.subtract)
                    e.tensor_mul(sq[:], hv, hv)
                    R(e, e.reduce_sum(st[:, 2:3], sq[:], AX.X))
                    return e.tensor_scalar(st[:, 3:4], st[:, 2:3], 1.0 / C, None, ALU.mult)
                sc.dve(d_ln2, waits=[("hld", hld_t), ("evl", sc.cur("evl"))], inc="dvl")
                def a_rstd4(e):
                    kb.ret(e, e.activation(st[:, 4:5], st[:, 3:4], AF.Ln,
                                           bias=g["eps_ln"][:], scale=1.0))
                    return e.activation(st[:, 5:6], st[:, 4:5], AF.Exp, scale=-0.5)
                sc.act(a_rstd4, waits=[("dvl", sc.cur("dvl"))], inc="evl")
                def d_hn2(e, hb=hb):
                    hv = ht[:, hb:hb + 1024]
                    e.tensor_scalar(hv, hv, st[:, 5:6], None, ALU.mult)
                    e.tensor_tensor(hv, hv, gbt[:, 0:1024], ALU.mult)
                    return e.tensor_tensor(hbf[:], hv, gbt[:, 1024:2048], ALU.add)
                sc.dve(d_hn2, waits=[("evl", sc.cur("evl")), ("mmt", sc.cur("mmt"))],
                       inc="dvh")
                def tr_h(e, ti=ti):
                    for j in range(8):
                        mm = e.transpose(psT[:, j * 128:(j + 1) * 128],
                                         hbf[:, j * 128:(j + 1) * 128], g["ident"][:])
                    return mm
                sc.mm(tr_h, waits=[("dvh", sc.cur("dvh")), ("evt", sc.cur("evt"))],
                      inc="mmt")
                def ev_h(e, ti=ti):
                    dst = hnf[:].rearrange("p (j f) -> p j f", f=512)[:, :, ti * 128:ti * 128 + 128]
                    return e.tensor_copy(dst, psT[:].rearrange("p (j f) -> p j f", f=128))
                sc.dve(ev_h, waits=[("mmt", sc.cur("mmt")), ("mma", sc.cur("mma")),
                                    ("mmg", sc.cur("mmg"))],
                       inc="evt")
            for j2 in range(2):
                sc.dma("sync", "w2ld",
                       w2r[:, j2 * 11 * 1024:(j2 + 1) * 11 * 1024].rearrange(
                           "p (c f) -> p c f", c=11),
                       kb.dram["bf_L1f_w2"].rearrange(
                           "(n c p) f -> n p c f", c=NFC, p=128)[lay][:, j2 * 11:(j2 + 1) * 11],
                       waits=[("mmo", sc.cur("mmo"))])
            w2ld_t = sc.cur("w2ld")
            for fc in range(NFC):
                pb = (fc % 2) * 512
                def mmg(e, fc=fc, pb=pb):
                    for c in range(8):
                        mm = e.matmul(psB[:, pb:pb + 512],
                                      w1g[:, c * 2816 + fc * 128: c * 2816 + (fc + 1) * 128],
                                      hnf[:, c * 512:(c + 1) * 512],
                                      start=(c == 0), stop=(c == 7))
                    return mm
                sc.mm(mmg, waits=[("evt", 4 * (tc + 1)), ("evg", sc.cur("evg") - 1)],
                      inc="mmg")
                sc.act(lambda e, fc=fc, pb=pb: e.activation(
                    gsb[:, pb:pb + 512], psB[:, pb:pb + 512], AF.Gelu,
                    bias=b1g[:, fc:fc + 1], scale=1.0),
                    waits=[("mmg", sc.cur("mmg")), ("dva", sc.cur("dva") - 1)], inc="evg")
                def mma(e, fc=fc, pb=pb):
                    for c in range(8):
                        mm = e.matmul(psA[:, pb:pb + 512],
                                      w1a[:, c * 2816 + fc * 128: c * 2816 + (fc + 1) * 128],
                                      hnf[:, c * 512:(c + 1) * 512],
                                      start=(c == 0), stop=(c == 7))
                    return mm
                sc.mm(mma, waits=[("dva", sc.cur("dva") - 1)], inc="mma")
                def d_a(e, fc=fc, pb=pb):
                    return e.scalar_tensor_tensor(
                        asil[:, fc * 512:(fc + 1) * 512], psA[:, pb:pb + 512],
                        b1a[:, fc:fc + 1], gsb[:, pb:pb + 512],
                        ALU.add, ALU.mult)
                sc.dve(d_a, waits=[("mma", sc.cur("mma")), ("evg", sc.cur("evg")),
                                   ("mmo", sc.cur("mmo"))], inc="dva")
            for ti in range(4):
                i = tc * 4 + ti
                ob = 0
                sc.dma("sync", "yld", ysb[:, ob:ob + 1024],
                       kb.dram[stream][i * 128:(i + 1) * 128, :],
                       waits=[("sty", 16 * (sc.cur("sty") // 16))])
                yld_t = sc.cur("yld")
                def mmo(e, ti=ti, ob=ob):
                    for fc in range(NFC):
                        for half in range(2):
                            mm = e.matmul(
                                psO[:, half * 512:(half + 1) * 512],
                                asil[:, fc * 512 + ti * 128: fc * 512 + (ti + 1) * 128],
                                w2r[:, fc * 1024 + half * 512: fc * 1024 + half * 512 + 512],
                                start=(fc == 0), stop=False)
                    for half in range(2):
                        mm = e.matmul(psO[:, half * 512:(half + 1) * 512],
                                      g["ones_rb"][:], b2r[0:1, half * 512:(half + 1) * 512],
                                      start=False, stop=True)
                    return mm
                sc.mm(mmo, waits=[("dva", sc.cur("dva")), ("w2ld", w2ld_t),
                                  ("dvo", sc.cur("dvo"))], inc="mmo")
                def d_res(e, ob=ob):
                    return e.tensor_tensor(ysb[:, ob:ob + 1024], psO[:],
                                           ysb[:, ob:ob + 1024], ALU.add)
                sc.dve(d_res, waits=[("mmo", sc.cur("mmo")), ("yld", yld_t),
                                     ("sty", 16 * (sc.cur("sty") // 16))], inc="dvo")
                sc.dma("gpsimd", "sty", kb.dram[stream][i * 128:(i + 1) * 128, :],
                       ysb[:, ob:ob + 1024], waits=[("dvo", sc.cur("dvo"))])
        sc.fence("gpsimd", "sty")
        sc.emit()


def emit_L1_ffn(kb, g, src, dst, tag):
    emit_norm(kb, g, tag + "ln", src, dst_tm="ybuf", dst_fm="yfm", kind="ln")
    for lay in range(2):
        emit_l1f_qkv(kb, g, f"{tag}q{lay}", lay)
        emit_flash2(kb, g, f"{tag}f{lay}", "qfm", "kfm", "vtm", "attout",
                    nheads=8, causal=True, scale=8.0, local_tiles=4)
        emit_proj_residual(kb, g, f"{tag}p{lay}", "attout", 512, "bf_L1f_wo",
                           "ybuf", "ybuf", w_row0=lay * 512)
        emit_tm_to_fm(kb, g, f"{tag}c{lay}", "ybuf", "yfm")
        emit_l1f_ffn(kb, g, f"{tag}m{lay}", lay)
        if lay == 0:
            emit_tm_to_fm(kb, g, f"{tag}d{lay}", "ybuf", "yfm")
    emit_stream_add(kb, g, tag + "sa", src, "ybuf", dst)



def emit_flash2(kb, g, tag, qfm, kfm, vtm, attout, nheads=16, causal=True,
                scale=0.125, kv_of_head=None, local_tiles=None):
    import os as _os
    HEAVY = _os.environ.get("FLASH_HEAVY", "") == "1"
    HEAVY_A = HEAVY or _os.environ.get("FLASH_HA", "") == "1"
    HEAVY_B = HEAVY or _os.environ.get("FLASH_HB", "") == "1"
    """Pass 1: negmax per (head, q-tile) from S in [tq, tk] orientation.
    Pass 2: S^T = k_aug^T q_aug per tk-tile (K=65 with ones-row in k and
    negmax-row in q -> S - max baked in), exp straight from PSUM (scale in
    exp), PV with ones-column in V for free denominators, normalize at the
    end. attout rows h*64 feature-major."""
    nc = kb.nc
    if kv_of_head is None:
        kv_of_head = lambda h: h
    NTQC = NT // 4
    with ExitStack() as _st:
        sb, ps = mk_tiles(nc, _st)
        kres = sb(f"{tag}_k", [128, 4 * T], BF16)
        vres = sb(f"{tag}_v", [128, 4 * NT * 65], BF16)
        qa = sb(f"{tag}_q", [128, 2 * 512], BF16)
        q1 = sb(f"{tag}_q1", [128, 2 * 128], BF16)
        Pt = sb(f"{tag}_P", [128, 2 * 512], BF16)
        osb = sb(f"{tag}_o", [128, 512])
        rrow = sb(f"{tag}_r", [1, 512])
        outst = sb(f"{tag}_os", [64, T], BF16)
        negm = sb(f"{tag}_n", [128, 16 * NT])
        mx = sb(f"{tag}_m", [128, 8])
        masks = sb(f"{tag}_msk", [128, 8 * 512])
        psS = ps(f"{tag}_pS", [128, 1024])
        psO = ps(f"{tag}_pO", [128, 512])
        psR = ps(f"{tag}_pR", [1, 512])
        psB = ps(f"{tag}_pB", [128, 512])

        m1 = sb(f"{tag}_m1", [128, 640])
        s0 = Sched(kb, tag + "0")
        if local_tiles:
            for k in range(8):
                s0.dma("sync", "ld", masks[:, k * 512:(k + 1) * 512],
                       kb.dram[f"lmaskT_{k}"])
            s0.dma("sync", "ld", m1[:], kb.dram["bandmask"])
        elif causal:
            for d in range(4):
                s0.dma("sync", "ld", masks[:, d * 512:(d + 1) * 512],
                       kb.dram[f"cmaskT_{d}"])
            s0.dma("sync", "ld", m1[:, 0:128], kb.dram["madd"])
        s0.fence("sync", "ld")
        s0.emit()

        groups = []
        cur, cur_kv = [], []
        for h in range(nheads):
            kv = kv_of_head(h)
            nk = cur_kv + ([kv] if kv not in cur_kv else [])
            if len(nk) > 4 and cur:
                groups.append((cur, cur_kv))
                cur, cur_kv = [], []
                nk = [kv]
            cur.append(h)
            cur_kv = nk
        groups.append((cur, cur_kv))

        sc = Sched(kb, tag)
        for heads, kvs in groups:
            kvpos = {kv: j for j, kv in enumerate(kvs)}
            gate = [("pvm", sc.cur("pvm")), ("mmS", sc.cur("mmS")),
                    ("mm1", sc.cur("mm1"))]
            for j, kv in enumerate(kvs):
                sc.dma("sync", "kld", kres[0:64, j * T:(j + 1) * T],
                       kb.dram[kfm][kv * 64:(kv + 1) * 64, :], waits=gate)
                sc.dma("sync", "kld",
                       vres[:, j * NT * 65:(j + 1) * NT * 65].rearrange(
                           "p (i c) -> p i c", c=65)[:, :, 0:64],
                       kb.dram[vtm][:, kv * 64:(kv + 1) * 64].rearrange(
                           "(i p) c -> p i c", p=128),
                       waits=[])
            kld_t = sc.cur("kld")
            def inst_ones(e, nkv=len(kvs)):
                e.memset(kres[64:65, 0:nkv * T], 1.0)
                return e.memset(
                    vres[:, 0:nkv * NT * 65].rearrange("p (i c) -> p i c", c=65)[:, :, 64:65],
                    1.0)
            sc.dve(inst_ones, waits=gate + [("kld", kld_t)], inc="ons")

            for h in heads:
                kj = kvpos[kv_of_head(h)]
                hh = h - heads[0]
                for qt in range(NT):
                    u1 = sc.cur("mx1")
                    qb = (sc.cur("q1l") // 16 % 2) * 128
                    sc.dma("sync", "q1l", q1[0:64, qb:qb + 128],
                           kb.dram[qfm][h * 64:(h + 1) * 64, qt * 128:(qt + 1) * 128],
                           waits=[("mm1", sc.cur("mm1"))])
                    q1l_t = sc.cur("q1l")
                    if local_tiles:
                        ks = max(0, qt - local_tiles)
                        strip = (qt - ks + 1) * 128
                    else:
                        ks = 0
                        strip = ((qt + 1) if causal else NT) * 128
                    nchunk = (strip + 511) // 512
                    for ci in range(nchunk):
                        c0 = ci * 512
                        w = min(512, strip - c0)
                        pb = (sc.cur("mm1") % 2) * 512
                        def s1mm(e, c0=c0, w=w, pb=pb, qb=qb, kj=kj, ks=ks):
                            return e.matmul(
                                psS[:, pb:pb + w], q1[0:64, qb:qb + 128],
                                kres[0:64, kj * T + ks * 128 + c0:
                                     kj * T + ks * 128 + c0 + w],
                                start=True, stop=True)
                        sc.mm(s1mm, waits=[("q1l", q1l_t), ("ons", sc.cur("ons")),
                                           ("dmx", sc.cur("dmx") - 1)], inc="mm1")
                        def d_mr(e, ci=ci, pb=pb, w=w, c0=c0, strip=strip):
                            if local_tiles:
                                off = 640 - strip
                                e.tensor_tensor(psS[:, pb:pb + w], psS[:, pb:pb + w],
                                                m1[:, off + c0: off + c0 + w], ALU.add)
                            elif causal and c0 + w == strip:
                                e.tensor_tensor(psS[:, pb + w - 128: pb + w],
                                                psS[:, pb + w - 128: pb + w],
                                                m1[:, 0:128], ALU.add)
                            return e.tensor_reduce(
                                mx[:, ci:ci + 1], psS[:, pb:pb + w], AX.X, ALU.max)
                        sc.dve(d_mr, waits=[("mm1", sc.cur("mm1"))], inc="dmx")
                    def d_fm(e, nchunk=nchunk, h=h, hh=hh, qt=qt):
                        return e.tensor_reduce(negm[:, hh * NT + qt: hh * NT + qt + 1],
                                               mx[:, 0:nchunk], AX.X, ALU.max,
                                               negate=True)
                    sc.dve(d_fm, waits=[("dmx", sc.cur("dmx")),
                                        ("mmN", sc.cur("mmN"))], inc="ngm")

            ngm_t = sc.cur("ngm")

            for h in heads:
                kj = kvpos[kv_of_head(h)]
                hh = h - heads[0]
                for tqc in range(NTQC):
                    qb = (sc.cur("qld") // 16 % 2) * 512
                    sc.dma("sync", "qld", qa[0:64, qb:qb + 512],
                           kb.dram[qfm][h * 64:(h + 1) * 64, tqc * 512:(tqc + 1) * 512],
                           waits=[("mmS", sc.cur("mmS"))])
                    qld_t = sc.cur("qld")
                    def t_n(e, hh=hh, tqc=tqc):
                        for s in range(4):
                            mm = e.transpose(psR[0:1, s * 128:(s + 1) * 128],
                                             negm[:, hh * NT + tqc * 4 + s:
                                                  hh * NT + tqc * 4 + s + 1],
                                             g["identf"][:])
                        return mm
                    sc.mm(t_n, waits=[("ngm", ngm_t), ("dvq", sc.cur("dvq"))], inc="mmN")
                    sc.dve(lambda e, qb=qb: e.tensor_copy(qa[64:65, qb:qb + 512],
                                                          psR[0:1, :]),
                           waits=[("mmN", sc.cur("mmN")), ("qld", qld_t)], inc="dvq")
                    dvq_t = sc.cur("dvq")
                    if local_tiles:
                        t_lo = max(0, tqc * 4 - local_tiles)
                        t_hi = tqc * 4 + 3
                    else:
                        t_lo = 0
                        t_hi = (tqc * 4 + 3) if causal else NT - 1
                    ntile = t_hi - t_lo + 1
                    first = True
                    for tk in range(t_lo, t_hi + 1):
                        pb = (sc.cur("mmS") % 2) * 512
                        def s2mm(e, tk=tk, pb=pb, qb=qb, kj=kj):
                            return e.matmul(
                                psS[:, pb:pb + 512],
                                kres[0:65, kj * T + tk * 128: kj * T + (tk + 1) * 128],
                                qa[0:65, qb:qb + 512],
                                start=True, stop=True)
                        sc.mm(s2mm, waits=[("dvq", dvq_t),
                                           ("evP", sc.cur("evP") - (0 if HEAVY_A else 1))],
                              inc="mmS")
                        d = tk - tqc * 4
                        if local_tiles:
                            mi = d + 4
                            need_mask = True
                        else:
                            mi = d
                            need_mask = causal and 0 <= d <= 3
                        if need_mask:
                            sc.dve(lambda e, pb=pb, mi=mi: e.tensor_tensor(
                                psS[:, pb:pb + 512], psS[:, pb:pb + 512],
                                masks[:, mi * 512:(mi + 1) * 512], ALU.add),
                                waits=[("mmS", sc.cur("mmS"))], inc="dvm")
                        ptb = (sc.cur("evP") % 2) * 512
                        def e_exp(e, pb=pb, ptb=ptb, scale=scale):
                            return e.activation(Pt[:, ptb:ptb + 512], psS[:, pb:pb + 512],
                                                AF.Exp, scale=scale)
                        sc.act(e_exp, waits=[("dvm", sc.cur("dvm")),
                                             ("mmS", sc.cur("mmS")),
                                             ("pvm", sc.cur("pvm") - (0 if HEAVY_B else 1))],
                               inc="evP")
                        def pvmm(e, tk=tk, ptb=ptb, kj=kj, first=first,
                                 last=(tk == t_hi)):
                            return e.matmul(
                                psO[0:65, :],
                                vres[:, kj * NT * 65 + tk * 65: kj * NT * 65 + (tk + 1) * 65],
                                Pt[:, ptb:ptb + 512],
                                start=first, stop=last)
                        sc.mm(pvmm, waits=[("evP", sc.cur("evP")),
                                           ("evO", sc.cur("evO"))], inc="pvm")
                        first = False
                    sc.act(lambda e: e.copy(osb[0:65, :], psO[0:65, :]),
                           waits=[("pvm", sc.cur("pvm")), ("dvf", sc.cur("dvf"))],
                           inc="evO")
                    def d_rr(e):
                        return e.reciprocal(rrow[:], osb[64:65, :])
                    sc.dve(d_rr, waits=[("evO", sc.cur("evO")), ("bcm", sc.cur("bcm"))],
                           inc="dvr")
                    def b_mm(e):
                        return e.matmul(psB[:, 0:512], g["ones_r"][:], rrow[:],
                                        start=True, stop=True)
                    sc.mm(b_mm, waits=[("dvr", sc.cur("dvr")), ("dvf", sc.cur("dvf"))],
                          inc="bcm")
                    def d_fin(e, tqc=tqc):
                        return e.tensor_tensor(outst[0:64, tqc * 512:(tqc + 1) * 512],
                                               osb[0:64, :], psB[0:64, 0:512], ALU.mult)
                    sc.dve(d_fin, waits=[("bcm", sc.cur("bcm")), ("ost", sc.cur("ost"))],
                           inc="dvf")
                sc.dma("gpsimd", "ost", kb.dram[attout][h * 64:(h + 1) * 64, :],
                       outst[0:64, :], waits=[("dvf", sc.cur("dvf"))])
        sc.fence("gpsimd", "ost")
        sc.emit()


# version 07 (final)
# speedup vs baseline: 1.1032x; 1.1032x over previous
from contextlib import ExitStack

import numpy as np
import ml_dtypes

import concourse.bass as bass
import concourse.mybir as mybir
from concourse.bass_utils import run_bass_kernel_spmd

F32 = mybir.dt.float32
BF16 = mybir.dt.bfloat16
AF = mybir.ActivationFunctionType
ALU = mybir.AluOpType
AX = mybir.AxisListType

T = 4096
C = 1024
NT = T // 128
PAD = 8
PI = float(np.pi)
INV_2PI = float(1.0 / (2 * np.pi))
MAGIC = 1.5 * 2.0 ** 23
CW1 = 6.25
CW2 = 17.0 / 512.0
CW3 = float(float(np.float64(2 * np.pi)) - 6.25 - 17.0 / 512.0)
NEG_BIG = -3.0e38
CLEAR_SEMS = False
LAG_OPS = False
bf16 = ml_dtypes.bfloat16

ENGINES = ("sync", "tensor", "vector", "scalar", "gpsimd")



def _colmajor(v, p=128):
    v = np.ascontiguousarray(v)
    n = v.shape[0] // p
    return np.ascontiguousarray(v.reshape(n, p).T)


def stage_weights(inp):
    s = {}
    s["L0aa_convw"] = np.ascontiguousarray(
        inp["L0_attn_a_conv_w"].squeeze(1).reshape(8, 128, 4).transpose(1, 0, 2).reshape(128, 32))
    s["L0aa_convb"] = _colmajor(inp["L0_attn_a_conv_b"])
    s["L0aa_qkw"] = np.ascontiguousarray(inp["L0_attn_a_qk_w"].T)
    s["L0aa_vw"] = np.ascontiguousarray(inp["L0_attn_a_v_w"].T)
    s["L0aa_gw"] = np.ascontiguousarray(inp["L0_attn_a_g_w"].T)
    s["L0aa_gb"] = _colmajor(inp["L0_attn_a_g_b"])
    s["L0aa_ow"] = np.ascontiguousarray(inp["L0_attn_a_o_w"].T)
    s["L0aa_lng"] = inp["L0_attn_a_ln_g"].reshape(1, C)
    s["L0aa_lnb"] = inp["L0_attn_a_ln_b"].reshape(1, C)
    s["L0aa_freqs"] = np.ascontiguousarray(inp["L0_attn_a_freqs"].T)
    s["L0aa_gamma"] = inp["L0_attn_a_gamma"].reshape(1, 16)

    s["L0f_g1"] = inp["L0_ffn_g1"].reshape(1, C)
    s["L0f_g2"] = inp["L0_ffn_g2"].reshape(1, C)
    s["L0f_wq"] = inp["L0_ffn_wq"]
    s["L0f_wk"] = inp["L0_ffn_wk"]
    s["L0f_wv"] = inp["L0_ffn_wv"]
    s["L0f_wo"] = inp["L0_ffn_wo"]
    gp = np.zeros((1024, 128), np.float32); gp[:, 0:4] = inp["L0_ffn_gate"]
    s["L0f_gate"] = gp
    s["L0f_w1"] = inp["L0_ffn_w1"]
    s["L0f_w2"] = inp["L0_ffn_w2"]
    s["L0f_w3"] = inp["L0_ffn_w3"]

    s["L0ab_r1w"] = inp["L0_attn_b_r1_w"]
    s["L0ab_r1b"] = _colmajor(inp["L0_attn_b_r1_b"])
    s["L0ab_lng"] = _colmajor(inp["L0_attn_b_ln_g"])
    s["L0ab_lnb"] = _colmajor(inp["L0_attn_b_ln_b"])
    r2p = np.zeros((512, 128), np.float32); r2p[:, 0:6] = inp["L0_attn_b_r2_w"]
    s["L0ab_r2w"] = r2p
    s["L0ab_r2b"] = inp["L0_attn_b_r2_b"].reshape(1, 6)
    s["L0ab_temp"] = inp["L0_attn_b_temp"].reshape(1, 1)
    s["L0ab_rbias"] = inp["L0_attn_b_route_bias"].reshape(1, 6)
    s["L0ab_W1"] = inp["L0_attn_b_W1"]
    s["L0ab_b1"] = np.concatenate([_colmajor(inp["L0_attn_b_b1"][e]) for e in range(6)], axis=1)
    s["L0ab_W2"] = inp["L0_attn_b_W2"]
    s["L0ab_b2"] = inp["L0_attn_b_b2"].reshape(1, 6 * 1024)

    s["L1aa_kvd"] = inp["L1_attn_a_w_kv_down"]
    s["L1aa_kvu"] = inp["L1_attn_a_w_kv_up"]
    s["L1aa_wq"] = inp["L1_attn_a_w_q"]
    s["L1aa_wo"] = inp["L1_attn_a_w_out"]

    wq = inp["L1_attn_b_wq"]; wk = inp["L1_attn_b_wk"]; wv = inp["L1_attn_b_wv"]
    s["L1ab_wq"] = np.stack([np.concatenate([wq[2 * i], wq[2 * i + 1]], axis=1) for i in range(8)])
    s["L1ab_wk"] = np.stack([np.concatenate([wk[2 * i], wk[2 * i + 1]], axis=1) for i in range(8)])
    s["L1ab_wv"] = np.stack([np.concatenate([wv[2 * i], wv[2 * i + 1]], axis=1) for i in range(8)])
    def bias2(b):
        return np.stack([np.concatenate([b[2 * i, 0], b[2 * i + 1, 0]]) for i in range(8)], axis=1)
    s["L1ab_bq"] = bias2(inp["L1_attn_b_bq"])
    s["L1ab_bk"] = bias2(inp["L1_attn_b_bk"])
    s["L1ab_bv"] = bias2(inp["L1_attn_b_bv"])
    s["L1ab_wo"] = np.ascontiguousarray(inp["L1_attn_b_wo"].T)
    s["L1ab_bo"] = inp["L1_attn_b_bo"].reshape(1, C)

    s["L1f_wqkv"] = inp["L1_ffn_wqkv"]
    s["L1f_qs"] = np.ascontiguousarray(np.concatenate([inp["L1_ffn_q_scale"]] * 2, axis=1).T)
    s["L1f_ks"] = np.ascontiguousarray(np.concatenate([inp["L1_ffn_k_scale"]] * 2, axis=1).T)
    s["L1f_wo"] = inp["L1_ffn_wo"]
    s["L1f_lng"] = inp["L1_ffn_ln_g"]
    s["L1f_lnb"] = inp["L1_ffn_ln_b"]
    HID, HIDP = 2730, 2816
    w1 = inp["L1_ffn_w1"]
    w1a = np.zeros((2, 1024, HIDP), np.float32); w1a[:, :, :HID] = w1[:, :, :HID]
    w1g = np.zeros((2, 1024, HIDP), np.float32); w1g[:, :, :HID] = w1[:, :, HID:]
    s["L1f_w1a"] = w1a; s["L1f_w1g"] = w1g
    b1 = inp["L1_ffn_b1"]
    b1a = np.zeros((2, HIDP), np.float32); b1a[:, :HID] = b1[:, :HID]
    b1g = np.zeros((2, HIDP), np.float32); b1g[:, :HID] = b1[:, HID:]
    s["L1f_b1a"] = np.concatenate([_colmajor(b1a[l]) for l in range(2)], axis=1)
    s["L1f_b1g"] = np.concatenate([_colmajor(b1g[l]) for l in range(2)], axis=1)
    w2p = np.zeros((2, HIDP, 1024), np.float32); w2p[:, :HID, :] = inp["L1_ffn_w2"]
    s["L1f_w2"] = w2p
    s["L1f_b2"] = inp["L1_ffn_b2"]

    out = {}
    for k, v in s.items():
        v = np.ascontiguousarray(v, dtype=np.float32)
        if k in WCAST:
            v = v.reshape(-1, v.shape[-1])
        out[k] = v
    return out


def stage_consts():
    c = {}
    hd = 64
    t = np.arange(T, dtype=np.float32)
    inv_freq = (1.0 / 10000.0 ** (np.arange(0, hd, 2, dtype=np.float32) / np.float32(hd))).astype(np.float32)
    fr = (t[:, None] * inv_freq[None, :]).astype(np.float32)
    emb = np.concatenate([fr, fr], axis=-1)
    cos = np.cos(emb.astype(np.float64)).astype(np.float32)
    sin = np.sin(emb.astype(np.float64)).astype(np.float32)
    c["ropecos"] = np.ascontiguousarray(np.vstack([cos.T, cos.T]))
    c["ropesin"] = np.ascontiguousarray(np.vstack([sin.T, sin.T]))

    half = np.arange(0, hd, 2, dtype=np.float32)
    inv2 = (1.0 / 10000.0 ** (half / np.float32(hd))).astype(np.float32)
    fr2 = (t[:, None] * inv2[None, :]).astype(np.float32)
    fr2 = np.concatenate([fr2, fr2], axis=-1)
    base = ((half + np.float32(0.4 * hd)) / np.float32(1.4 * hd)).astype(np.float32)
    power = ((t - T // 2) / np.float32(256.0)).astype(np.float32)
    scale = (base[None, :] ** power[:, None]).astype(np.float32)
    scale = np.concatenate([scale, scale], axis=-1)
    xcos = np.cos(fr2.astype(np.float64)).astype(np.float32)
    xsin = np.sin(fr2.astype(np.float64)).astype(np.float32)
    for nm, tab in [("xposcsA", xcos * scale), ("xpossnA", xsin * scale),
                    ("xposcsB", xcos / scale), ("xpossnB", xsin / scale)]:
        tab = tab.astype(np.float32)
        c[nm] = np.ascontiguousarray(np.vstack([tab.T, tab.T]))

    R = np.zeros((hd, hd), np.float32)
    for d in range(32):
        R[d + 32, d] = -1.0
        R[d, d + 32] = 1.0
    R2 = np.zeros((128, 128), np.float32)
    R2[:64, :64] = R; R2[64:, 64:] = R
    c["rot2"] = R2

    ii = np.arange(128)
    c["madd"] = np.where(ii[None, :] <= ii[:, None], 0.0, NEG_BIG).astype(np.float32)
    cc = np.arange(640)
    ok = (cc[None, :] >= ii[:, None]) & (cc[None, :] <= ii[:, None] + 512)
    c["bandmask"] = np.where(ok, 0.0, NEG_BIG).astype(np.float32)
    c["trow"] = t.reshape(1, T).copy()
    rr = np.arange(128)[:, None]
    ccc = np.arange(128)[None, :]
    tri_le = np.where(rr <= ccc, 0.0, NEG_BIG).astype(np.float32)
    tri_ge = np.where(rr >= ccc, 0.0, NEG_BIG).astype(np.float32)
    zz = np.zeros((128, 128), np.float32)
    bb = np.full((128, 128), NEG_BIG, np.float32)
    for d in range(4):
        blocks = []
        for s in range(4):
            if s > d:
                blocks.append(zz)
            elif s == d:
                blocks.append(tri_le)
            else:
                blocks.append(bb)
        c[f"cmaskT_{d}"] = np.concatenate(blocks, axis=1)
    for k in range(8):
        dd = k - 4
        blocks = []
        for s in range(4):
            dl = dd - s
            if dl == 0:
                blocks.append(tri_le)
            elif dl == -4:
                blocks.append(tri_ge)
            elif -3 <= dl <= -1:
                blocks.append(zz)
            else:
                blocks.append(bb)
        c[f"lmaskT_{k}"] = np.concatenate(blocks, axis=1)
    ones2 = np.zeros((128, 2), np.float32); ones2[:64, 0] = 1.0; ones2[64:, 1] = 1.0
    c["ones2h"] = ones2
    sel2 = np.zeros((2, 128), np.float32); sel2[0, :64] = 1.0; sel2[1, 64:] = 1.0
    c["sel2h"] = sel2
    return c


WCAST = {
    "L0aa_qkw": (1024, 2048), "L0aa_vw": (1024, 2048), "L0aa_gw": (1024, 2048),
    "L0aa_ow": (2048, 1024),
    "L0f_wq": (1024, 1024), "L0f_wk": (1024, 256), "L0f_wv": (1024, 256),
    "L0f_wo": (1024, 1024), "L0f_gate": (1024, 128),
    "L0f_w1": (4, 1024, 3072), "L0f_w2": (4, 3072, 1024), "L0f_w3": (4, 1024, 3072),
    "L0ab_r1w": (1024, 512), "L0ab_r2w": (512, 128),
    "L0ab_W1": (6, 1024, 1536), "L0ab_W2": (6, 1536, 1024),
    "L1aa_kvd": (1024, 128), "L1aa_kvu": (128, 2048), "L1aa_wq": (1024, 1024),
    "L1aa_wo": (1024, 1024),
    "L1ab_wq": (8, 1024, 128), "L1ab_wk": (8, 1024, 128), "L1ab_wv": (8, 1024, 128),
    "L1ab_wo": (1024, 1024),
    "L1f_wqkv": (2, 1024, 1536), "L1f_wo": (2, 512, 1024),
    "L1f_w1a": (2, 1024, 2816), "L1f_w1g": (2, 1024, 2816), "L1f_w2": (2, 2816, 1024),
    "rot2": (128, 128), "ropecos": (128, T), "ropesin": (128, T),
    "xposcsA": (128, T), "xpossnA": (128, T), "xposcsB": (128, T), "xpossnB": (128, T),
}
WCAST["L0ab_r2w"] = (512, 128)



def mk_tiles(nc, stack):
    def sb(name, shape, dt=F32):
        return stack.enter_context(nc.sbuf_tensor(name, shape, dt))
    def ps(name, shape, dt=F32):
        return stack.enter_context(nc.psum_tensor(name, shape, dt))
    return sb, ps


class KB:
    def __init__(self, nc):
        self.nc = nc
        self.dram = {}
        self._sems = {}
        self._retcnt = {}
        self.pgscr = nc.alloc_sbuf_tensor("kb_pgscr", [128, 8], F32)

    def ret(self, e, ins):
        key = str(e.engine)
        sem = self.sem("_ret_" + key)
        self._retcnt[key] = self._retcnt.get(key, 0) + 1
        ins.then_inc(sem, 1)
        e.wait_ge(sem, self._retcnt[key])
        return ins

    def pg(self, e, n=2):
        scr = self.pgscr
        key = str(e.engine)
        if "DVE" in key:
            for _ in range(n):
                self.ret(e, e.tensor_copy(scr[:, 0:1], scr[:, 1:2]))
        elif "Activation" in key:
            for _ in range(n):
                self.ret(e, e.copy(scr[:, 2:3], scr[:, 3:4]))
        else:
            for _ in range(n):
                self.ret(e, e.memset(scr[:, 4:5], 0.0))

    def lagop(self, e):
        scr = self.pgscr
        key = str(e.engine)
        if "DVE" in key:
            return e.tensor_copy(scr[:, 0:1], scr[:, 1:2])
        if "Activation" in key:
            return e.copy(scr[:, 2:3], scr[:, 3:4])
        return e.memset(scr[:, 4:5], 0.0)

    def sem(self, name):
        if name not in self._sems:
            self._sems[name] = self.nc.ctx.enter_context(self.nc.semaphore(name))
        return self._sems[name]

    def dram_t(self, name, shape, dt=F32, kind="Internal"):
        ap = self.nc.dram_tensor(name, list(shape), dt, kind=kind).ap()
        self.dram[name] = ap
        return ap


class Sched:

    def __init__(self, kb, tag):
        self.kb = kb
        self.tag = tag
        self.ops = {e: [] for e in ENGINES}
        self.base = {}
        self.pe_sems = set()

    def s(self, name):
        if name not in self.base:
            if not hasattr(self.kb, "gcnt"):
                self.kb.gcnt = {}
            phys = f"p{len(self.base):02d}_{id(self) % 7}" if False else f"p{len(self.base):02d}"
            self.base[name] = (phys, self.kb.gcnt.get(phys, 0))
        return self.base[name][0]

    def cur(self, sem):
        self.s(sem)
        phys, b = self.base[sem]
        return self.kb.gcnt.get(phys, 0) - b

    def add(self, eng, fn, waits=(), inc=None, inc_by=1, dma=False):
        if eng == "tensor" and inc is not None:
            self.pe_sems.add(inc)
        if (not dma and eng in ("vector", "scalar", "gpsimd")
                and any(sm in self.pe_sems for sm, v in waits if v > 0)):
            inner = fn
            def fn(e, inner=inner):
                self.kb.pg(e)
                return inner(e)
        if LAG_OPS and not dma and inc is not None and eng in ("vector", "scalar", "gpsimd"):
            inner2 = fn
            def fn(e, inner2=inner2):
                ins = inner2(e)
                self.kb.ret(e, ins)
                return self.kb.lagop(e)
        w = []
        for sm, v in waits:
            if v > 0:
                self.s(sm)
                phys, b = self.base[sm]
                w.append((phys, b + v))
        w = tuple(w)
        tgt = None
        if inc is not None:
            self.s(inc)
            phys, b = self.base[inc]
            self.kb.gcnt[phys] = self.kb.gcnt.get(phys, 0) + inc_by
            tgt = (phys, inc_by)
        self.ops[eng].append((fn, w, tgt))

    def dma(self, eng, sem, dst, src, waits=(), **kw):
        def fn(e):
            return e.dma_start(dst, src, **kw)
        self.add(eng, fn, waits, inc=sem, inc_by=16, dma=True)

    def emit(self):
        kb = self.kb
        active = [e for e in ENGINES if self.ops[e]]
        with kb.nc.Block() as blk:
            for engname in active:
                ops = self.ops[engname]

                def body(e, ops=ops):
                    for fn, waits, tgt in ops:
                        for sname, v in waits:
                            e.wait_ge(kb.sem(sname), v)
                        ins = fn(e)
                        if tgt is not None:
                            sname, by = tgt
                            ins.then_inc(kb.sem(sname), by)
                getattr(blk, engname)(body)
        if CLEAR_SEMS:
            used = sorted({phys for (phys, b) in self.base.values()})
            if used:
                with kb.nc.Block() as blk2:
                    @blk2.gpsimd
                    def _(e):
                        for p in used:
                            e.sem_clear(kb.sem(p))
                for p in used:
                    kb.gcnt[p] = 0

    def fence(self, eng, sem):
        tgt = self.cur(sem)
        self.add(eng, lambda e: e.nop(), waits=[(sem, tgt)])

    def sw(self, sem, slot):
        nm = f"{sem}{slot}"
        return (nm, self.cur(nm))

    def mm(self, fn, waits=(), inc=None):
        self.add("tensor", fn, waits, inc)

    def dve(self, fn, waits=(), inc=None):
        self.add("vector", fn, waits, inc)

    def act(self, fn, waits=(), inc=None):
        self.add("scalar", fn, waits, inc)

    def pool(self, fn, waits=(), inc=None):
        self.add("gpsimd", fn, waits, inc)



def emit_init(kb, g):
    nc = kb.nc
    sc = Sched(kb, "ini")
    sc.pool(lambda e: e.memset(kb.pgscr[:], 0.0))
    def mk_ident(e, t="ident"):
        kb.ret(e, e.memset(g[t][:], 1.0))
        return e.affine_select(g[t][:], g[t][:], [[1, 128]],
                               ALU.is_equal, 0.0, channel_multiplier=-1)
    sc.pool(mk_ident)
    sc.pool(lambda e: mk_ident(e, "identf"))
    sc.pool(lambda e: e.memset(g["ones_r"][:], 1.0))
    sc.pool(lambda e: e.memset(g["ones_rb"][:], 1.0))
    sc.pool(lambda e: e.memset(g["ones_c"][:], 1.0))
    sc.pool(lambda e: e.memset(g["pihalf"][:], PI / 2))
    sc.pool(lambda e: e.memset(g["onesT"][:], 1.0))
    sc.pool(lambda e: e.memset(g["eps_ln"][:], 1e-5))
    sc.pool(lambda e: e.memset(g["eps_rms"][:], float(2.0 ** -23)))
    sc.pool(lambda e: e.memset(g["zero_c"][:], 0.0))
    sc.emit()


def alloc_globals(nc):
    g = {}
    g["ident"] = nc.alloc_sbuf_tensor("g_ident", [128, 128], BF16)
    g["identf"] = nc.alloc_sbuf_tensor("g_identf", [128, 128], F32)
    g["ones_r"] = nc.alloc_sbuf_tensor("g_ones_r", [1, 128], F32)
    g["ones_rb"] = nc.alloc_sbuf_tensor("g_ones_rb", [1, 128], BF16)
    g["ones_c"] = nc.alloc_sbuf_tensor("g_ones_c", [128, 1], F32)
    g["pihalf"] = nc.alloc_sbuf_tensor("g_pihalf", [128, 1], F32)
    g["onesT"] = nc.alloc_sbuf_tensor("g_onesT", [128, 2048], F32)
    g["eps_ln"] = nc.alloc_sbuf_tensor("g_eps_ln", [128, 1], F32)
    g["eps_rms"] = nc.alloc_sbuf_tensor("g_eps_rms", [128, 1], F32)
    g["zero_c"] = nc.alloc_sbuf_tensor("g_zero_c", [128, 1], F32)
    return g



def declare_bf_weights(kb, names):
    for nm in names:
        shp = WCAST[nm]
        flat = (int(np.prod(shp[:-1])), shp[-1])
        kb.dram_t("bf_" + nm, flat, BF16)


def emit_wcast(kb, names):
    nc = kb.nc
    jobs = []
    for nm in names:
        shp = WCAST[nm]
        rows = int(np.prod(shp[:-1]))
        cols = shp[-1]
        assert rows % 128 == 0
        for r in range(rows // 128):
            c0 = 0
            while c0 < cols:
                cw = min(4096, cols - c0)
                jobs.append((nm, r, cw, c0))
                c0 += cw
    with (
        nc.sbuf_tensor("wc_f", [128, 2 * 4096], F32) as tf,
        nc.sbuf_tensor("wc_b", [128, 2 * 4096], BF16) as tb,
    ):
        sc = Sched(kb, "wc")
        for i, (nm, r, cw, c0) in enumerate(jobs):
            b = (i % 2) * 4096
            sl = i % 2
            src = kb.dram[nm].rearrange("(n p) m -> n p m", p=128)[r, :, c0:c0 + cw]
            dst = kb.dram["bf_" + nm].rearrange("(n p) m -> n p m", p=128)[r, :, c0:c0 + cw]
            sc.dma("sync", f"ld{sl}", tf[:, b:b + cw], src,
                   waits=[("cp", i - 1)])
            stw = sc.sw("st", sl)
            sc.dve(lambda e, tb=tb, tf=tf, b=b, cw=cw: e.tensor_copy(tb[:, b:b + cw], tf[:, b:b + cw]),
                   waits=[sc.sw("ld", sl), stw], inc="cp")
            sc.dma("gpsimd", f"st{sl}", dst, tb[:, b:b + cw], waits=[("cp", i + 1)])
        sc.fence("gpsimd", "st0")
        sc.fence("gpsimd", "st1")
        sc.emit()
    return len(jobs)



def emit_norm(kb, g, tag, src_name, dst_fm=None, dst_tm=None, kind="ln",
              gain_row=None):
    nc = kb.nc
    src = kb.dram[src_name]
    eps = g["eps_ln"] if kind == "ln" else g["eps_rms"]
    with (
        nc.sbuf_tensor(f"{tag}_x", [128, 2 * C], F32) as xt,
        nc.sbuf_tensor(f"{tag}_c", [128, C], F32) as cent,
        nc.sbuf_tensor(f"{tag}_b", [128, 2 * C], BF16) as xbf,
        nc.sbuf_tensor(f"{tag}_q", [128, C], F32) as sq,
        nc.sbuf_tensor(f"{tag}_s", [128, 8], F32) as st,
        nc.sbuf_tensor(f"{tag}_gr", [1, C], F32) as grow,
        nc.sbuf_tensor(f"{tag}_gb", [128, C], F32) as gaint,
        nc.sbuf_tensor(f"{tag}_fs", [128, 2 * 4096], BF16) as fms,
        nc.sbuf_tensor(f"{tag}_ts", [128, 2 * C], F32) as tms,
        nc.psum_tensor(f"{tag}_pt", [128, 2048], BF16) as pst,
        nc.psum_tensor(f"{tag}_pg", [128, C], F32) as psg,
    ):
        do_fm = dst_fm is not None
        do_tm = dst_tm is not None

        if gain_row is not None:
            sc0 = Sched(kb, tag + "0")
            sc0.dma("sync", "ld", grow[:], kb.dram[gain_row])
            def gbc(e):
                e.matmul(psg[:, 0:512], g["ones_r"][:], grow[0:1, 0:512],
                         start=True, stop=True)
                return e.matmul(psg[:, 512:1024], g["ones_r"][:], grow[0:1, 512:1024],
                                start=True, stop=True)
            sc0.mm(gbc, waits=[("ld", 16)], inc="bc")
            sc0.act(lambda e: e.copy(gaint[:], psg[:]), waits=[("bc", 1)])
            sc0.emit()

        sc = Sched(kb, tag)
        for i in range(NT):
            b = (i % 2) * C
            sl = i % 2
            xv = xt[:, b:b + C]
            sc.dma("sync", f"ld{sl}", xv, src[i * 128:(i + 1) * 128, :],
                   waits=[("dv", i - 1)])
            def stats(e, xv=xv, i=i, kind=kind):
                R = kb.ret
                if kind == "ln":
                    R(e, e.reduce_sum(st[:, 0:1], xv, AX.X))
                    R(e, e.tensor_scalar(st[:, 1:2], st[:, 0:1], 1.0 / C, None, ALU.mult))
                    e.tensor_scalar(cent[:], xv, st[:, 1:2], None, ALU.subtract)
                    e.tensor_mul(sq[:], cent[:], cent[:])
                    R(e, e.reduce_sum(st[:, 2:3], sq[:], AX.X))
                    return e.tensor_scalar(st[:, 3:4], st[:, 2:3], 1.0 / C, None, ALU.mult)
                else:
                    e.tensor_mul(sq[:], xv, xv)
                    R(e, e.reduce_sum(st[:, 2:3], sq[:], AX.X))
                    return e.tensor_scalar(st[:, 3:4], st[:, 2:3], 1.0 / C, None, ALU.mult)
            sc.dve(stats, waits=[sc.sw("ld", sl)], inc="va")
            def a_rstd(e):
                kb.ret(e, e.activation(st[:, 4:5], st[:, 3:4], AF.Ln,
                                       bias=eps[:], scale=1.0))
                return e.activation(st[:, 5:6], st[:, 4:5], AF.Exp, scale=-0.5)
            sc.act(a_rstd, waits=[("va", i + 1)], inc="sq")

            def norm(e, i=i, b=b, xv=xv):
                srcv = cent[:] if kind == "ln" else xv
                if gain_row is not None:
                    e.tensor_scalar(sq[:], srcv, st[:, 5:6], None, ALU.mult)
                    ins = e.tensor_tensor(xbf[:, b:b + C], sq[:], gaint[:], ALU.mult)
                    if do_tm:
                        ins = e.tensor_tensor(tms[:, b:b + C], sq[:], gaint[:], ALU.mult)
                else:
                    ins = e.tensor_scalar(xbf[:, b:b + C], srcv, st[:, 5:6], None, ALU.mult)
                    if do_tm:
                        ins = e.tensor_scalar(tms[:, b:b + C], srcv, st[:, 5:6], None, ALU.mult)
                return ins
            nwaits = [("sq", i + 1)]
            if do_tm:
                nwaits.append(sc.sw("tm", sl))
            if do_fm:
                nwaits.append(("pe", i - 1))
            sc.dve(norm, waits=nwaits, inc="dv")
            if do_tm:
                sc.dma("gpsimd", f"tm{sl}", kb.dram[dst_tm][i * 128:(i + 1) * 128, :],
                       tms[:, b:b + C], waits=[("dv", i + 1)])
            if do_fm:
                gi = i // 4
                ti = i % 4
                pb = (i % 2) * 1024
                def trs(e, b=b, pb=pb):
                    for j in range(8):
                        mm = e.transpose(pst[:, pb + j * 128: pb + j * 128 + 128],
                                         xbf[:, b + j * 128: b + j * 128 + 128],
                                         g["ident"][:])
                    return mm
                sc.mm(trs, waits=[("dv", i + 1), ("ev", i - 1)], inc="pe")
                half = (gi % 2) * 4096
                def evac(e, half=half, ti=ti, pb=pb):
                    dst = fms[:, half:half + 4096].rearrange(
                        "p (j f) -> p j f", f=512)[:, :, ti * 128:ti * 128 + 128]
                    return e.tensor_copy(
                        dst, pst[:, pb:pb + 1024].rearrange("p (j f) -> p j f", f=128))
                sc.dve(evac, waits=[("pe", i + 1), sc.sw("fm", gi % 2)], inc="ev")
                if ti == 3:
                    for j in range(8):
                        sc.dma("gpsimd", f"fm{gi % 2}",
                               kb.dram[dst_fm][j * 128:(j + 1) * 128,
                                               PAD + gi * 512: PAD + (gi + 1) * 512],
                               fms[:, half + j * 512: half + j * 512 + 512],
                               waits=[("ev", i + 1)])
        if do_fm:
            sc.fence("gpsimd", "fm0")
            sc.fence("gpsimd", "fm1")
        if do_tm:
            sc.fence("gpsimd", "tm0")
            sc.fence("gpsimd", "tm1")
        sc.emit()



def emit_zero_pads(kb, fm_names):
    nc = kb.nc
    with nc.sbuf_tensor("zp", [128, PAD], BF16) as zp:
        sc = Sched(kb, "zp")
        sc.pool(lambda e: e.memset(zp[:], 0.0), inc="ms")
        for nm in fm_names:
            rows = kb.dram[nm].shape[0]
            for j in range(rows // 128):
                sc.dma("sync", "st", kb.dram[nm][j * 128:(j + 1) * 128, 0:PAD], zp[:],
                       waits=[("ms", 1)])
        sc.fence("sync", "st")
        sc.emit()



def emit_l0aa_conv(kb, g, src_fm, dst_fm):
    nc = kb.nc
    with (
        nc.sbuf_tensor("cv_x", [128, 2 * 1027], BF16) as xt,
        nc.sbuf_tensor("cv_a", [128, 1024], F32) as acc,
        nc.sbuf_tensor("cv_o", [128, 2 * 1024], BF16) as oc,
        nc.sbuf_tensor("cv_w", [128, 32], F32) as wt,
        nc.sbuf_tensor("cv_b", [128, 8], F32) as bt,
    ):
        sc = Sched(kb, "cv")
        sc.dma("sync", "ld", wt[:], kb.dram["L0aa_convw"])
        sc.dma("sync", "ld", bt[:], kb.dram["L0aa_convb"])
        u = 0
        for j in range(8):
            for q in range(4):
                b = (u % 2) * 1027
                ob = (u % 2) * 1024
                sc.dma("sync", f"ldt{u % 2}",
                       xt[:, b:b + 1027],
                       kb.dram[src_fm][j * 128:(j + 1) * 128,
                                       PAD + q * 1024 - 3: PAD + q * 1024 + 1024],
                       waits=[("dv", u - 1)])

                def comp(e, b=b, ob=ob, j=j, u=u):
                    e.tensor_scalar(acc[:], xt[:, b + 3:b + 1027],
                                    wt[:, j * 4 + 3:j * 4 + 4], bt[:, j:j + 1],
                                    ALU.mult, ALU.add)
                    for tap in (2, 1, 0):
                        e.scalar_tensor_tensor(acc[:], xt[:, b + tap:b + tap + 1024],
                                               wt[:, j * 4 + tap:j * 4 + tap + 1],
                                               acc[:], ALU.mult, ALU.add)
                    return e.tensor_copy(oc[:, ob:ob + 1024], acc[:])
                sc.dve(comp, waits=[("ld", 32), sc.sw("ldt", u % 2),
                                    sc.sw("st", u % 2)], inc="dv")
                sc.dma("gpsimd", f"st{u % 2}",
                       kb.dram[dst_fm][j * 128:(j + 1) * 128,
                                       PAD + q * 1024: PAD + (q + 1) * 1024],
                       oc[:, ob:ob + 1024], waits=[("dv", u + 1)])
                u += 1
        sc.fence("gpsimd", "st0")
        sc.fence("gpsimd", "st1")
        sc.emit()



def emit_l0aa_retnet(kb, g, xc_fm, attout):
    nc = kb.nc
    with ExitStack() as _st:
        sb, ps = mk_tiles(nc, _st)
        xcs = sb("rn_xc", [128, 8 * 4096], BF16)
        whd = sb("rn_w", [128, 3072], BF16)
        t_b = sb("rn_tb", [128, 2 * 2048])
        trow = sb("rn_tr", [1, T])
        qk = sb("rn_qk", [128, 2048])
        vv = sb("rn_v", [128, 2048])
        gg = sb("rn_g", [128, 2048])
        sqb = sb("rn_sq", [128, 2048])
        cosb = sb("rn_cos", [128, 2048])
        tA = sb("rn_tA", [128, 2048])
        tB = sb("rn_tB", [128, 2048])
        decb = sb("rn_de", [128, 2048])
        outst = sb("rn_os", [128, 2 * 2048], BF16)
        freqs = sb("rn_fq", [128, 16])
        gbs = sb("rn_gb", [128, 16])
        gmrow = sb("rn_gm", [1, 16])
        lngr = sb("rn_ln", [1, 16])
        rowA = sb("rn_rA", [1, 2048])
        rowB = sb("rn_rB", [1, 2048])
        qcar = sb("rn_qc", [128, 1])
        ccar = sb("rn_cc", [128, 1])
        psA = ps("rn_pA", [128, 2048])
        psB = ps("rn_pB", [128, 2048])

        s0 = Sched(kb, "rn0")
        for c in range(8):
            s0.dma("sync", "ld", xcs[:, c * 4096:(c + 1) * 4096],
                   kb.dram[xc_fm][c * 128:(c + 1) * 128, PAD:PAD + T])
        s0.dma("sync", "ld", freqs[:], kb.dram["L0aa_freqs"])
        s0.dma("sync", "ld", gbs[:], kb.dram["L0aa_gb"])
        s0.dma("sync", "ld", gmrow[:], kb.dram["L0aa_gamma"])
        s0.dma("sync", "ld", trow[:], kb.dram["trow"])
        nld = s0.cur("ld")
        s0.act(lambda e: e.activation(lngr[:], gmrow[:], AF.Ln), waits=[("ld", nld)])
        for hf in range(2):
            def tbmm(e, hf=hf):
                for tc in range(4):
                    mm = e.matmul(psB[:, tc * 512:(tc + 1) * 512], g["ones_r"][:],
                                  trow[0:1, hf * 2048 + tc * 512: hf * 2048 + (tc + 1) * 512],
                                  start=True, stop=True)
                return mm
            s0.mm(tbmm, waits=[("ld", nld), ("tb", hf)], inc="pe")
            s0.act(lambda e, hf=hf: e.copy(t_b[:, hf * 2048:(hf + 1) * 2048], psB[:]),
                   waits=[("pe", hf + 1)], inc="tb")
        s0.emit()

        sc = Sched(kb, "rn")
        HD = 128

        for h in range(16):
            for wi, wnm in enumerate(["bf_L0aa_qkw", "bf_L0aa_vw", "bf_L0aa_gw"]):
                src = kb.dram[wnm].rearrange("(c p) f -> p c f", p=128)[:, :, h * 128:(h + 1) * 128]
                sc.dma("sync", "wld", whd[:, wi * 1024:(wi + 1) * 1024].rearrange(
                    "p (c f) -> p c f", c=8), src,
                    waits=[("peA", 3 * (2 * h - 1) + 3 if h > 0 else 0)])
            for hf in range(2):
                u = 2 * h + hf
                tcol = slice(hf * 2048, (hf + 1) * 2048)

                def p_qk(e, u=u, hf=hf):
                    for tc in range(4):
                        for cch in range(8):
                            mm = e.matmul(
                                psA[:, tc * 512:(tc + 1) * 512],
                                whd[:, 0 + cch * 128: 0 + cch * 128 + 128],
                                xcs[:, cch * 4096 + hf * 2048 + tc * 512:
                                        cch * 4096 + hf * 2048 + (tc + 1) * 512],
                                start=(cch == 0), stop=(cch == 7))
                    return mm
                sc.mm(p_qk, waits=[("acA", 3 * u), ("wld", 48 * (h + 1))], inc="peA")
                sc.act(lambda e: e.copy(qk[:], psA[:]), waits=[("peA", 3 * u + 1), ("dvG", u)], inc="acA")
                sc.dve(lambda e: e.tensor_mul(sqb[:], qk[:], qk[:]),
                       waits=[("acA", 3 * u + 1)], inc="dvA")
                def p_rs(e):
                    for tc in range(4):
                        mm = e.matmul(psB[0:1, tc * 512:(tc + 1) * 512], g["ones_c"][:],
                                      sqb[:, tc * 512:(tc + 1) * 512], start=True, stop=True)
                    return mm
                sc.mm(p_rs, waits=[("dvA", u + 1), ("dvG", u)], inc="peB")
                def a_rnrm(e):
                    kb.ret(e, e.activation(rowA[:], psB[0:1, :], AF.Ln))
                    return e.activation(rowA[:], rowA[:], AF.Exp, scale=-0.5)
                sc.act(a_rnrm, waits=[("peB", 4 * u + 1), ("dvG", u)], inc="acB")
                def d_nrm(e):
                    return e.tensor_scalar(rowA[:], rowA[:], 1e12, None, ALU.min)
                sc.dve(d_nrm, waits=[("acB", u + 1)], inc="dvB")
                def p_bc1(e):
                    for tc in range(4):
                        mm = e.matmul(psB[:, tc * 512:(tc + 1) * 512], g["ones_r"][:],
                                      rowA[0:1, tc * 512:(tc + 1) * 512], start=True, stop=True)
                    return mm
                sc.mm(p_bc1, waits=[("dvB", u + 1), ("acB", u + 1)], inc="peB")
                import os as _os
                if _os.environ.get("RETNET_DUMP", "") == "qkraw":
                    sc.dve(lambda e: e.tensor_copy(tA[:], psB[:]),
                           waits=[("peB", 4 * u + 2)], inc="dvC")
                else:
                    sc.dve(lambda e: e.tensor_tensor(qk[:], qk[:], psB[:], ALU.mult),
                           waits=[("peB", 4 * u + 2)], inc="dvC")
                def p_v(e, hf=hf):
                    for tc in range(4):
                        for cch in range(8):
                            mm = e.matmul(
                                psA[:, tc * 512:(tc + 1) * 512],
                                whd[:, 1024 + cch * 128: 1024 + cch * 128 + 128],
                                xcs[:, cch * 4096 + hf * 2048 + tc * 512:
                                        cch * 4096 + hf * 2048 + (tc + 1) * 512],
                                start=(cch == 0), stop=(cch == 7))
                    return mm
                sc.mm(p_v, waits=[("acA", 3 * u + 1)], inc="peA")
                sc.act(lambda e: e.copy(vv[:], psA[:]), waits=[("peA", 3 * u + 2), ("dvG", u)], inc="acA")
                def p_g(e, hf=hf):
                    for tc in range(4):
                        for cch in range(8):
                            mm = e.matmul(
                                psA[:, tc * 512:(tc + 1) * 512],
                                whd[:, 2048 + cch * 128: 2048 + cch * 128 + 128],
                                xcs[:, cch * 4096 + hf * 2048 + tc * 512:
                                        cch * 4096 + hf * 2048 + (tc + 1) * 512],
                                start=(cch == 0), stop=(cch == 7))
                    return mm
                sc.mm(p_g, waits=[("acA", 3 * u + 2)], inc="peA")
                sc.act(lambda e, h=h: e.activation(gg[:], psA[:], AF.Sigmoid,
                                                   bias=gbs[:, h:h + 1]),
                       waits=[("peA", 3 * u + 3), ("dvG", u)], inc="acA")
                def d_ang(e, h=h, hf=hf):
                    e.tensor_scalar(tA[:], t_b[:, hf * 2048:(hf + 1) * 2048],
                                    freqs[:, h:h + 1], None, ALU.mult)
                    e.tensor_scalar(tB[:], tA[:], INV_2PI, MAGIC, ALU.mult, ALU.add)
                    e.tensor_scalar(tB[:], tB[:], -MAGIC, None, ALU.add)
                    e.scalar_tensor_tensor(tA[:], tB[:], -CW1, tA[:], ALU.mult, ALU.add)
                    e.scalar_tensor_tensor(tA[:], tB[:], -CW2, tA[:], ALU.mult, ALU.add)
                    e.scalar_tensor_tensor(tA[:], tB[:], -CW3, tA[:], ALU.mult, ALU.add)
                    e.tensor_scalar(tB[:], tA[:], -1.0, None, ALU.mult)
                    return e.tensor_max(tB[:], tB[:], tA[:])
                sc.dve(d_ang, waits=[("acC", u)], inc="dvE")
                sc.act(lambda e: e.activation(cosb[:], tB[:], AF.Sin,
                                              bias=g["pihalf"][:], scale=-1.0),
                       waits=[("dvE", u + 1), ("dvG", u)], inc="acC")
                def d_t1(e, h=h, hf=hf):
                    return e.tensor_scalar(rowB[:], trow[0:1, hf * 2048:(hf + 1) * 2048],
                                           lngr[0:1, h:h + 1], None, ALU.mult)
                sc.dve(d_t1, waits=[("acD", u), ("peB", 4 * u - 1)], inc="dvF")
                sc.act(lambda e: e.activation(rowB[:], rowB[:], AF.Exp),
                       waits=[("dvF", u + 1), ("dvD", u)], inc="acD")
                def d_rd(e):
                    kb.ret(e, e.tensor_scalar(rowA[:], rowB[:], 1e-6, None, ALU.add))
                    return e.reciprocal(rowA[:], rowA[:])
                sc.dve(d_rd, waits=[("acD", u + 1), ("peB", 4 * u + 2)], inc="dvD")
                def p_bc2(e):
                    for tc in range(4):
                        mm = e.matmul(psB[:, tc * 512:(tc + 1) * 512], g["ones_r"][:],
                                      rowB[0:1, tc * 512:(tc + 1) * 512], start=True, stop=True)
                    return mm
                sc.mm(p_bc2, waits=[("acD", u + 1), ("dvC", u + 1)], inc="peB")
                sc.act(lambda e: e.copy(decb[:], psB[:]), waits=[("peB", 4 * u + 3), ("dvG", u)], inc="acE")
                def p_bc3(e):
                    for tc in range(4):
                        mm = e.matmul(psB[:, tc * 512:(tc + 1) * 512], g["ones_r"][:],
                                      rowA[0:1, tc * 512:(tc + 1) * 512], start=True, stop=True)
                    return mm
                sc.mm(p_bc3, waits=[("dvD", u + 1), ("acE", u + 1)], inc="peB")
                def d_ksh(e, hf=hf):
                    e.tensor_copy(tB[:, 1:2048], qk[:, 0:2047])
                    if hf == 0:
                        kb.ret(e, e.tensor_copy(tB[:, 0:1], g["zero_c"][:]))
                    else:
                        kb.ret(e, e.tensor_copy(tB[:, 0:1], qcar[:]))
                    return kb.ret(e, e.tensor_copy(qcar[:], qk[:, 2047:2048]))
                sc.dve(d_ksh, waits=[("acC", u + 1)], inc=None)
                def d_state(e):
                    e.tensor_add(tA[:], cosb[:], tB[:])
                    e.tensor_mul(tA[:], vv[:], tA[:])
                    e.tensor_mul(tA[:], tA[:], gg[:])
                    return e.tensor_mul(tA[:], tA[:], decb[:])
                sc.dve(d_state, waits=[("acC", u + 1), ("acA", 3 * u + 3), ("acE", u + 1)],
                       inc=None)
                def d_scan(e, hf=hf):
                    init = g["zero_c"][:] if hf == 0 else ccar[:]
                    kb.ret(e, e.tensor_tensor_scan(sqb[:], g["onesT"][:], tA[:], init,
                                                   ALU.mult, ALU.add))
                    return kb.ret(e, e.tensor_copy(ccar[:], sqb[:, 2047:2048]))
                sc.dve(d_scan, waits=[], inc=None)
                ob = (u % 2) * 2048
                import os as _os
                _dump = _os.environ.get("RETNET_DUMP", "")
                def d_out(e, ob=ob, _dump=_dump):
                    e.tensor_tensor(tB[:], sqb[:], psB[:], ALU.mult)
                    if _dump:
                        srcmap = {"q": qk, "qkraw": qk, "rnrmb": tA, "v": vv, "g": gg, "cos": cosb,
                                  "sd": tA, "cs": sqb, "dec": decb}
                        return e.tensor_copy(outst[:, ob:ob + 2048], srcmap[_dump][:])
                    return e.tensor_tensor(outst[:, ob:ob + 2048], tB[:], qk[:], ALU.mult)
                sc.dve(d_out, waits=[("peB", 4 * u + 4), sc.sw("st", u % 2)], inc="dvG")
                sc.dma("gpsimd", f"st{u % 2}",
                       kb.dram[attout][h * 128:(h + 1) * 128, hf * 2048:(hf + 1) * 2048],
                       outst[:, ob:ob + 2048], waits=[("dvG", u + 1)])
        sc.fence("gpsimd", "st0")
        sc.fence("gpsimd", "st1")
        sc.emit()



def emit_l0aa_proj(kb, g, attout, src, dst):
    nc = kb.nc
    with (
        nc.sbuf_tensor("pj_w", [128, 16 * 1024], BF16) as ows,
        nc.sbuf_tensor("pj_a", [128, 2 * 2048], BF16) as att,
        nc.sbuf_tensor("pj_y", [128, 2 * 1024], F32) as ysb,
        nc.sbuf_tensor("pj_x", [128, 2 * 1024], F32) as xsb,
        nc.sbuf_tensor("pj_o", [128, 2 * 1024], F32) as osb,
        nc.sbuf_tensor("pj_q", [128, 1024], F32) as sq,
        nc.sbuf_tensor("pj_s", [128, 8], F32) as st,
        nc.sbuf_tensor("pj_gr", [1, 2 * C], F32) as grow,
        nc.sbuf_tensor("pj_gb", [128, 2 * C], F32) as gbt,
        nc.psum_tensor("pj_p", [128, 2 * 1024], F32) as psY,
        nc.psum_tensor("pj_pg", [128, 2 * 1024], F32) as psG,
    ):
        s0 = Sched(kb, "pj0")
        s0.dma("sync", "ld", ows[:].rearrange("p (c f) -> p c f", c=16),
               kb.dram["bf_L0aa_ow"].rearrange("(c p) f -> p c f", p=128))
        s0.dma("sync", "ld", grow[0:1, 0:C], kb.dram["L0aa_lng"])
        s0.dma("sync", "ld", grow[0:1, C:2 * C], kb.dram["L0aa_lnb"])
        def bc(e):
            e.matmul(psG[:, 0:512], g["ones_r"][:], grow[0:1, 0:512], start=True, stop=True)
            e.matmul(psG[:, 512:1024], g["ones_r"][:], grow[0:1, 512:1024], start=True, stop=True)
            e.matmul(psG[:, 1024:1536], g["ones_r"][:], grow[0:1, 1024:1536], start=True, stop=True)
            return e.matmul(psG[:, 1536:2048], g["ones_r"][:], grow[0:1, 1536:2048], start=True, stop=True)
        s0.mm(bc, waits=[("ld", 48)], inc="pe")
        s0.act(lambda e: e.copy(gbt[:], psG[:]), waits=[("pe", 1)])
        s0.emit()

        sc = Sched(kb, "pj")
        for i in range(NT):
            ab = (i % 2) * 2048
            yb = (i % 2) * 1024
            sl = i % 2
            sc.dma("sync", f"ld{sl}", att[:, ab:ab + 2048].rearrange("p (c f) -> p c f", c=16),
                   kb.dram[attout][:, i * 128:(i + 1) * 128].rearrange("(c p) f -> p c f", p=128),
                   waits=[("pe", i - 1)])
            sc.dma("sync", f"ldx{sl}", xsb[:, yb:yb + 1024], kb.dram[src][i * 128:(i + 1) * 128, :],
                   waits=[("dv", i - 1)])

            def p_mm(e, ab=ab, yb=yb):
                for half in range(2):
                    for cch in range(16):
                        mm = e.matmul(psY[:, yb + half * 512: yb + half * 512 + 512],
                                      att[:, ab + cch * 128: ab + cch * 128 + 128],
                                      ows[:, cch * 1024 + half * 512: cch * 1024 + half * 512 + 512],
                                      start=(cch == 0), stop=(cch == 15))
                return mm
            sc.mm(p_mm, waits=[sc.sw("ld", sl), ("ac", 2 * i - 3)], inc="pe")
            sc.act(lambda e, yb=yb: e.copy(ysb[:, yb:yb + 1024], psY[:, yb:yb + 1024]),
                   waits=[("pe", i + 1)], inc="ac")

            def d_ln(e, yb=yb, i=i):
                R = kb.ret
                yv = ysb[:, yb:yb + 1024]
                R(e, e.reduce_sum(st[:, 0:1], yv, AX.X))
                R(e, e.tensor_scalar(st[:, 1:2], st[:, 0:1], 1.0 / C, None, ALU.mult))
                e.tensor_scalar(osb[:, yb:yb + 1024], yv, st[:, 1:2], None, ALU.subtract)
                e.tensor_mul(sq[:], osb[:, yb:yb + 1024], osb[:, yb:yb + 1024])
                R(e, e.reduce_sum(st[:, 2:3], sq[:], AX.X))
                return e.tensor_scalar(st[:, 3:4], st[:, 2:3], 1.0 / C, None, ALU.mult)
            sc.dve(d_ln, waits=[("ac", 2 * i + 1)], inc="dvv")
            def a_rstd2(e):
                kb.ret(e, e.activation(st[:, 4:5], st[:, 3:4], AF.Ln,
                                       bias=g["eps_ln"][:], scale=1.0))
                return e.activation(st[:, 5:6], st[:, 4:5], AF.Exp, scale=-0.5)
            sc.act(a_rstd2, waits=[("dvv", i + 1)], inc="ac")

            def d_fin(e, yb=yb, i=i):
                ov = osb[:, yb:yb + 1024]
                e.tensor_scalar(ov, ov, st[:, 5:6], None, ALU.mult)
                e.tensor_tensor(ov, ov, gbt[:, 0:1024], ALU.mult)
                e.tensor_tensor(ov, ov, gbt[:, 1024:2048], ALU.add)
                return e.tensor_tensor(ov, ov, xsb[:, yb:yb + 1024], ALU.add)
            sc.dve(d_fin, waits=[("ac", 2 * i + 2), sc.sw("ldx", sl),
                                 sc.sw("stx", sl)], inc="dv")
            sc.dma("gpsimd", f"stx{sl}", kb.dram[dst][i * 128:(i + 1) * 128, :],
                   osb[:, yb:yb + 1024], waits=[("dv", i + 1)])
        sc.fence("gpsimd", "stx0")
        sc.fence("gpsimd", "stx1")
        sc.emit()


def emit_L0_attn_a(kb, g, src, dst):
    emit_norm(kb, g, "aaln", src, dst_fm="xnbuf", kind="ln")
    emit_l0aa_conv(kb, g, "xnbuf", "fm2")
    emit_l0aa_retnet(kb, g, "fm2", "attout")
    emit_l0aa_proj(kb, g, "attout", src, dst)



SMALL_SHAPES = {
    "L0aa_convw": (128, 32), "L0aa_convb": (128, 8), "L0aa_gb": (128, 16),
    "L0aa_lng": (1, 1024), "L0aa_lnb": (1, 1024), "L0aa_freqs": (128, 16),
    "L0aa_gamma": (1, 16),
    "L0f_g1": (1, 1024), "L0f_g2": (1, 1024),
    "L0ab_r1b": (128, 4), "L0ab_lng": (128, 4), "L0ab_lnb": (128, 4),
    "L0ab_r2b": (1, 6), "L0ab_temp": (1, 1), "L0ab_rbias": (1, 6),
    "L0ab_b1": (128, 72), "L0ab_b2": (1, 6144),
    "L1ab_bq": (128, 8), "L1ab_bk": (128, 8), "L1ab_bv": (128, 8),
    "L1ab_bo": (1, 1024),
    "L1f_qs": (128, 2), "L1f_ks": (128, 2), "L1f_lng": (2, 1024),
    "L1f_lnb": (2, 1024), "L1f_b1a": (128, 44), "L1f_b1g": (128, 44),
    "L1f_b2": (2, 1024),
    "madd": (128, 128), "bandmask": (128, 640), "trow": (1, 4096),
    "cmaskT_0": (128, 512), "cmaskT_1": (128, 512), "cmaskT_2": (128, 512),
    "cmaskT_3": (128, 512),
    "lmaskT_0": (128, 512), "lmaskT_1": (128, 512), "lmaskT_2": (128, 512),
    "lmaskT_3": (128, 512), "lmaskT_4": (128, 512), "lmaskT_5": (128, 512),
    "lmaskT_6": (128, 512), "lmaskT_7": (128, 512),
    "ones2h": (128, 2), "sel2h": (2, 128),
}

SUB_WCAST = {
    "L0aa": ["L0aa_qkw", "L0aa_vw", "L0aa_gw", "L0aa_ow"],
    "L0f": ["L0f_wq", "L0f_wk", "L0f_wv", "L0f_wo", "L0f_gate", "L0f_w1",
            "L0f_w2", "L0f_w3", "ropecos", "ropesin", "rot2"],
    "L0ab": ["L0ab_r1w", "L0ab_r2w", "L0ab_W1", "L0ab_W2"],
    "L1aa": ["L1aa_kvd", "L1aa_kvu", "L1aa_wq", "L1aa_wo"],
    "L1ab": ["L1ab_wq", "L1ab_wk", "L1ab_wv", "L1ab_wo"],
    "L1f": ["L1f_wqkv", "L1f_wo", "L1f_w1a", "L1f_w1g", "L1f_w2",
            "xposcsA", "xpossnA", "xposcsB", "xpossnB", "rot2"],
}

FULL_ORDER = ["L0aa", "L0f", "L0ab", "L0f", "L1aa", "L1f", "L1ab", "L1f"]


def build_program(sublayers, dbg=()):
    nc = bass.Bass("TRN2", target_bir_lowering=False)
    kb = KB(nc)
    g = alloc_globals(nc)

    kb.dram_t("x", (T, C), F32, kind="ExternalInput")
    kb.dram_t("out", (T, C), F32, kind="ExternalOutput")

    def scr(name, shape, dt):
        kind = "ExternalOutput" if name in dbg else "Internal"
        kb.dram_t(name, shape, dt, kind=kind)

    scr("xres", (T, C), F32)
    scr("ybuf", (T, C), F32)
    scr("xnbuf", (C, PAD + T), BF16)
    scr("fm2", (C, PAD + T), BF16)
    scr("attout", (2048, T), BF16)
    scr("comb", (T, 8), F32)
    scr("qfm", (C, T), BF16)
    scr("kfm", (C, T), BF16)
    scr("vtm", (T, C), BF16)
    scr("yfm", (C, PAD + T), BF16)

    need_w = []
    for s in set(sublayers):
        need_w += SUB_WCAST[s]
    need_w = sorted(set(need_w))
    for nm in need_w:
        shp = WCAST[nm]
        kb.dram_t(nm, (int(np.prod(shp[:-1])), shp[-1]), F32, kind="ExternalInput")
    for nm, shp in SMALL_SHAPES.items():
        kb.dram_t(nm, shp, F32, kind="ExternalInput")

    declare_bf_weights(kb, need_w)
    emit_init(kb, g)
    emit_zero_pads(kb, ["xnbuf", "fm2", "yfm"])
    emit_wcast(kb, need_w)

    n = len(sublayers)
    for i, s in enumerate(sublayers):
        src = "x" if i == 0 else "xres"
        dst = "out" if i == n - 1 else "xres"
        if s == "L0aa":
            emit_L0_attn_a(kb, g, src, dst)
        elif s == "L0f":
            emit_L0_ffn(kb, g, src, dst, tag=f"f{i}")
        elif s == "L0ab":
            emit_L0_attn_b(kb, g, src, dst, tag=f"b{i}")
        elif s == "L1aa":
            emit_L1_attn_a(kb, g, src, dst)
        elif s == "L1ab":
            emit_L1_attn_b(kb, g, src, dst)
        elif s == "L1f":
            emit_L1_ffn(kb, g, src, dst, tag=f"lf{i}")
        else:
            raise ValueError(s)
    nc.finalize()
    return nc, kb


def kernel(**inputs):
    staged = stage_weights(inputs)
    staged.update(stage_consts())
    x_all = np.ascontiguousarray(inputs["L0_attn_a_x"], dtype=np.float32)
    nc, kb = build_program(FULL_ORDER)
    in_maps = []
    for b in range(8):
        m = dict(staged)
        m["x"] = np.ascontiguousarray(x_all[b])
        in_maps.append(m)
    res = run_bass_kernel_spmd(nc, in_maps, core_ids=list(range(8)))
    return np.stack([np.asarray(res.results[i]["out"], dtype=np.float32)
                     for i in range(8)])



def emit_proj_residual(kb, g, tag, fm_src, nrows, w_rhs, src, dst,
                       ln=None, bias_row=None, w_row0=0):
    nc = kb.nc
    nch = nrows // 128
    with ExitStack() as _st:
        sb, ps = mk_tiles(nc, _st)
        ows = sb(f"{tag}_w", [128, nch * 1024], BF16)
        att = sb(f"{tag}_a", [128, 2 * nch * 128], BF16)
        ysb = sb(f"{tag}_y", [128, 2 * 1024])
        xsb = sb(f"{tag}_x", [128, 2 * 1024])
        osb = sb(f"{tag}_o", [128, 2 * 1024])
        sq = sb(f"{tag}_q", [128, 1024])
        st = sb(f"{tag}_s", [128, 8])
        grow = sb(f"{tag}_gr", [1, 2 * C])
        gbt = sb(f"{tag}_gb", [128, 2 * C])
        psY = ps(f"{tag}_p", [128, 2 * 1024])
        psG = ps(f"{tag}_pg", [128, 1024])

        s0 = Sched(kb, tag + "0")
        s0.dma("sync", "ld", ows[:, 0:nch * 1024].rearrange("p (c f) -> p c f", c=nch),
               kb.dram[w_rhs][w_row0:w_row0 + nrows, :].rearrange(
                   "(c p) f -> p c f", p=128))
        if ln is not None:
            s0.dma("sync", "ld", grow[0:1, 0:C], kb.dram[ln[0]])
            s0.dma("sync", "ld", grow[0:1, C:2 * C], kb.dram[ln[1]])
        elif bias_row is not None:
            s0.dma("sync", "ld", grow[0:1, 0:C], kb.dram[bias_row])
        if ln is not None or bias_row is not None:
            wide = 2 * C if ln is not None else C
            for k0 in range(0, wide, 512):
                s0.mm(lambda e, k0=k0: e.matmul(psG[:, 0:512], g["ones_r"][:],
                                                grow[0:1, k0:k0 + 512], start=True, stop=True),
                      waits=[("ld", s0.cur("ld")), ("bce", s0.cur("bce"))], inc="bcm")
                s0.act(lambda e, k0=k0: e.copy(gbt[:, k0:k0 + 512], psG[:, 0:512]),
                       waits=[("bcm", s0.cur("bcm"))], inc="bce")
        s0.fence("sync", "ld")
        s0.emit()

        sc = Sched(kb, tag)
        for i in range(NT):
            ab = (i % 2) * nch * 128
            yb = (i % 2) * 1024
            sl = i % 2
            sc.dma("sync", f"ld{sl}", att[:, ab:ab + nch * 128].rearrange("p (c f) -> p c f", c=nch),
                   kb.dram[fm_src][0:nrows, i * 128:(i + 1) * 128].rearrange(
                       "(c p) f -> p c f", p=128),
                   waits=[("pe", i - 1)])
            sc.dma("sync", f"ldx{sl}", xsb[:, yb:yb + 1024], kb.dram[src][i * 128:(i + 1) * 128, :],
                   waits=[("dv", i - 1)])

            def p_mm(e, ab=ab, yb=yb):
                for half in range(2):
                    for cch in range(nch):
                        mm = e.matmul(psY[:, yb + half * 512: yb + half * 512 + 512],
                                      att[:, ab + cch * 128: ab + cch * 128 + 128],
                                      ows[:, cch * 1024 + half * 512: cch * 1024 + half * 512 + 512],
                                      start=(cch == 0), stop=(cch == nch - 1))
                return mm
            sc.mm(p_mm, waits=[sc.sw("ld", sl),
                               ("acv", max(0, i - 1))], inc="pe")
            sc.act(lambda e, yb=yb: e.copy(ysb[:, yb:yb + 1024], psY[:, yb:yb + 1024]),
                   waits=[("pe", i + 1)], inc="acv")

            if ln is not None:
                def d_ln(e, yb=yb):
                    R = kb.ret
                    yv = ysb[:, yb:yb + 1024]
                    R(e, e.reduce_sum(st[:, 0:1], yv, AX.X))
                    R(e, e.tensor_scalar(st[:, 1:2], st[:, 0:1], 1.0 / C, None, ALU.mult))
                    e.tensor_scalar(osb[:, yb:yb + 1024], yv, st[:, 1:2], None, ALU.subtract)
                    e.tensor_mul(sq[:], osb[:, yb:yb + 1024], osb[:, yb:yb + 1024])
                    R(e, e.reduce_sum(st[:, 2:3], sq[:], AX.X))
                    return e.tensor_scalar(st[:, 3:4], st[:, 2:3], 1.0 / C, None, ALU.mult)
                sc.dve(d_ln, waits=[("acv", i + 1)], inc="dvv")
                def a_rstd(e):
                    kb.ret(e, e.activation(st[:, 4:5], st[:, 3:4], AF.Ln,
                                           bias=g["eps_ln"][:], scale=1.0))
                    return e.activation(st[:, 5:6], st[:, 4:5], AF.Exp, scale=-0.5)
                sc.act(a_rstd, waits=[("dvv", sc.cur("dvv"))], inc="acs")

                def d_fin(e, yb=yb):
                    ov = osb[:, yb:yb + 1024]
                    e.tensor_scalar(ov, ov, st[:, 5:6], None, ALU.mult)
                    e.tensor_tensor(ov, ov, gbt[:, 0:1024], ALU.mult)
                    e.tensor_tensor(ov, ov, gbt[:, 1024:2048], ALU.add)
                    return e.tensor_tensor(ov, ov, xsb[:, yb:yb + 1024], ALU.add)
                sc.dve(d_fin, waits=[("acs", sc.cur("acs")), sc.sw("ldx", sl),
                                     sc.sw("stx", sl)], inc="dv")
            else:
                def d_fin(e, yb=yb):
                    ov = osb[:, yb:yb + 1024]
                    if bias_row is not None:
                        e.tensor_add(ov, ysb[:, yb:yb + 1024], gbt[:, 0:1024])
                        return e.tensor_tensor(ov, ov, xsb[:, yb:yb + 1024], ALU.add)
                    return e.tensor_tensor(ov, ysb[:, yb:yb + 1024],
                                           xsb[:, yb:yb + 1024], ALU.add)
                sc.dve(d_fin, waits=[("acv", i + 1), sc.sw("ldx", sl),
                                     sc.sw("stx", sl)], inc="dv")
            sc.dma("gpsimd", f"stx{sl}", kb.dram[dst][i * 128:(i + 1) * 128, :],
                   osb[:, yb:yb + 1024], waits=[("dv", i + 1)])
        sc.fence("gpsimd", "stx0")
        sc.fence("gpsimd", "stx1")
        sc.emit()



def emit_flash(kb, g, tag, qfm, kfm, vtm, attout, nheads=16, vcols=1024,
               causal=True, scale=0.125, kv_of_head=None, local_tiles=None):
    nc = kb.nc
    if kv_of_head is None:
        kv_of_head = lambda h: h
    npair = nheads // 2
    kv_rows = max(kv_of_head(h) for h in range(nheads)) * 64 + 64
    half_pairs = npair // 2 if kv_rows * 2 // 128 * T * 2 > 40 * 1024 * 128 else npair
    with ExitStack() as _st:
        sb, ps = mk_tiles(nc, _st)
        max_kch = 4
        kres = sb(f"{tag}_k", [128, max_kch * T], BF16)
        vres = sb(f"{tag}_v", [128, NT * 512], BF16)
        qt_sb = sb(f"{tag}_qt", [128, 2 * 128], BF16)
        S_sb = sb(f"{tag}_S", [128, T])
        P_sb = sb(f"{tag}_P", [128, T], BF16)
        Pt_sb = sb(f"{tag}_Pt", [128, 1024], BF16)
        madd = sb(f"{tag}_md", [128, 128])
        outst = sb(f"{tag}_os", [128, T], BF16)
        rows = sb(f"{tag}_r", [128, 8])
        psS = ps(f"{tag}_pS", [128, 1024])
        psT = ps(f"{tag}_pT", [128, 1024], BF16)
        ps_e = ps(f"{tag}_pe", [64, 128])
        ps_o = ps(f"{tag}_po", [64, 128])

        bmask = sb(f"{tag}_bm", [128, 640]) if local_tiles else None
        s0 = Sched(kb, tag + "0")
        if local_tiles:
            s0.dma("sync", "ld", bmask[:], kb.dram["bandmask"])
        elif causal:
            s0.dma("sync", "ld", madd[:], kb.dram["madd"])
        s0.fence("sync", "ld")
        s0.emit()

        kv_all = sorted(set(kv_of_head(h) // 2 for h in range(nheads)))
        groups = []
        cur = []
        cur_kv = set()
        for p in range(npair):
            kvs = {kv_of_head(2 * p) // 2, kv_of_head(2 * p + 1) // 2}
            if len(cur_kv | kvs) > max_kch and cur:
                groups.append((cur, sorted(cur_kv)))
                cur, cur_kv = [], set()
            cur.append(p)
            cur_kv |= kvs
        groups.append((cur, sorted(cur_kv)))

        sc = Sched(kb, tag)
        uidx = 0
        for pairs, kvchunks in groups:
            kvpos = {kc: j for j, kc in enumerate(kvchunks)}
            assert all(kvchunks[j + 1] == kvchunks[j] + 1 for j in range(len(kvchunks) - 1))
            gate = [("pvE", sc.cur("pvE")), ("mmS", sc.cur("mmS"))]
            for j, kc in enumerate(kvchunks):
                sc.dma("sync", "kld", kres[:, j * T:(j + 1) * T],
                       kb.dram[kfm][kc * 128:(kc + 1) * 128, :], waits=gate)
            nvc = len(kvchunks) * 128
            sc.dma("sync", "kld",
                   vres[:, 0:NT * nvc].rearrange("p (i f) -> p i f", f=nvc),
                   kb.dram[vtm].rearrange("(i p) f -> p i f", p=128)[
                       :, :, kvchunks[0] * 128: kvchunks[0] * 128 + nvc],
                   waits=gate)
            kld_target = sc.cur("kld")

            for p in pairs:
                for qt in range(NT):
                    qb = (uidx % 2) * 128
                    sc.dma("sync", "qld", qt_sb[:, qb:qb + 128],
                           kb.dram[qfm][p * 128:(p + 1) * 128, qt * 128:(qt + 1) * 128],
                           waits=[("mmS", sc.cur("mmS"))])
                    qld_t = sc.cur("qld")
                    if local_tiles:
                        ks = max(0, qt - local_tiles)
                        ntile = qt - ks + 1
                    else:
                        ks = 0
                        ntile = (qt + 1) if causal else NT
                    strip = ntile * 128
                    nchunk = (strip + 511) // 512
                    for h2 in range(2):
                        h = 2 * p + h2
                        kv = kv_of_head(h)
                        kj = kvpos[kv // 2]
                        kbase = (kv % 2) * 64
                        qbase = h2 * 64
                        for cči in range(nchunk):
                            c0 = cči * 512
                            w = min(512, strip - c0)
                            pb = (cči % 2) * 512
                            def s_mm(e, c0=c0, w=w, pb=pb, qb=qb, qbase=qbase,
                                     kj=kj, kbase=kbase, ks=ks):
                                return e.matmul(
                                    psS[:, pb:pb + w],
                                    qt_sb[qbase:qbase + 64, qb:qb + 128],
                                    kres[kbase:kbase + 64,
                                         kj * T + ks * 128 + c0: kj * T + ks * 128 + c0 + w],
                                    start=True, stop=True)
                            sc.mm(s_mm, waits=[("kld", kld_target), ("qld", qld_t),
                                               ("evS", sc.cur("evS") - 1)], inc="mmS")
                            def e_ev(e, c0=c0, w=w, pb=pb):
                                return e.activation(S_sb[:, c0:c0 + w], psS[:, pb:pb + w],
                                                    AF.Copy, bias=0.0, scale=scale)
                            sc.act(e_ev, waits=[("mmS", sc.cur("mmS")),
                                                ("dvX", sc.cur("dvX"))], inc="evS")
                        if local_tiles:
                            def d_mask(e, strip=strip, ntile=ntile):
                                off = 640 - ntile * 128
                                e.tensor_tensor(S_sb[:, 0:strip], S_sb[:, 0:strip],
                                                bmask[:, off:640], ALU.add)
                                return e.tensor_reduce(rows[:, 0:1], S_sb[:, 0:strip],
                                                       AX.X, ALU.max, negate=True)
                        elif causal:
                            def d_mask(e, strip=strip):
                                e.tensor_tensor(S_sb[:, strip - 128:strip],
                                                S_sb[:, strip - 128:strip], madd[:], ALU.add)
                                return e.tensor_reduce(rows[:, 0:1], S_sb[:, 0:strip],
                                                       AX.X, ALU.max, negate=True)
                        else:
                            def d_mask(e, strip=strip):
                                return e.tensor_reduce(rows[:, 0:1], S_sb[:, 0:strip],
                                                       AX.X, ALU.max, negate=True)
                        sc.dve(d_mask, waits=[("evS", sc.cur("evS")),
                                              ("evE", sc.cur("evE"))], inc="dvM")
                        sc.act(lambda e, strip=strip: e.activation(
                            P_sb[:, 0:strip], S_sb[:, 0:strip], AF.Exp,
                            bias=rows[:, 0:1], scale=1.0, accum_out=rows[:, 1:2]),
                            waits=[("dvM", sc.cur("dvM")), ("mmT", sc.cur("mmT"))],
                            inc="evE")
                        def d_scale(e, strip=strip):
                            kb.ret(e, e.reciprocal(rows[:, 2:3], rows[:, 1:2]))
                            return e.tensor_scalar(P_sb[:, 0:strip], P_sb[:, 0:strip],
                                                   rows[:, 2:3], None, ALU.mult)
                        sc.dve(d_scale, waits=[("evE", sc.cur("evE"))], inc="dvX")
                        psO = ps_e if h2 == 0 else ps_o
                        for b0 in range(0, ntile, 8):
                            bn = min(8, ntile - b0)
                            def p_tr(e, b0=b0, bn=bn):
                                for j in range(bn):
                                    mm = e.transpose(psT[:, j * 128:(j + 1) * 128],
                                                     P_sb[:, (b0 + j) * 128:(b0 + j + 1) * 128],
                                                     g["ident"][:])
                                return mm
                            sc.mm(p_tr, waits=[("dvX", sc.cur("dvX")),
                                               ("dvT", sc.cur("dvT"))], inc="mmT")
                            def d_evT(e, bn=bn):
                                return e.tensor_copy(Pt_sb[:, 0:bn * 128], psT[:, 0:bn * 128])
                            sc.dve(d_evT, waits=[("mmT", sc.cur("mmT")),
                                                 ("pvB", sc.cur("pvB"))], inc="dvT")
                            def p_pv(e, b0=b0, bn=bn, kj=kj, kbase=kbase, psO=psO,
                                     ntile=ntile, nvc=nvc, ks=ks):
                                for j in range(bn):
                                    tk = b0 + j
                                    mm = e.matmul(
                                        psO[:],
                                        vres[:, (ks + tk) * nvc + kj * 128 + kbase:
                                                (ks + tk) * nvc + kj * 128 + kbase + 64],
                                        Pt_sb[:, j * 128:(j + 1) * 128],
                                        start=(tk == 0), stop=(tk == ntile - 1))
                                return mm
                            sc.mm(p_pv, waits=[("dvT", sc.cur("dvT")),
                                               ("pvE", sc.cur("pvE"))], inc="pvB")
                        def a_out(e, h2=h2, qt=qt, psO=psO, qbase=qbase):
                            return e.copy(outst[qbase:qbase + 64, qt * 128:(qt + 1) * 128],
                                          psO[:])
                        sc.act(a_out, waits=[("pvB", sc.cur("pvB")),
                                             ("osf", sc.cur("osf"))], inc="pvE")
                    uidx += 1
                sc.dma("gpsimd", "osf", kb.dram[attout][p * 128:(p + 1) * 128, :],
                       outst[:], waits=[("pvE", sc.cur("pvE"))])
        sc.fence("gpsimd", "osf")
        sc.emit()



def emit_l1aa_qkv(kb, g, src_fm="xnbuf"):
    nc = kb.nc
    with ExitStack() as _st:
        sb, ps = mk_tiles(nc, _st)
        xnr = sb("l1q_x", [128, 8 * T], BF16)
        latf = sb("l1q_l", [128, T], BF16)
        wkvd = sb("l1q_wd", [128, 8 * 128], BF16)
        wkvu = sb("l1q_wu", [128, 2048], BF16)
        wqr = sb("l1q_wq", [128, 8 * 1024], BF16)
        est = sb("l1q_e", [128, 2 * 512], BF16)
        vst = sb("l1q_v", [128, 2 * 512], BF16)
        psA = ps("l1q_pA", [128, 1024])
        psT = ps("l1q_pT", [128, 512], BF16)

        s0 = Sched(kb, "l1q0")
        for c in range(8):
            s0.dma("sync", "ld", xnr[:, c * T:(c + 1) * T],
                   kb.dram[src_fm][c * 128:(c + 1) * 128, PAD:PAD + T])
        s0.dma("sync", "ld", wkvd[:].rearrange("p (c f) -> p c f", c=8),
               kb.dram["bf_L1aa_kvd"].rearrange("(c p) f -> p c f", p=128))
        s0.dma("sync", "ld", wkvu[:], kb.dram["bf_L1aa_kvu"])
        s0.dma("sync", "ld", wqr[:].rearrange("p (c f) -> p c f", c=8),
               kb.dram["bf_L1aa_wq"].rearrange("(c p) f -> p c f", p=128))
        s0.fence("sync", "ld")
        s0.emit()

        sc = Sched(kb, "l1q")
        for tc in range(8):
            pb = (tc % 2) * 512
            def mmA(e, tc=tc, pb=pb):
                for c in range(8):
                    mm = e.matmul(psA[:, pb:pb + 512], wkvd[:, c * 128:(c + 1) * 128],
                                  xnr[:, c * T + tc * 512: c * T + (tc + 1) * 512],
                                  start=(c == 0), stop=(c == 7))
                return mm
            sc.mm(mmA, waits=[("evA", sc.cur("evA") - 1)], inc="mmA")
            sc.act(lambda e, tc=tc, pb=pb: e.copy(latf[:, tc * 512:(tc + 1) * 512],
                                                  psA[:, pb:pb + 512]),
                   waits=[("mmA", sc.cur("mmA"))], inc="evA")
        vidx = 0
        for fch in range(16):
            for tc in range(8):
                pb = (tc % 2) * 512
                eb = (tc % 2) * 512
                def mmB(e, fch=fch, tc=tc, pb=pb):
                    return e.matmul(psA[:, pb:pb + 512],
                                    wkvu[:, fch * 128:(fch + 1) * 128],
                                    latf[:, tc * 512:(tc + 1) * 512],
                                    start=True, stop=True)
                sc.mm(mmB, waits=[("evA", 8), ("evB", sc.cur("evB") - 1)], inc="mmB")
                if fch < 8:
                    sc.act(lambda e, pb=pb, eb=eb: e.copy(est[:, eb:eb + 512], psA[:, pb:pb + 512]),
                           waits=[("mmB", sc.cur("mmB")), sc.sw("stk", tc % 2)],
                           inc="evB")
                    sc.dma("gpsimd", f"stk{tc % 2}",
                           kb.dram["kfm"][fch * 128:(fch + 1) * 128, tc * 512:(tc + 1) * 512],
                           est[:, eb:eb + 512], waits=[("evB", sc.cur("evB"))])
                else:
                    sc.act(lambda e, pb=pb, eb=eb: e.copy(est[:, eb:eb + 512], psA[:, pb:pb + 512]),
                           waits=[("mmB", sc.cur("mmB")), ("mmT", sc.cur("mmT") - 1),
                                  ("stk0", 16 * 32), ("stk1", 16 * 32)],
                           inc="evB")
                    def trB(e, eb=eb):
                        for j in range(4):
                            mm = e.transpose(psT[:, j * 128:(j + 1) * 128],
                                             est[:, eb + j * 128: eb + j * 128 + 128],
                                             g["ident"][:])
                        return mm
                    sc.mm(trB, waits=[("evB", sc.cur("evB")),
                                      ("evT", sc.cur("evT"))], inc="mmT")
                    vb = (vidx % 2) * 512
                    def evT(e, vb=vb):
                        return e.tensor_copy(vst[:, vb:vb + 512], psT[:])
                    sc.dve(evT, waits=[("mmT", sc.cur("mmT")),
                                       sc.sw("stv", vidx % 2)], inc="evT")
                    for j in range(4):
                        sc.dma("gpsimd", f"stv{vidx % 2}",
                               kb.dram["vtm"][(tc * 4 + j) * 128:(tc * 4 + j + 1) * 128,
                                              (fch - 8) * 128:(fch - 7) * 128],
                               vst[:, vb + j * 128: vb + j * 128 + 128],
                               waits=[("evT", sc.cur("evT"))])
                    vidx += 1
        for fch in range(8):
            for tc in range(8):
                pb = (tc % 2) * 512
                eb = (tc % 2) * 512
                def mmC(e, fch=fch, tc=tc, pb=pb):
                    for c in range(8):
                        mm = e.matmul(psA[:, pb:pb + 512],
                                      wqr[:, c * 1024 + fch * 128: c * 1024 + (fch + 1) * 128],
                                      xnr[:, c * T + tc * 512: c * T + (tc + 1) * 512],
                                      start=(c == 0), stop=(c == 7))
                    return mm
                sc.mm(mmC, waits=[("evC", sc.cur("evC") - 1), ("evB", 128)], inc="mmC")
                sc.act(lambda e, pb=pb, eb=eb: e.copy(est[:, eb:eb + 512], psA[:, pb:pb + 512]),
                       waits=[("mmC", sc.cur("mmC")), sc.sw("stq", tc % 2),
                              ("stk0", 16 * 32), ("stk1", 16 * 32)],
                       inc="evC")
                sc.dma("gpsimd", f"stq{tc % 2}",
                       kb.dram["qfm"][fch * 128:(fch + 1) * 128, tc * 512:(tc + 1) * 512],
                       est[:, eb:eb + 512], waits=[("evC", sc.cur("evC"))])
        sc.fence("gpsimd", "stk0")
        sc.fence("gpsimd", "stk1")
        sc.fence("gpsimd", "stv0")
        sc.fence("gpsimd", "stv1")
        sc.fence("gpsimd", "stq0")
        sc.fence("gpsimd", "stq1")
        sc.emit()


def emit_L1_attn_a(kb, g, src, dst):
    emit_norm(kb, g, "n1a", src, dst_fm="xnbuf", kind="ln")
    emit_l1aa_qkv(kb, g)
    emit_flash2(kb, g, "f1a", "qfm", "kfm", "vtm", "attout",
                nheads=16, causal=True, scale=0.125)
    emit_proj_residual(kb, g, "p1a", "attout", 1024, "bf_L1aa_wo", src, dst)



def emit_l1ab_qkv(kb, g, src_fm="xnbuf"):
    nc = kb.nc
    with ExitStack() as _st:
        sb, ps = mk_tiles(nc, _st)
        xnr = sb("abq_x", [128, 8 * T], BF16)
        wres = sb("abq_w", [128, 2 * 8 * 128], BF16)
        bres = sb("abq_b", [128, 3 * 8])
        est = sb("abq_e", [128, 2 * 512], BF16)
        vst = sb("abq_v", [128, 2 * 512], BF16)
        psA = ps("abq_pA", [128, 1024])
        psT = ps("abq_pT", [128, 512], BF16)

        s0 = Sched(kb, "abq0")
        for c in range(8):
            s0.dma("sync", "ld", xnr[:, c * T:(c + 1) * T],
                   kb.dram[src_fm][c * 128:(c + 1) * 128, PAD:PAD + T])
        s0.dma("sync", "ld", bres[:, 0:8], kb.dram["L1ab_bq"])
        s0.dma("sync", "ld", bres[:, 8:16], kb.dram["L1ab_bk"])
        s0.dma("sync", "ld", bres[:, 16:24], kb.dram["L1ab_bv"])
        s0.fence("sync", "ld")
        s0.emit()

        sc = Sched(kb, "abq")
        vidx = 0
        widx = 0
        for wi, wnm in enumerate(["bf_L1ab_wq", "bf_L1ab_wk", "bf_L1ab_wv"]):
            for pr in range(8):
                wb = (widx % 2) * 1024
                sc.dma("sync", "wld",
                       wres[:, wb:wb + 1024].rearrange("p (c f) -> p c f", c=8),
                       kb.dram[wnm].rearrange("(n c p) f -> n p c f", c=8, p=128)[pr],
                       waits=[("mmP", sc.cur("mmP"))])
                wld_t = sc.cur("wld")
                widx += 1
                for tc in range(8):
                    pb = (tc % 2) * 512
                    eb = (tc % 2) * 512
                    def mmP(e, wb=wb, pb=pb, tc=tc):
                        for c in range(8):
                            mm = e.matmul(psA[:, pb:pb + 512],
                                          wres[:, wb + c * 128: wb + (c + 1) * 128],
                                          xnr[:, c * T + tc * 512: c * T + (tc + 1) * 512],
                                          start=(c == 0), stop=(c == 7))
                        return mm
                    sc.mm(mmP, waits=[("wld", wld_t), ("evP", sc.cur("evP") - 1)],
                          inc="mmP")
                    if wi < 2:
                        dstn = "qfm" if wi == 0 else "kfm"
                        sc.dve(lambda e, pb=pb, eb=eb, wi=wi, pr=pr: e.tensor_scalar(
                            est[:, eb:eb + 512], psA[:, pb:pb + 512],
                            bres[:, wi * 8 + pr: wi * 8 + pr + 1], None, ALU.add),
                            waits=[("mmP", sc.cur("mmP")),
                                   sc.sw("stk", tc % 2)], inc="evP")
                        sc.dma("gpsimd", f"stk{tc % 2}",
                               kb.dram[dstn][pr * 128:(pr + 1) * 128, tc * 512:(tc + 1) * 512],
                               est[:, eb:eb + 512], waits=[("evP", sc.cur("evP"))])
                    else:
                        sc.dve(lambda e, pb=pb, eb=eb, pr=pr: e.tensor_scalar(
                            est[:, eb:eb + 512], psA[:, pb:pb + 512],
                            bres[:, 16 + pr:16 + pr + 1], None, ALU.add),
                            waits=[("mmP", sc.cur("mmP")), ("mmT", sc.cur("mmT") - 1),
                                   ("stk0", 16 * 64), ("stk1", 16 * 64)], inc="evP")
                        def trB(e, eb=eb):
                            for j in range(4):
                                mm = e.transpose(psT[:, j * 128:(j + 1) * 128],
                                                 est[:, eb + j * 128: eb + j * 128 + 128],
                                                 g["ident"][:])
                            return mm
                        sc.mm(trB, waits=[("evP", sc.cur("evP")),
                                          ("evT", sc.cur("evT"))], inc="mmT")
                        vb = (vidx % 2) * 512
                        def evT(e, vb=vb):
                            return e.tensor_copy(vst[:, vb:vb + 512], psT[:])
                        sc.dve(evT, waits=[("mmT", sc.cur("mmT")),
                                           sc.sw("stv", vidx % 2)], inc="evT")
                        for j in range(4):
                            sc.dma("gpsimd", f"stv{vidx % 2}",
                                   kb.dram["vtm"][(tc * 4 + j) * 128:(tc * 4 + j + 1) * 128,
                                                  pr * 128:(pr + 1) * 128],
                                   vst[:, vb + j * 128: vb + j * 128 + 128],
                                   waits=[("evT", sc.cur("evT"))])
                        vidx += 1
        sc.fence("gpsimd", "stk0")
        sc.fence("gpsimd", "stk1")
        sc.fence("gpsimd", "stv0")
        sc.fence("gpsimd", "stv1")
        sc.emit()


def emit_L1_attn_b(kb, g, src, dst):
    emit_norm(kb, g, "n1b", src, dst_fm="xnbuf", kind="ln")
    emit_l1ab_qkv(kb, g)
    emit_flash2(kb, g, "f1b", "qfm", "kfm", "vtm", "attout",
                nheads=16, causal=False, scale=0.125)
    emit_proj_residual(kb, g, "p1b", "attout", 1024, "bf_L1ab_wo", src, dst,
                       bias_row="L1ab_bo")



def emit_l0f_qkv(kb, g, tag, src_fm="xnbuf"):
    nc = kb.nc
    with ExitStack() as _st:
        sb, ps = mk_tiles(nc, _st)
        xnr = sb(f"{tag}_x", [128, 8 * T], BF16)
        wq = sb(f"{tag}_wq", [128, 8 * 1024], BF16)
        wkv = sb(f"{tag}_wkv", [128, 8 * 512], BF16)
        rot = sb(f"{tag}_rot", [128, 128], BF16)
        tcos = sb(f"{tag}_tc", [128, T], BF16)
        tsin = sb(f"{tag}_ts", [128, T], BF16)
        qsb = sb(f"{tag}_q", [128, 2 * 512], BF16)
        tmp1 = sb(f"{tag}_t1", [128, 512])
        tmp2 = sb(f"{tag}_t2", [128, 512])
        est = sb(f"{tag}_e", [128, 2 * 512], BF16)
        vst = sb(f"{tag}_v", [128, 2 * 512], BF16)
        psA = ps(f"{tag}_pA", [128, 1024])
        psR = ps(f"{tag}_pR", [128, 512])
        psT = ps(f"{tag}_pT", [128, 512], BF16)

        s0 = Sched(kb, tag + "0")
        for c in range(8):
            s0.dma("sync", "ld", xnr[:, c * T:(c + 1) * T],
                   kb.dram[src_fm][c * 128:(c + 1) * 128, PAD:PAD + T])
        s0.dma("sync", "ld", wq[:].rearrange("p (c f) -> p c f", c=8),
               kb.dram["bf_L0f_wq"].rearrange("(c p) f -> p c f", p=128))
        s0.dma("sync", "ld", wkv[:, 0:8 * 256].rearrange("p (c f) -> p c f", c=8),
               kb.dram["bf_L0f_wk"].rearrange("(c p) f -> p c f", p=128))
        s0.dma("sync", "ld", wkv[:, 8 * 256:8 * 512].rearrange("p (c f) -> p c f", c=8),
               kb.dram["bf_L0f_wv"].rearrange("(c p) f -> p c f", p=128))
        s0.dma("sync", "ld", rot[:], kb.dram["bf_rot2"])
        s0.dma("sync", "ld", tcos[:], kb.dram["bf_ropecos"])
        s0.dma("sync", "ld", tsin[:], kb.dram["bf_ropesin"])
        s0.fence("sync", "ld")
        s0.emit()

        sc = Sched(kb, tag)

        def rope_block(fch, tc, wsl, dst_kind):
            pb = (sc.cur("mmQ") % 2) * 512
            qb = (sc.cur("mmQ") % 2) * 512
            buf, off = wsl
            def mmQ(e, pb=pb, off=off, tc=tc, buf=buf):
                for c in range(8):
                    mm = e.matmul(psA[:, pb:pb + 512],
                                  buf[:, off + c * (1024 if buf is wq else 256):
                                      off + c * (1024 if buf is wq else 256) + 128],
                                  xnr[:, c * T + tc * 512: c * T + (tc + 1) * 512],
                                  start=(c == 0), stop=(c == 7))
                return mm
            sc.mm(mmQ, waits=[("evQ", sc.cur("evQ") - 1)], inc="mmQ")
            sc.act(lambda e, pb=pb, qb=qb: e.copy(qsb[:, qb:qb + 512], psA[:, pb:pb + 512]),
                   waits=[("mmQ", sc.cur("mmQ")), ("dvQ", sc.cur("dvQ") - 1)], inc="evQ")
            def mmR(e, qb=qb):
                return e.matmul(psR[:], rot[:], qsb[:, qb:qb + 512], start=True, stop=True)
            sc.mm(mmR, waits=[("evQ", sc.cur("evQ")), ("dvQ", sc.cur("dvQ"))], inc="mmR")
            ri = sc.cur("dvQ")
            eb = (ri % 2) * 512
            def d_rope(e, qb=qb, tc=tc, eb=eb):
                e.tensor_tensor(tmp1[:], qsb[:, qb:qb + 512],
                                tcos[:, tc * 512:(tc + 1) * 512], ALU.mult)
                e.tensor_tensor(tmp2[:], psR[:], tsin[:, tc * 512:(tc + 1) * 512], ALU.mult)
                return e.tensor_tensor(est[:, eb:eb + 512], tmp1[:], tmp2[:], ALU.add)
            wsem = sc.sw("stq" if dst_kind == "q" else "stk", ri % 2)
            sc.dve(d_rope, waits=[("mmR", sc.cur("mmR")), wsem], inc="dvQ")
            return eb, ri % 2

        for fch in range(8):
            for tc in range(8):
                eb, ssl = rope_block(fch, tc, (wq, fch * 128), "q")
                sc.dma("gpsimd", f"stq{ssl}",
                       kb.dram["qfm"][fch * 128:(fch + 1) * 128, tc * 512:(tc + 1) * 512],
                       est[:, eb:eb + 512], waits=[("dvQ", sc.cur("dvQ"))])
        for fch in range(2):
            for tc in range(8):
                eb, ssl = rope_block(fch, tc, (wkv, fch * 128), "k")
                sc.dma("gpsimd", f"stk{ssl}",
                       kb.dram["kfm"][fch * 128:(fch + 1) * 128, tc * 512:(tc + 1) * 512],
                       est[:, eb:eb + 512], waits=[("dvQ", sc.cur("dvQ"))])
        vidx = 0
        for fch in range(2):
            for tc in range(8):
                pb = (tc % 2) * 512
                eb = (tc % 2) * 512
                def mmV(e, fch=fch, tc=tc, pb=pb):
                    for c in range(8):
                        mm = e.matmul(psA[:, pb:pb + 512],
                                      wkv[:, 8 * 256 + c * 256 + fch * 128:
                                          8 * 256 + c * 256 + (fch + 1) * 128],
                                      xnr[:, c * T + tc * 512: c * T + (tc + 1) * 512],
                                      start=(c == 0), stop=(c == 7))
                    return mm
                sc.mm(mmV, waits=[("evV", sc.cur("evV") - 1), ("dvQ", sc.cur("dvQ"))],
                      inc="mmV")
                sc.act(lambda e, pb=pb, eb=eb: e.copy(est[:, eb:eb + 512], psA[:, pb:pb + 512]),
                       waits=[("mmV", sc.cur("mmV")), ("mmT", sc.cur("mmT") - 1),
                              ("stq0", 16 * 32), ("stq1", 16 * 32),
                              sc.sw("stk", 0), sc.sw("stk", 1)], inc="evV")
                def trV(e, eb=eb):
                    for j in range(4):
                        mm = e.transpose(psT[:, j * 128:(j + 1) * 128],
                                         est[:, eb + j * 128: eb + j * 128 + 128],
                                         g["ident"][:])
                    return mm
                sc.mm(trV, waits=[("evV", sc.cur("evV")), ("evT", sc.cur("evT"))], inc="mmT")
                vb = (vidx % 2) * 512
                def evT(e, vb=vb):
                    return e.tensor_copy(vst[:, vb:vb + 512], psT[:])
                sc.dve(evT, waits=[("mmT", sc.cur("mmT")), sc.sw("stv", vidx % 2)],
                       inc="evT")
                for j in range(4):
                    sc.dma("gpsimd", f"stv{vidx % 2}",
                           kb.dram["vtm"][(tc * 4 + j) * 128:(tc * 4 + j + 1) * 128,
                                          fch * 128:(fch + 1) * 128],
                           vst[:, vb + j * 128: vb + j * 128 + 128],
                           waits=[("evT", sc.cur("evT"))])
                vidx += 1
        sc.fence("gpsimd", "stq0")
        sc.fence("gpsimd", "stq1")
        sc.fence("gpsimd", "stk0")
        sc.fence("gpsimd", "stk1")
        sc.fence("gpsimd", "stv0")
        sc.fence("gpsimd", "stv1")
        sc.emit()


def emit_l0f_router(kb, g, tag, src_fm="xnbuf", n_exp=4, gate_w="bf_L0f_gate",
                    comb_dst="comb", renorm=True, dbg_dst=None):
    nc = kb.nc
    with ExitStack() as _st:
        sb, ps = mk_tiles(nc, _st)
        xnr = sb(f"{tag}_x", [128, 2 * 1024], BF16)
        gw = sb(f"{tag}_g", [128, 8 * 128], BF16)
        lgb = sb(f"{tag}_l", [128, 2 * n_exp])
        zzb = sb(f"{tag}_z", [128, 2 * n_exp])
        kpb = sb(f"{tag}_k", [128, 2 * n_exp])
        rowsb = sb(f"{tag}_r", [128, 16])
        cmb = sb(f"{tag}_c", [128, 2 * n_exp])
        dbgt = sb(f"{tag}_d", [128, 16])
        psL = ps(f"{tag}_p", [128, 512])

        s0 = Sched(kb, tag + "0")
        s0.dma("sync", "ld", gw[:].rearrange("p (c f) -> p c f", c=8),
               kb.dram[gate_w].rearrange("(c p) f -> p c f", p=128))
        s0.fence("sync", "ld")
        s0.emit()

        sc = Sched(kb, tag)
        for i in range(NT):
            lg = lgb[:, 0:n_exp]
            zz = zzb[:, 0:n_exp]
            kp = kpb[:, 0:n_exp]
            rows = rowsb[:, 0:8]
            xb = (i % 2) * 1024
            sl = i % 2
            sc.dma("sync", f"ld{sl}",
                   xnr[:, xb:xb + 1024].rearrange("p (c f) -> p c f", c=8),
                   kb.dram[src_fm][:, PAD + i * 128:PAD + (i + 1) * 128].rearrange(
                       "(c p) f -> p c f", p=128),
                   waits=[("mmL", sc.cur("mmL") - 1)])
            def mmL(e, xb=xb):
                for c in range(8):
                    mm = e.matmul(psL[:, 0:128], xnr[:, xb + c * 128: xb + (c + 1) * 128],
                                  gw[:, c * 128:(c + 1) * 128],
                                  start=(c == 0), stop=(c == 7))
                return mm
            sc.mm(mmL, waits=[sc.sw("ld", sl), ("dvL", sc.cur("dvL"))], inc="mmL")

            def d_rt(e, lg=lg, rows=rows):
                kb.ret(e, e.tensor_copy(lg[:], psL[:, 0:n_exp]))
                return e.tensor_reduce(rows[:, 0:1], lg[:], AX.X, ALU.max, negate=True)
            sc.dve(d_rt, waits=[("mmL", sc.cur("mmL")), ("evL", sc.cur("evL"))], inc="dvL")
            sc.act(lambda e, lg=lg, zz=zz, rows=rows: e.activation(
                zz[:], lg[:], AF.Exp, bias=rows[:, 0:1], scale=1.0,
                accum_out=rows[:, 1:2]),
                   waits=[("dvL", sc.cur("dvL")), ("dvC", sc.cur("dvC"))], inc="evL")

            cb = (i % 2) * n_exp
            def d_cmb(e, cb=cb, lg=lg, zz=zz, kp=kp, rows=rows):
                R = kb.ret
                R(e, e.tensor_scalar(rows[:, 3:4], rows[:, 0:1], -1.0, None, ALU.mult))
                R(e, e.tensor_scalar(kp[:], lg[:], rows[:, 3:4], None, ALU.is_ge))
                R(e, e.scalar_tensor_tensor(kp[:], kp[:], NEG_BIG, lg[:], ALU.mult, ALU.add))
                R(e, e.tensor_reduce(rows[:, 4:5], kp[:], AX.X, ALU.max))
                R(e, e.tensor_scalar(kp[:], lg[:], rows[:, 4:5], None, ALU.is_ge))
                R(e, e.tensor_tensor(zz[:], zz[:], kp[:], ALU.mult))
                if renorm:
                    R(e, e.reduce_sum(rows[:, 6:7], zz[:], AX.X))
                    R(e, e.reciprocal(rows[:, 5:6], rows[:, 6:7]))
                else:
                    R(e, e.reciprocal(rows[:, 5:6], rows[:, 1:2]))
                return e.tensor_scalar(cmb[:, cb:cb + n_exp], zz[:], rows[:, 5:6],
                                       None, ALU.mult)
            sc.dve(d_cmb, waits=[("evL", sc.cur("evL")), sc.sw("stc", sl)], inc="dvC")
            sc.dma("gpsimd", f"stc{sl}", kb.dram[comb_dst][i * 128:(i + 1) * 128, 0:n_exp],
                   cmb[:, cb:cb + n_exp], waits=[("dvC", sc.cur("dvC"))])
            if dbg_dst is not None:
                def d_dbg(e, lg=lg, zz=zz, kp=kp, rows=rows):
                    e.tensor_copy(dbgt[:, 0:n_exp], lg[:])
                    e.tensor_copy(dbgt[:, 4:4 + n_exp], zz[:])
                    return e.tensor_copy(dbgt[:, 8:16], rows[:])
                sc.dve(d_dbg, waits=[sc.sw("stdb", sl)], inc="dvD")
                sc.dma("gpsimd", f"stdb{sl}", kb.dram[dbg_dst][i * 128:(i + 1) * 128, :],
                       dbgt[:], waits=[("dvD", sc.cur("dvD"))])
        sc.fence("gpsimd", "stc0")
        sc.fence("gpsimd", "stc1")
        sc.emit()


def emit_l0f_moe(kb, g, tag, src_fm="xnbuf", acc_dst="ybuf"):
    nc = kb.nc
    TC = 512
    NFC = 24
    with ExitStack() as _st:
        sb, ps = mk_tiles(nc, _st)
        w1r = sb(f"{tag}_w1", [128, 8 * 3072], BF16)
        w3r = sb(f"{tag}_w3", [128, 8 * 3072], BF16)
        w2r = sb(f"{tag}_w2", [128, NFC * 1024], BF16)
        xnr = sb(f"{tag}_x", [128, 8 * TC], BF16)
        asil = sb(f"{tag}_a", [128, NFC * TC], BF16)
        sil = sb(f"{tag}_s", [128, 2 * 512], BF16)
        cmb = sb(f"{tag}_c", [128, 16])
        acc = sb(f"{tag}_o", [128, 2 * 1024])
        psA = ps(f"{tag}_pA", [128, 1024])
        psB = ps(f"{tag}_pB", [128, 1024])
        psO = ps(f"{tag}_pO", [128, 2 * 1024])

        sc = Sched(kb, tag)
        for e_i in range(4):
            for wnm, wbuf in (("bf_L0f_w1", w1r), ("bf_L0f_w3", w3r)):
                sc.dma("sync", "wld",
                       wbuf[:].rearrange("p (c f) -> p c f", c=8),
                       kb.dram[wnm].rearrange("(n c p) f -> n p c f", c=8, p=128)[e_i],
                       waits=[("mmH", sc.cur("mmH"))])
            wld_t = sc.cur("wld")
            for tc in range(T // TC):
                sc.dma("sync", "xld",
                       xnr[:].rearrange("p (c f) -> p c f", c=8),
                       kb.dram[src_fm][:, PAD + tc * TC:PAD + (tc + 1) * TC].rearrange(
                           "(c p) f -> p c f", p=128),
                       waits=[("mmH", sc.cur("mmH") - 8), ("mmO", sc.cur("mmO"))])
                sc.dma("sync", "xld",
                       cmb[:].rearrange("p (i f) -> p i f", f=4),
                       kb.dram["comb"][tc * TC:(tc + 1) * TC, 0:4].rearrange(
                           "(i p) f -> p i f", p=128),
                       waits=[("dvO", sc.cur("dvO"))])
                xld_t = sc.cur("xld")
                for j4 in range(4):
                    sc.dma("sync", "w2ld",
                           w2r[:, j4 * 6 * 1024:(j4 + 1) * 6 * 1024].rearrange(
                               "p (c f) -> p c f", c=6),
                           kb.dram["bf_L0f_w2"].rearrange(
                               "(n c p) f -> n p c f", c=NFC, p=128)[e_i][:, j4 * 6:(j4 + 1) * 6],
                           waits=[("mmO", sc.cur("mmO"))])
                w2ld_t = sc.cur("w2ld")
                for fc in range(NFC):
                    pb = (fc % 2) * 512
                    def mmH1(e, fc=fc, pb=pb):
                        for c in range(8):
                            mm = e.matmul(psA[:, pb:pb + 512],
                                          w1r[:, c * 3072 + fc * 128: c * 3072 + (fc + 1) * 128],
                                          xnr[:, c * TC: c * TC + TC],
                                          start=(c == 0), stop=(c == 7))
                        return mm
                    sc.mm(mmH1, waits=[("wld", wld_t), ("xld", xld_t),
                                       ("evA", sc.cur("evA") - 1)], inc="mmH")
                    def mmH3(e, fc=fc, pb=pb):
                        for c in range(8):
                            mm = e.matmul(psB[:, pb:pb + 512],
                                          w3r[:, c * 3072 + fc * 128: c * 3072 + (fc + 1) * 128],
                                          xnr[:, c * TC: c * TC + TC],
                                          start=(c == 0), stop=(c == 7))
                        return mm
                    sc.mm(mmH3, waits=[("dvA", sc.cur("dvA") - 1)], inc="mmH")
                    sc.act(lambda e, pb=pb: e.activation(sil[:, pb:pb + 512],
                                                         psA[:, pb:pb + 512], AF.Silu),
                           waits=[("mmH", sc.cur("mmH"))], inc="evA")
                    sc.dve(lambda e, fc=fc, pb=pb: e.tensor_tensor(
                        asil[:, fc * TC: fc * TC + TC], sil[:, pb:pb + 512],
                        psB[:, pb:pb + 512], ALU.mult),
                        waits=[("evA", sc.cur("evA"))], inc="dvA")
                for tt in range(TC // 128):
                    ob = (tt % 2) * 1024
                    def mmO(e, tt=tt, ob=ob):
                        for fc in range(NFC):
                            for half in range(2):
                                mm = e.matmul(
                                    psO[:, ob + half * 512: ob + half * 512 + 512],
                                    asil[:, fc * TC + tt * 128: fc * TC + (tt + 1) * 128],
                                    w2r[:, fc * 1024 + half * 512: fc * 1024 + half * 512 + 512],
                                    start=(fc == 0), stop=(fc == NFC - 1))
                        return mm
                    sc.mm(mmO, waits=[("dvA", sc.cur("dvA")), ("w2ld", w2ld_t),
                                      ("dvO", sc.cur("dvO") - 1)], inc="mmO")
                    def d_out(e, tt=tt, ob=ob, e_i=e_i):
                        return e.tensor_scalar(acc[:, ob:ob + 1024], psO[:, ob:ob + 1024],
                                               cmb[:, tt * 4 + e_i: tt * 4 + e_i + 1],
                                               None, ALU.mult)
                    sc.dve(d_out, waits=[("mmO", sc.cur("mmO")),
                                         sc.sw("accst", tt % 2)],
                           inc="dvO")
                    sc.dma("gpsimd", f"accst{tt % 2}",
                           kb.dram[acc_dst][tc * TC + tt * 128: tc * TC + (tt + 1) * 128, :],
                           acc[:, ob:ob + 1024], accum_op=ALU.add,
                           waits=[("dvO", sc.cur("dvO")),
                                  ("accst0", 16 * 16 * e_i), ("accst1", 16 * 16 * e_i)])
        sc.fence("gpsimd", "accst0")
        sc.fence("gpsimd", "accst1")
        sc.emit()


def emit_stream_add(kb, g, tag, a, b, dst):
    nc = kb.nc
    with ExitStack() as _st:
        sb, ps = mk_tiles(nc, _st)
        at = sb(f"{tag}_a", [128, 2 * 1024])
        bt = sb(f"{tag}_b", [128, 2 * 1024])
        ot = sb(f"{tag}_o", [128, 2 * 1024])
        sc = Sched(kb, tag)
        for i in range(NT):
            ab = (i % 2) * 1024
            sl = i % 2
            sc.dma("sync", f"ld{sl}", at[:, ab:ab + 1024], kb.dram[a][i * 128:(i + 1) * 128, :],
                   waits=[("dv", i - 1)])
            sc.dma("sync", f"ld{sl}", bt[:, ab:ab + 1024], kb.dram[b][i * 128:(i + 1) * 128, :],
                   waits=[("dv", i - 1)])
            stw = sc.sw("st", sl)
            sc.dve(lambda e, ab=ab: e.tensor_tensor(ot[:, ab:ab + 1024], at[:, ab:ab + 1024],
                                                    bt[:, ab:ab + 1024], ALU.add),
                   waits=[sc.sw("ld", sl), stw], inc="dv")
            sc.dma("gpsimd", f"st{sl}", kb.dram[dst][i * 128:(i + 1) * 128, :],
                   ot[:, ab:ab + 1024], waits=[("dv", i + 1)])
        sc.fence("gpsimd", "st0")
        sc.fence("gpsimd", "st1")
        sc.emit()


def emit_L0_ffn(kb, g, src, dst, tag):
    emit_norm(kb, g, tag + "ln", src, dst_tm="ybuf", kind="ln")
    emit_norm(kb, g, tag + "r1", "ybuf", dst_fm="xnbuf", kind="rms", gain_row="L0f_g1")
    emit_l0f_qkv(kb, g, tag + "qk")
    emit_flash2(kb, g, tag + "fl", "qfm", "kfm", "vtm", "attout",
                nheads=16, causal=True, scale=0.125,
                kv_of_head=lambda h: h // 4)
    emit_proj_residual(kb, g, tag + "pj", "attout", 1024, "bf_L0f_wo",
                       "ybuf", "ybuf")
    emit_norm(kb, g, tag + "r2", "ybuf", dst_fm="xnbuf", kind="rms", gain_row="L0f_g2")
    emit_l0f_router(kb, g, tag + "rt")
    emit_l0f_moe(kb, g, tag + "mo")
    emit_stream_add(kb, g, tag + "sa", src, "ybuf", dst)



def emit_l0ab_hg(kb, g, tag, src_fm="xnbuf", dst="kfm"):
    nc = kb.nc
    with ExitStack() as _st:
        sb, ps = mk_tiles(nc, _st)
        xnr = sb(f"{tag}_x", [128, 8 * T], BF16)
        w1 = sb(f"{tag}_w", [128, 8 * 512], BF16)
        bcols = sb(f"{tag}_b", [128, 12])
        hsb = sb(f"{tag}_h", [128, 4 * 512])
        cent = sb(f"{tag}_ce", [128, 4 * 512])
        sq = sb(f"{tag}_sq", [128, 512])
        mrow = sb(f"{tag}_m", [1, 3 * 512])
        bc = sb(f"{tag}_bc", [128, 2 * 512])
        est = sb(f"{tag}_e", [128, 2 * 512], BF16)
        psA = ps(f"{tag}_pA", [128, 1024])
        psR = ps(f"{tag}_pR", [128, 512])

        s0 = Sched(kb, tag + "0")
        for c in range(8):
            s0.dma("sync", "ld", xnr[:, c * T:(c + 1) * T],
                   kb.dram[src_fm][c * 128:(c + 1) * 128, PAD:PAD + T])
        s0.dma("sync", "ld", w1[:].rearrange("p (c f) -> p c f", c=8),
               kb.dram["bf_L0ab_r1w"].rearrange("(c p) f -> p c f", p=128))
        s0.dma("sync", "ld", bcols[:, 0:4], kb.dram["L0ab_r1b"])
        s0.dma("sync", "ld", bcols[:, 4:8], kb.dram["L0ab_lng"])
        s0.dma("sync", "ld", bcols[:, 8:12], kb.dram["L0ab_lnb"])
        s0.fence("sync", "ld")
        s0.emit()

        sc = Sched(kb, tag)
        for tc in range(8):
            for fc in range(4):
                pb = (fc % 2) * 512
                def mmh(e, fc=fc, tc=tc, pb=pb):
                    for c in range(8):
                        mm = e.matmul(psA[:, pb:pb + 512],
                                      w1[:, c * 512 + fc * 128: c * 512 + (fc + 1) * 128],
                                      xnr[:, c * T + tc * 512: c * T + (tc + 1) * 512],
                                      start=(c == 0), stop=(c == 7))
                    return mm
                sc.mm(mmh, waits=[("evh", sc.cur("evh") - 1)], inc="mmh")
                sc.dve(lambda e, fc=fc, pb=pb: e.tensor_scalar(
                    hsb[:, fc * 512:(fc + 1) * 512], psA[:, pb:pb + 512],
                    bcols[:, fc:fc + 1], None, ALU.add),
                    waits=[("mmh", sc.cur("mmh")), ("evg", sc.cur("evg") - 1)], inc="evh")
            def mms(e):
                for fc in range(4):
                    mm = e.matmul(psR[0:1, 0:512], g["ones_c"][:],
                                  hsb[:, fc * 512:(fc + 1) * 512],
                                  start=(fc == 0), stop=(fc == 3))
                return mm
            sc.mm(mms, waits=[("evh", sc.cur("evh")), ("dvv", sc.cur("dvv"))], inc="mms")
            def d_mu(e):
                return e.tensor_scalar(mrow[0:1, 0:512], psR[0:1, 0:512],
                                       1.0 / 512.0, None, ALU.mult)
            sc.dve(d_mu, waits=[("mms", sc.cur("mms"))], inc="dmu")
            def mmb(e):
                return e.matmul(psR[:, 0:512], g["ones_r"][:], mrow[0:1, 0:512],
                                start=True, stop=True)
            sc.mm(mmb, waits=[("dmu", sc.cur("dmu"))], inc="mmb")
            sc.act(lambda e: e.copy(bc[:, 0:512], psR[:]),
                   waits=[("mmb", sc.cur("mmb"))], inc="evb")
            for fc in range(4):
                sc.dve(lambda e, fc=fc: e.tensor_tensor(
                    cent[:, fc * 512:(fc + 1) * 512], hsb[:, fc * 512:(fc + 1) * 512],
                    bc[:, 0:512], ALU.subtract),
                    waits=[("evb", sc.cur("evb"))], inc="dvc")
            for fc in range(4):
                sc.dve(lambda e, fc=fc: e.tensor_mul(sq[:], cent[:, fc * 512:(fc + 1) * 512],
                                                     cent[:, fc * 512:(fc + 1) * 512]),
                       waits=[("dvc", sc.cur("dvc")), ("mmv", sc.cur("mmv"))], inc="evq")
                def mmv(e, fc=fc):
                    return e.matmul(psR[0:1, 0:512], g["ones_c"][:], sq[:, 0:512],
                                    start=(fc == 0), stop=(fc == 3))
                sc.mm(mmv, waits=[("evq", sc.cur("evq"))], inc="mmv")
            def d_var(e):
                R = kb.ret
                R(e, e.tensor_scalar(mrow[0:1, 512:1024], psR[0:1, 0:512],
                                     1.0 / 512.0, None, ALU.mult))
                return e.tensor_copy(mrow[0:1, 1024:1536], mrow[0:1, 512:1024])
            sc.dve(d_var, waits=[("mmv", sc.cur("mmv"))], inc="dva")
            def a_rstd3(e):
                kb.ret(e, e.activation(mrow[0:1, 512:1024], mrow[0:1, 1024:1536],
                                       AF.Ln, bias=g["eps_ln"][0:1, :], scale=1.0))
                return e.activation(mrow[0:1, 512:1024], mrow[0:1, 512:1024],
                                    AF.Exp, scale=-0.5)
            sc.act(a_rstd3, waits=[("dva", sc.cur("dva"))], inc="eva")
            def mmb2(e):
                return e.matmul(psR[:, 0:512], g["ones_r"][:], mrow[0:1, 512:1024],
                                start=True, stop=True)
            sc.mm(mmb2, waits=[("eva", sc.cur("eva")), ("evb", sc.cur("evb"))], inc="mmb2")
            sc.act(lambda e: e.copy(bc[:, 512:1024], psR[:]),
                   waits=[("mmb2", sc.cur("mmb2"))], inc="evb2")
            for fc in range(4):
                hsl = sc.cur("evg") % 2
                eb = hsl * 512
                def d_hn(e, fc=fc):
                    e.tensor_tensor(cent[:, fc * 512:(fc + 1) * 512],
                                    cent[:, fc * 512:(fc + 1) * 512], bc[:, 512:1024],
                                    ALU.mult)
                    return e.tensor_scalar(cent[:, fc * 512:(fc + 1) * 512],
                                           cent[:, fc * 512:(fc + 1) * 512],
                                           bcols[:, 4 + fc:5 + fc], bcols[:, 8 + fc:9 + fc],
                                           ALU.mult, ALU.add)
                sc.dve(d_hn, waits=[("evb2", sc.cur("evb2"))], inc="dvv")
                sc.act(lambda e, fc=fc, eb=eb: e.activation(
                    est[:, eb:eb + 512], cent[:, fc * 512:(fc + 1) * 512], AF.Gelu),
                    waits=[("dvv", sc.cur("dvv")), sc.sw("sth", hsl)],
                    inc="evg")
                sc.dma("gpsimd", f"sth{hsl}",
                       kb.dram[dst][fc * 128:(fc + 1) * 128, tc * 512:(tc + 1) * 512],
                       est[:, eb:eb + 512], waits=[("evg", sc.cur("evg"))])
        sc.fence("gpsimd", "sth0")
        sc.fence("gpsimd", "sth1")
        sc.emit()


def emit_l0ab_router(kb, g, tag, hg_src="kfm"):
    nc = kb.nc
    n_exp = 6
    with ExitStack() as _st:
        sb, ps = mk_tiles(nc, _st)
        hgr = sb(f"{tag}_x", [128, 2 * 512], BF16)
        gw = sb(f"{tag}_g", [128, 4 * 128], BF16)
        lgb = sb(f"{tag}_l", [128, 2 * n_exp])
        zzb = sb(f"{tag}_z", [128, 2 * n_exp])
        kpb = sb(f"{tag}_k", [128, 2 * n_exp])
        rowsb = sb(f"{tag}_r", [128, 16])
        cmb = sb(f"{tag}_c", [128, 2 * n_exp])
        brow = sb(f"{tag}_br", [1, 16])
        bbc = sb(f"{tag}_bb", [128, 2 * n_exp + 2])
        psL = ps(f"{tag}_p", [128, 512])

        s0 = Sched(kb, tag + "0")
        s0.dma("sync", "ld", gw[:].rearrange("p (c f) -> p c f", c=4),
               kb.dram["bf_L0ab_r2w"].rearrange("(c p) f -> p c f", p=128))
        s0.dma("sync", "ld", brow[0:1, 0:6], kb.dram["L0ab_r2b"])
        s0.dma("sync", "ld", brow[0:1, 6:12], kb.dram["L0ab_rbias"])
        s0.dma("sync", "ld", brow[0:1, 12:13], kb.dram["L0ab_temp"])
        nld = s0.cur("ld")
        def d_t(e):
            return e.reciprocal(brow[0:1, 13:14], brow[0:1, 12:13])
        s0.dve(d_t, waits=[("ld", nld)], inc="dvt")
        def bcast(e):
            e.matmul(psL[:, 0:6], g["ones_r"][:], brow[0:1, 0:6], start=True, stop=True)
            e.matmul(psL[:, 6:12], g["ones_r"][:], brow[0:1, 6:12], start=True, stop=True)
            return e.matmul(psL[:, 12:13], g["ones_r"][:], brow[0:1, 13:14],
                            start=True, stop=True)
        s0.mm(bcast, waits=[("dvt", 1)], inc="mmb")
        s0.act(lambda e: e.copy(bbc[:, 0:13], psL[:, 0:13]), waits=[("mmb", 1)], inc="evb")
        s0.emit()

        sc = Sched(kb, tag)
        for i in range(NT):
            lg = lgb[:, (i % 2) * n_exp:(i % 2 + 1) * n_exp]
            zz = zzb[:, (i % 2) * n_exp:(i % 2 + 1) * n_exp]
            kp = kpb[:, (i % 2) * n_exp:(i % 2 + 1) * n_exp]
            rows = rowsb[:, (i % 2) * 8:(i % 2) * 8 + 8]
            xb = (i % 2) * 512
            sl = i % 2
            sc.dma("sync", f"ld{sl}",
                   hgr[:, xb:xb + 512].rearrange("p (c f) -> p c f", c=4),
                   kb.dram[hg_src][0:512, i * 128:(i + 1) * 128].rearrange(
                       "(c p) f -> p c f", p=128),
                   waits=[("mmL", sc.cur("mmL") - 1)])
            def mmL(e, xb=xb):
                for c in range(4):
                    mm = e.matmul(psL[:, 0:128], hgr[:, xb + c * 128: xb + (c + 1) * 128],
                                  gw[:, c * 128:(c + 1) * 128],
                                  start=(c == 0), stop=(c == 3))
                return mm
            sc.mm(mmL, waits=[sc.sw("ld", sl), ("dvL", sc.cur("dvL"))], inc="mmL")

            def d_rt(e, lg=lg, rows=rows):
                R = kb.ret
                R(e, e.tensor_copy(lg[:], psL[:, 0:n_exp]))
                R(e, e.tensor_tensor(lg[:], lg[:], bbc[:, 0:6], ALU.add))
                R(e, e.tensor_scalar(lg[:], lg[:], bbc[:, 12:13], None, ALU.mult))
                R(e, e.tensor_tensor(lg[:], lg[:], bbc[:, 6:12], ALU.add))
                return e.tensor_reduce(rows[:, 0:1], lg[:], AX.X, ALU.max, negate=True)
            sc.dve(d_rt, waits=[("mmL", sc.cur("mmL")), ("evL", sc.cur("evL"))],
                   inc="dvL")
            sc.act(lambda e, lg=lg, zz=zz, rows=rows: e.activation(
                zz[:], lg[:], AF.Exp, bias=rows[:, 0:1], scale=1.0,
                accum_out=rows[:, 1:2]),
                waits=[("dvL", sc.cur("dvL")), ("dvC", sc.cur("dvC"))], inc="evL")
            cb = (i % 2) * n_exp
            def d_cmb(e, cb=cb, lg=lg, zz=zz, kp=kp, rows=rows):
                R = kb.ret
                R(e, e.tensor_scalar(rows[:, 3:4], rows[:, 0:1], -1.0, None, ALU.mult))
                R(e, e.tensor_scalar(kp[:], lg[:], rows[:, 3:4], None, ALU.is_ge))
                R(e, e.scalar_tensor_tensor(kp[:], kp[:], NEG_BIG, lg[:], ALU.mult, ALU.add))
                R(e, e.tensor_reduce(rows[:, 4:5], kp[:], AX.X, ALU.max))
                R(e, e.tensor_scalar(kp[:], lg[:], rows[:, 4:5], None, ALU.is_ge))
                R(e, e.tensor_tensor(zz[:], zz[:], kp[:], ALU.mult))
                R(e, e.reciprocal(rows[:, 5:6], rows[:, 1:2]))
                return e.tensor_scalar(cmb[:, cb:cb + n_exp], zz[:], rows[:, 5:6],
                                       None, ALU.mult)
            sc.dve(d_cmb, waits=[("evL", sc.cur("evL")), sc.sw("stc", sl)], inc="dvC")
            sc.dma("gpsimd", f"stc{sl}", kb.dram["comb"][i * 128:(i + 1) * 128, 0:n_exp],
                   cmb[:, cb:cb + n_exp], waits=[("dvC", sc.cur("dvC"))])
        sc.fence("gpsimd", "stc0")
        sc.fence("gpsimd", "stc1")
        sc.emit()


def emit_l0ab_experts(kb, g, tag, src_fm="xnbuf", acc_dst="xres"):
    nc = kb.nc
    TC = 512
    NFC = 12
    with ExitStack() as _st:
        sb, ps = mk_tiles(nc, _st)
        w1r = sb(f"{tag}_w1", [128, 8 * 1536], BF16)
        w2r = sb(f"{tag}_w2", [128, NFC * 1024], BF16)
        b1c = sb(f"{tag}_b1", [128, 72])
        b2r = sb(f"{tag}_b2", [1, 1024], BF16)
        xnr = sb(f"{tag}_x", [128, 8 * TC], BF16)
        asil = sb(f"{tag}_a", [128, NFC * TC], BF16)
        cmb = sb(f"{tag}_c", [128, 4 * 8])
        acc = sb(f"{tag}_o", [128, 2 * 1024])
        psA = ps(f"{tag}_pA", [128, 1024])
        psO = ps(f"{tag}_pO", [128, 2 * 1024])

        s00 = Sched(kb, tag + "00")
        s00.dma("sync", "ld", b1c[:], kb.dram["L0ab_b1"])
        s00.fence("sync", "ld")
        s00.emit()

        sc = Sched(kb, tag)
        for e_i in range(6):
            act_fn = AF.Gelu if e_i % 2 == 0 else AF.Silu
            sc.dma("sync", "wld",
                   w1r[:].rearrange("p (c f) -> p c f", c=8),
                   kb.dram["bf_L0ab_W1"].rearrange("(n c p) f -> n p c f", c=8, p=128)[e_i],
                   waits=[("mmH", sc.cur("mmH"))])
            sc.dma("gpsimd", "bld", b2r[:],
                   kb.dram["L0ab_b2"][0:1, e_i * 1024:(e_i + 1) * 1024],
                   waits=[("mmO", sc.cur("mmO"))])
            wld_t = sc.cur("wld")
            bld_t = sc.cur("bld")
            for tc in range(T // TC):
                sc.dma("sync", "xld",
                       xnr[:].rearrange("p (c f) -> p c f", c=8),
                       kb.dram[src_fm][:, PAD + tc * TC:PAD + (tc + 1) * TC].rearrange(
                           "(c p) f -> p c f", p=128),
                       waits=[("mmH", sc.cur("mmH") - 8), ("mmO", sc.cur("mmO"))])
                sc.dma("sync", "xld",
                       cmb[:].rearrange("p (i f) -> p i f", f=8),
                       kb.dram["comb"][tc * TC:(tc + 1) * TC, :].rearrange(
                           "(i p) f -> p i f", p=128),
                       waits=[("dvO", sc.cur("dvO"))])
                xld_t = sc.cur("xld")
                for j2 in range(2):
                    sc.dma("sync", "w2ld",
                           w2r[:, j2 * 6 * 1024:(j2 + 1) * 6 * 1024].rearrange(
                               "p (c f) -> p c f", c=6),
                           kb.dram["bf_L0ab_W2"].rearrange(
                               "(n c p) f -> n p c f", c=NFC, p=128)[e_i][:, j2 * 6:(j2 + 1) * 6],
                           waits=[("mmO", sc.cur("mmO"))])
                w2ld_t = sc.cur("w2ld")
                for fc in range(NFC):
                    pb = (fc % 2) * 512
                    def mmH1(e, fc=fc, pb=pb):
                        for c in range(8):
                            mm = e.matmul(psA[:, pb:pb + 512],
                                          w1r[:, c * 1536 + fc * 128: c * 1536 + (fc + 1) * 128],
                                          xnr[:, c * TC: c * TC + TC],
                                          start=(c == 0), stop=(c == 7))
                        return mm
                    sc.mm(mmH1, waits=[("wld", wld_t), ("xld", xld_t),
                                       ("evA", sc.cur("evA") - 1)], inc="mmH")
                    sc.act(lambda e, fc=fc, pb=pb, e_i=e_i, act_fn=act_fn: e.activation(
                        asil[:, fc * TC: fc * TC + TC], psA[:, pb:pb + 512], act_fn,
                        bias=b1c[:, e_i * 12 + fc: e_i * 12 + fc + 1], scale=1.0),
                        waits=[("mmH", sc.cur("mmH")), ("mmO", sc.cur("mmO"))], inc="evA")
                for tt in range(TC // 128):
                    ob = (tt % 2) * 1024
                    def mmO(e, tt=tt, ob=ob):
                        for fc in range(NFC):
                            for half in range(2):
                                mm = e.matmul(
                                    psO[:, ob + half * 512: ob + half * 512 + 512],
                                    asil[:, fc * TC + tt * 128: fc * TC + (tt + 1) * 128],
                                    w2r[:, fc * 1024 + half * 512: fc * 1024 + half * 512 + 512],
                                    start=(fc == 0), stop=False)
                        for half in range(2):
                            mm = e.matmul(psO[:, ob + half * 512: ob + half * 512 + 512],
                                          g["ones_rb"][:],
                                          b2r[0:1, half * 512:(half + 1) * 512],
                                          start=False, stop=True)
                        return mm
                    sc.mm(mmO, waits=[("evA", sc.cur("evA")), ("w2ld", w2ld_t),
                                      ("bld", bld_t), ("dvO", sc.cur("dvO") - 1)],
                          inc="mmO")
                    def d_out(e, tt=tt, ob=ob, e_i=e_i):
                        return e.tensor_scalar(acc[:, ob:ob + 1024], psO[:, ob:ob + 1024],
                                               cmb[:, tt * 8 + e_i: tt * 8 + e_i + 1],
                                               None, ALU.mult)
                    sc.dve(d_out, waits=[("mmO", sc.cur("mmO")),
                                         sc.sw("accst", tt % 2)],
                           inc="dvO")
                    sc.dma("gpsimd", f"accst{tt % 2}",
                           kb.dram[acc_dst][tc * TC + tt * 128: tc * TC + (tt + 1) * 128, :],
                           acc[:, ob:ob + 1024], accum_op=ALU.add,
                           waits=[("dvO", sc.cur("dvO")),
                                  ("accst0", 16 * 16 * e_i), ("accst1", 16 * 16 * e_i)])
        sc.fence("gpsimd", "accst0")
        sc.fence("gpsimd", "accst1")
        sc.emit()


def emit_copy_stream(kb, g, tag, src, dst):
    sc = Sched(kb, tag)
    sc.dma("sync", "cp", kb.dram[dst], kb.dram[src])
    sc.fence("sync", "cp")
    sc.emit()


def emit_L0_attn_b(kb, g, src, dst, tag):
    emit_norm(kb, g, tag + "ln", src, dst_fm="xnbuf", kind="ln")
    emit_l0ab_hg(kb, g, tag + "hg")
    emit_l0ab_router(kb, g, tag + "rt")
    if src != dst:
        emit_copy_stream(kb, g, tag + "cp", src, dst)
    emit_l0ab_experts(kb, g, tag + "ex", acc_dst=dst)



def emit_tm_to_fm(kb, g, tag, src, dst_fm):
    nc = kb.nc
    with ExitStack() as _st:
        sb, ps = mk_tiles(nc, _st)
        xt = sb(f"{tag}_x", [128, 2 * C])
        xbf = sb(f"{tag}_b", [128, 2 * C], BF16)
        fms = sb(f"{tag}_f", [128, 2 * 4096], BF16)
        pst = ps(f"{tag}_p", [128, 2048], BF16)
        sc = Sched(kb, tag)
        for i in range(NT):
            b = (i % 2) * C
            sl = i % 2
            sc.dma("sync", f"ld{sl}", xt[:, b:b + C], kb.dram[src][i * 128:(i + 1) * 128, :],
                   waits=[("dv", i - 1)])
            sc.dve(lambda e, b=b: e.tensor_copy(xbf[:, b:b + C], xt[:, b:b + C]),
                   waits=[sc.sw("ld", sl), ("pe", i - 1)], inc="dv")
            gi = i // 4
            ti = i % 4
            pb = (i % 2) * 1024
            def trs(e, b=b, pb=pb):
                for j in range(8):
                    mm = e.transpose(pst[:, pb + j * 128: pb + j * 128 + 128],
                                     xbf[:, b + j * 128: b + j * 128 + 128],
                                     g["ident"][:])
                return mm
            sc.mm(trs, waits=[("dv", i + 1), ("ev", i - 1)], inc="pe")
            half = (gi % 2) * 4096
            def evac(e, half=half, ti=ti, pb=pb):
                dst = fms[:, half:half + 4096].rearrange(
                    "p (j f) -> p j f", f=512)[:, :, ti * 128:ti * 128 + 128]
                return e.tensor_copy(dst, pst[:, pb:pb + 1024].rearrange(
                    "p (j f) -> p j f", f=128))
            sc.dve(evac, waits=[("pe", i + 1), sc.sw("fm", gi % 2)], inc="ev")
            if ti == 3:
                for j in range(8):
                    sc.dma("gpsimd", f"fm{gi % 2}",
                           kb.dram[dst_fm][j * 128:(j + 1) * 128,
                                           PAD + gi * 512: PAD + (gi + 1) * 512],
                           fms[:, half + j * 512: half + j * 512 + 512],
                           waits=[("ev", i + 1)])
        sc.fence("gpsimd", "fm0")
        sc.fence("gpsimd", "fm1")
        sc.emit()


def emit_l1f_qkv(kb, g, tag, lay):
    nc = kb.nc
    with ExitStack() as _st:
        sb, ps = mk_tiles(nc, _st)
        yf = sb(f"{tag}_y", [128, 8 * 4104], BF16)
        wr = sb(f"{tag}_w", [128, 8 * 1536], BF16)
        tabs = sb(f"{tag}_t", [128, 4 * T], BF16)
        rot = sb(f"{tag}_ro", [128, 128], BF16)
        qscol = sb(f"{tag}_qs", [128, 2])
        qsb = sb(f"{tag}_q", [128, 512])
        sqsb = sb(f"{tag}_sq", [128, 512])
        qbf = sb(f"{tag}_qb", [128, 512], BF16)
        tmp1 = sb(f"{tag}_t1", [128, 512])
        tmp2 = sb(f"{tag}_t2", [128, 512])
        est = sb(f"{tag}_e", [128, 2 * 512], BF16)
        vst = sb(f"{tag}_v", [128, 2 * 512], BF16)
        rrow = sb(f"{tag}_r", [2, 512])
        psA = ps(f"{tag}_pA", [128, 1024])
        psR = ps(f"{tag}_pR", [128, 512])
        psB = ps(f"{tag}_pB", [128, 512])
        psT = ps(f"{tag}_pT", [128, 512], BF16)
        o2h = sb(f"{tag}_o2", [128, 2])
        s2h = sb(f"{tag}_s2", [2, 128])

        s0 = Sched(kb, tag + "0")
        for c in range(8):
            s0.dma("sync", "ld", yf[:, c * 4104:(c + 1) * 4104],
                   kb.dram["yfm"][c * 128:(c + 1) * 128, :])
        s0.dma("sync", "ld", wr[:].rearrange("p (c f) -> p c f", c=8),
               kb.dram["bf_L1f_wqkv"].rearrange("(n c p) f -> n p c f", c=8, p=128)[lay])
        for ti, tn in enumerate(["bf_xposcsA", "bf_xpossnA", "bf_xposcsB", "bf_xpossnB"]):
            s0.dma("sync", "ld", tabs[:, ti * T:(ti + 1) * T], kb.dram[tn])
        s0.dma("sync", "ld", rot[:], kb.dram["bf_rot2"])
        s0.dma("sync", "ld", qscol[:, 0:1], kb.dram["L1f_qs"][:, lay:lay + 1],
               allow_slow_non_contiguous=True)
        s0.dma("sync", "ld", qscol[:, 1:2], kb.dram["L1f_ks"][:, lay:lay + 1],
               allow_slow_non_contiguous=True)
        s0.dma("sync", "ld", o2h[:], kb.dram["ones2h"])
        s0.dma("sync", "ld", s2h[:], kb.dram["sel2h"])
        s0.fence("sync", "ld")
        s0.emit()

        sc = Sched(kb, tag)

        def qkv_mm(e, fch, tc):
            for c in range(8):
                off = PAD + tc * 512 - (1 if c >= 4 else 0)
                mm = e.matmul(psA[:, 0:512],
                              wr[:, c * 1536 + fch * 128: c * 1536 + (fch + 1) * 128],
                              yf[:, c * 4104 + off: c * 4104 + off + 512],
                              start=(c == 0), stop=(c == 7))
            return mm

        for fch in range(8):
            is_q = fch < 4
            dstn = "qfm" if is_q else "kfm"
            cs_off = (0 if is_q else 2) * T
            sn_off = (1 if is_q else 3) * T
            qcol = 0 if is_q else 1
            for tc in range(8):
                sc.mm(lambda e, fch=fch, tc=tc: qkv_mm(e, fch, tc),
                      waits=[("evq", sc.cur("evq"))], inc="mmq")
                sc.act(lambda e: e.copy(qsb[:], psA[:, 0:512]),
                       waits=[("mmq", sc.cur("mmq")), ("dvn", sc.cur("dvn"))], inc="evq")
                sc.dve(lambda e: e.tensor_mul(sqsb[:], qsb[:], qsb[:]),
                       waits=[("dvs", sc.cur("dvs")), ("evq", sc.cur("evq"))], inc="evsq")
                def mm_ss(e):
                    return e.matmul(psR[0:2, 0:512], o2h[:], sqsb[:], start=True, stop=True)
                sc.mm(mm_ss, waits=[("evsq", sc.cur("evsq")), ("dvn", sc.cur("dvn"))],
                      inc="mmss")
                def a_rn(e):
                    kb.ret(e, e.activation(rrow[0:2, :], psR[0:2, 0:512], AF.Ln))
                    return e.activation(rrow[0:2, :], rrow[0:2, :], AF.Exp, scale=-0.5)
                sc.act(a_rn, waits=[("mmss", sc.cur("mmss"))], inc="evr")
                def d_rn(e):
                    return e.tensor_scalar(rrow[0:2, :], rrow[0:2, :], 1e12, None, ALU.min)
                sc.dve(d_rn, waits=[("evr", sc.cur("evr"))], inc="dvr")
                def mm_bc(e):
                    return e.matmul(psB[:, 0:512], s2h[:], rrow[0:2, :],
                                    start=True, stop=True)
                sc.mm(mm_bc, waits=[("dvr", sc.cur("dvr")), ("dvs", sc.cur("dvs"))],
                      inc="mmbc")
                def d_qn(e, qcol=qcol):
                    return e.scalar_tensor_tensor(qbf[:], qsb[:], qscol[:, qcol:qcol + 1],
                                                  psB[:, 0:512], ALU.mult, ALU.mult)
                sc.dve(d_qn, waits=[("mmbc", sc.cur("mmbc"))], inc="dvs")
                def mm_rot(e):
                    return e.matmul(psR[:, 0:512], rot[:], qbf[:], start=True, stop=True)
                sc.mm(mm_rot, waits=[("dvs", sc.cur("dvs")), ("dvn", sc.cur("dvn"))],
                      inc="mmrot")
                qsl = sc.cur("dvn") % 2
                eb = qsl * 512
                def d_rope(e, tc=tc, eb=eb, cs_off=cs_off, sn_off=sn_off):
                    e.tensor_tensor(tmp1[:], qbf[:],
                                    tabs[:, cs_off + tc * 512: cs_off + (tc + 1) * 512],
                                    ALU.mult)
                    e.tensor_tensor(tmp2[:], psR[:, 0:512],
                                    tabs[:, sn_off + tc * 512: sn_off + (tc + 1) * 512],
                                    ALU.mult)
                    return e.tensor_tensor(est[:, eb:eb + 512], tmp1[:], tmp2[:], ALU.add)
                sc.dve(d_rope, waits=[("mmrot", sc.cur("mmrot")),
                                      sc.sw("stq", qsl)], inc="dvn")
                sc.dma("gpsimd", f"stq{qsl}",
                       kb.dram[dstn][(fch % 4) * 128:(fch % 4 + 1) * 128,
                                     tc * 512:(tc + 1) * 512],
                       est[:, eb:eb + 512], waits=[("dvn", sc.cur("dvn"))])
        vidx = 0
        for fch in range(8, 12):
            for tc in range(8):
                pb2 = (vidx % 2) * 512
                sc.mm(lambda e, fch=fch, tc=tc: qkv_mm(e, fch, tc),
                      waits=[("evv", sc.cur("evv")), ("dvn", sc.cur("dvn"))], inc="mmq")
                sc.act(lambda e, pb2=pb2: e.copy(est[:, pb2:pb2 + 512], psA[:, 0:512]),
                       waits=[("mmq", sc.cur("mmq")), ("mmtv", sc.cur("mmtv") - 1),
                              ("stq0", 16 * 32), ("stq1", 16 * 32)], inc="evv")
                def trv(e, pb2=pb2):
                    for j in range(4):
                        mm = e.transpose(psT[:, j * 128:(j + 1) * 128],
                                         est[:, pb2 + j * 128: pb2 + j * 128 + 128],
                                         g["ident"][:])
                    return mm
                sc.mm(trv, waits=[("evv", sc.cur("evv")), ("evtv", sc.cur("evtv"))],
                      inc="mmtv")
                vb = (vidx % 2) * 512
                def evtv(e, vb=vb):
                    return e.tensor_copy(vst[:, vb:vb + 512], psT[:])
                sc.dve(evtv, waits=[("mmtv", sc.cur("mmtv")), sc.sw("stv", vidx % 2)],
                       inc="evtv")
                for j in range(4):
                    sc.dma("gpsimd", f"stv{vidx % 2}",
                           kb.dram["vtm"][(tc * 4 + j) * 128:(tc * 4 + j + 1) * 128,
                                          (fch - 8) * 128:(fch - 7) * 128],
                           vst[:, vb + j * 128: vb + j * 128 + 128],
                           waits=[("evtv", sc.cur("evtv"))])
                vidx += 1
        sc.fence("gpsimd", "stq0")
        sc.fence("gpsimd", "stq1")
        sc.fence("gpsimd", "stv0")
        sc.fence("gpsimd", "stv1")
        sc.emit()


def emit_l1f_ffn(kb, g, tag, lay, stream="ybuf"):
    nc = kb.nc
    NFC = 22
    with ExitStack() as _st:
        sb, ps = mk_tiles(nc, _st)
        w1a = sb(f"{tag}_wa", [128, 8 * 2816], BF16)
        w1g = sb(f"{tag}_wg", [128, 8 * 2816], BF16)
        w2r = sb(f"{tag}_w2", [128, NFC * 1024], BF16)
        b1a = sb(f"{tag}_ba", [128, 22])
        b1g = sb(f"{tag}_bg", [128, 22])
        b2r = sb(f"{tag}_b2", [1, 1024], BF16)
        gbt = sb(f"{tag}_gb", [128, 2 * C])
        grow = sb(f"{tag}_gr", [1, 2 * C])
        ht = sb(f"{tag}_h", [128, C])
        hbf = sb(f"{tag}_hb", [128, C], BF16)
        sq = sb(f"{tag}_q", [128, C])
        st = sb(f"{tag}_s", [128, 8])
        hnf = sb(f"{tag}_hf", [128, 8 * 512], BF16)
        gsb = sb(f"{tag}_g", [128, 2 * 512], BF16)
        asil = sb(f"{tag}_a", [128, NFC * 512], BF16)
        ysb = sb(f"{tag}_y", [128, 1024])
        psA = ps(f"{tag}_pA", [128, 1024])
        psB = ps(f"{tag}_pB", [128, 1024])
        psT = ps(f"{tag}_pT", [128, 1024], BF16)
        psO = ps(f"{tag}_pO", [128, 1024])
        psG = psO

        s0 = Sched(kb, tag + "0")
        s0.dma("sync", "ld", w1a[:].rearrange("p (c f) -> p c f", c=8),
               kb.dram["bf_L1f_w1a"].rearrange("(n c p) f -> n p c f", c=8, p=128)[lay])
        s0.dma("sync", "ld", w1g[:].rearrange("p (c f) -> p c f", c=8),
               kb.dram["bf_L1f_w1g"].rearrange("(n c p) f -> n p c f", c=8, p=128)[lay])
        s0.dma("sync", "ld", b1a[:], kb.dram["L1f_b1a"][:, lay * 22:(lay + 1) * 22])
        s0.dma("sync", "ld", b1g[:], kb.dram["L1f_b1g"][:, lay * 22:(lay + 1) * 22])
        s0.dma("gpsimd", "ldb", b2r[:], kb.dram["L1f_b2"][lay:lay + 1, :])
        s0.fence("gpsimd", "ldb")
        s0.dma("sync", "ld", grow[0:1, 0:C], kb.dram["L1f_lng"][lay:lay + 1, :])
        s0.dma("sync", "ld", grow[0:1, C:2 * C], kb.dram["L1f_lnb"][lay:lay + 1, :])
        nld = s0.cur("ld")
        s0.mm(lambda e: (e.matmul(psG[:, 0:512], g["ones_r"][:], grow[0:1, 0:512],
                                  start=True, stop=True),
                         e.matmul(psG[:, 512:1024], g["ones_r"][:], grow[0:1, 512:1024],
                                  start=True, stop=True))[-1],
              waits=[("ld", nld)], inc="bcm")
        s0.act(lambda e: e.copy(gbt[:, 0:1024], psG[:]), waits=[("bcm", 1)], inc="bce")
        s0.mm(lambda e: (e.matmul(psG[:, 0:512], g["ones_r"][:], grow[0:1, 1024:1536],
                                  start=True, stop=True),
                         e.matmul(psG[:, 512:1024], g["ones_r"][:], grow[0:1, 1536:2048],
                                  start=True, stop=True))[-1],
              waits=[("bce", 1)], inc="bcm")
        s0.act(lambda e: e.copy(gbt[:, 1024:2048], psG[:]), waits=[("bcm", 2)], inc="bce")
        s0.emit()

        sc = Sched(kb, tag)
        for tc in range(8):
            for ti in range(4):
                i = tc * 4 + ti
                hb = 0
                sc.dma("sync", "hld", ht[:, hb:hb + 512],
                       kb.dram[stream][i * 128:(i + 1) * 128, 0:512],
                       waits=[("dvh", sc.cur("dvh"))])
                if i == 0:
                    sc.dma("sync", "hld", ht[1:128, hb + 512:hb + 1024],
                           kb.dram[stream][0:127, 512:1024], waits=[])
                else:
                    sc.dma("sync", "hld", ht[:, hb + 512:hb + 1024],
                           kb.dram[stream][i * 128 - 1:(i + 1) * 128 - 1, 512:1024],
                           waits=[])
                hld_t = sc.cur("hld")

                def d_ln2(e, hb=hb, i=i):
                    R = kb.ret
                    if i == 0:
                        kb.ret(e, e.memset(ht[0:1, hb + 512:hb + 1024], 0.0))
                    hv = ht[:, hb:hb + 1024]
                    R(e, e.reduce_sum(st[:, 0:1], hv, AX.X))
                    R(e, e.tensor_scalar(st[:, 1:2], st[:, 0:1], 1.0 / C, None, ALU.mult))
                    e.tensor_scalar(hv, hv, st[:, 1:2], None, ALU.subtract)
                    e.tensor_mul(sq[:], hv, hv)
                    R(e, e.reduce_sum(st[:, 2:3], sq[:], AX.X))
                    return e.tensor_scalar(st[:, 3:4], st[:, 2:3], 1.0 / C, None, ALU.mult)
                sc.dve(d_ln2, waits=[("hld", hld_t), ("evl", sc.cur("evl"))], inc="dvl")
                def a_rstd4(e):
                    kb.ret(e, e.activation(st[:, 4:5], st[:, 3:4], AF.Ln,
                                           bias=g["eps_ln"][:], scale=1.0))
                    return e.activation(st[:, 5:6], st[:, 4:5], AF.Exp, scale=-0.5)
                sc.act(a_rstd4, waits=[("dvl", sc.cur("dvl"))], inc="evl")
                def d_hn2(e, hb=hb):
                    hv = ht[:, hb:hb + 1024]
                    e.tensor_scalar(hv, hv, st[:, 5:6], None, ALU.mult)
                    e.tensor_tensor(hv, hv, gbt[:, 0:1024], ALU.mult)
                    return e.tensor_tensor(hbf[:], hv, gbt[:, 1024:2048], ALU.add)
                sc.dve(d_hn2, waits=[("evl", sc.cur("evl")), ("mmt", sc.cur("mmt"))],
                       inc="dvh")
                def tr_h(e, ti=ti):
                    for j in range(8):
                        mm = e.transpose(psT[:, j * 128:(j + 1) * 128],
                                         hbf[:, j * 128:(j + 1) * 128], g["ident"][:])
                    return mm
                sc.mm(tr_h, waits=[("dvh", sc.cur("dvh")), ("evt", sc.cur("evt"))],
                      inc="mmt")
                def ev_h(e, ti=ti):
                    dst = hnf[:].rearrange("p (j f) -> p j f", f=512)[:, :, ti * 128:ti * 128 + 128]
                    return e.tensor_copy(dst, psT[:].rearrange("p (j f) -> p j f", f=128))
                sc.dve(ev_h, waits=[("mmt", sc.cur("mmt")), ("mma", sc.cur("mma")),
                                    ("mmg", sc.cur("mmg"))],
                       inc="evt")
            for j2 in range(2):
                sc.dma("sync", "w2ld",
                       w2r[:, j2 * 11 * 1024:(j2 + 1) * 11 * 1024].rearrange(
                           "p (c f) -> p c f", c=11),
                       kb.dram["bf_L1f_w2"].rearrange(
                           "(n c p) f -> n p c f", c=NFC, p=128)[lay][:, j2 * 11:(j2 + 1) * 11],
                       waits=[("mmo", sc.cur("mmo"))])
            w2ld_t = sc.cur("w2ld")
            for fc in range(NFC):
                pb = (fc % 2) * 512
                def mmg(e, fc=fc, pb=pb):
                    for c in range(8):
                        mm = e.matmul(psB[:, pb:pb + 512],
                                      w1g[:, c * 2816 + fc * 128: c * 2816 + (fc + 1) * 128],
                                      hnf[:, c * 512:(c + 1) * 512],
                                      start=(c == 0), stop=(c == 7))
                    return mm
                sc.mm(mmg, waits=[("evt", 4 * (tc + 1)), ("evg", sc.cur("evg") - 1)],
                      inc="mmg")
                sc.act(lambda e, fc=fc, pb=pb: e.activation(
                    gsb[:, pb:pb + 512], psB[:, pb:pb + 512], AF.Gelu,
                    bias=b1g[:, fc:fc + 1], scale=1.0),
                    waits=[("mmg", sc.cur("mmg")), ("dva", sc.cur("dva") - 1)], inc="evg")
                def mma(e, fc=fc, pb=pb):
                    for c in range(8):
                        mm = e.matmul(psA[:, pb:pb + 512],
                                      w1a[:, c * 2816 + fc * 128: c * 2816 + (fc + 1) * 128],
                                      hnf[:, c * 512:(c + 1) * 512],
                                      start=(c == 0), stop=(c == 7))
                    return mm
                sc.mm(mma, waits=[("dva", sc.cur("dva") - 1)], inc="mma")
                def d_a(e, fc=fc, pb=pb):
                    return e.scalar_tensor_tensor(
                        asil[:, fc * 512:(fc + 1) * 512], psA[:, pb:pb + 512],
                        b1a[:, fc:fc + 1], gsb[:, pb:pb + 512],
                        ALU.add, ALU.mult)
                sc.dve(d_a, waits=[("mma", sc.cur("mma")), ("evg", sc.cur("evg")),
                                   ("mmo", sc.cur("mmo"))], inc="dva")
            for ti in range(4):
                i = tc * 4 + ti
                ob = 0
                sc.dma("sync", "yld", ysb[:, ob:ob + 1024],
                       kb.dram[stream][i * 128:(i + 1) * 128, :],
                       waits=[("sty", 16 * (sc.cur("sty") // 16))])
                yld_t = sc.cur("yld")
                def mmo(e, ti=ti, ob=ob):
                    for fc in range(NFC):
                        for half in range(2):
                            mm = e.matmul(
                                psO[:, half * 512:(half + 1) * 512],
                                asil[:, fc * 512 + ti * 128: fc * 512 + (ti + 1) * 128],
                                w2r[:, fc * 1024 + half * 512: fc * 1024 + half * 512 + 512],
                                start=(fc == 0), stop=False)
                    for half in range(2):
                        mm = e.matmul(psO[:, half * 512:(half + 1) * 512],
                                      g["ones_rb"][:], b2r[0:1, half * 512:(half + 1) * 512],
                                      start=False, stop=True)
                    return mm
                sc.mm(mmo, waits=[("dva", sc.cur("dva")), ("w2ld", w2ld_t),
                                  ("dvo", sc.cur("dvo"))], inc="mmo")
                def d_res(e, ob=ob):
                    return e.tensor_tensor(ysb[:, ob:ob + 1024], psO[:],
                                           ysb[:, ob:ob + 1024], ALU.add)
                sc.dve(d_res, waits=[("mmo", sc.cur("mmo")), ("yld", yld_t),
                                     ("sty", 16 * (sc.cur("sty") // 16))], inc="dvo")
                sc.dma("gpsimd", "sty", kb.dram[stream][i * 128:(i + 1) * 128, :],
                       ysb[:, ob:ob + 1024], waits=[("dvo", sc.cur("dvo"))])
        sc.fence("gpsimd", "sty")
        sc.emit()


def emit_L1_ffn(kb, g, src, dst, tag):
    emit_norm(kb, g, tag + "ln", src, dst_tm="ybuf", dst_fm="yfm", kind="ln")
    for lay in range(2):
        emit_l1f_qkv(kb, g, f"{tag}q{lay}", lay)
        emit_flash2(kb, g, f"{tag}f{lay}", "qfm", "kfm", "vtm", "attout",
                    nheads=8, causal=True, scale=8.0, local_tiles=4)
        emit_proj_residual(kb, g, f"{tag}p{lay}", "attout", 512, "bf_L1f_wo",
                           "ybuf", "ybuf", w_row0=lay * 512)
        emit_tm_to_fm(kb, g, f"{tag}c{lay}", "ybuf", "yfm")
        emit_l1f_ffn(kb, g, f"{tag}m{lay}", lay)
        if lay == 0:
            emit_tm_to_fm(kb, g, f"{tag}d{lay}", "ybuf", "yfm")
    emit_stream_add(kb, g, tag + "sa", src, "ybuf", dst)



def emit_flash2(kb, g, tag, qfm, kfm, vtm, attout, nheads=16, causal=True,
                scale=0.125, kv_of_head=None, local_tiles=None):
    import os as _os
    HEAVY = _os.environ.get("FLASH_HEAVY", "") == "1"
    HEAVY_A = HEAVY or _os.environ.get("FLASH_HA", "") == "1"
    HEAVY_B = HEAVY or _os.environ.get("FLASH_HB", "") == "1"
    """Pass 1: negmax per (head, q-tile) from S in [tq, tk] orientation.
    Pass 2: S^T = k_aug^T q_aug per tk-tile (K=65 with ones-row in k and
    negmax-row in q -> S - max baked in), exp straight from PSUM (scale in
    exp), PV with ones-column in V for free denominators, normalize at the
    end. attout rows h*64 feature-major."""
    nc = kb.nc
    if kv_of_head is None:
        kv_of_head = lambda h: h
    NTQC = NT // 4
    with ExitStack() as _st:
        sb, ps = mk_tiles(nc, _st)
        kres = sb(f"{tag}_k", [128, 4 * T], BF16)
        vres = sb(f"{tag}_v", [128, 4 * NT * 65], BF16)
        qa = sb(f"{tag}_q", [128, 2 * 512], BF16)
        q1 = sb(f"{tag}_q1", [128, 2 * 128], BF16)
        Pt = sb(f"{tag}_P", [128, 2 * 512], BF16)
        osb = sb(f"{tag}_o", [128, 512])
        rrow = sb(f"{tag}_r", [1, 512])
        outst = sb(f"{tag}_os", [64, T], BF16)
        negm = sb(f"{tag}_n", [128, 16 * NT])
        mx = sb(f"{tag}_m", [128, 8])
        masks = sb(f"{tag}_msk", [128, 8 * 512])
        psS = ps(f"{tag}_pS", [128, 1024])
        psO = ps(f"{tag}_pO", [128, 512])
        psR = ps(f"{tag}_pR", [1, 512])
        psB = ps(f"{tag}_pB", [128, 512])

        m1 = sb(f"{tag}_m1", [128, 640])
        s0 = Sched(kb, tag + "0")
        if local_tiles:
            for k in range(8):
                s0.dma("sync", "ld", masks[:, k * 512:(k + 1) * 512],
                       kb.dram[f"lmaskT_{k}"])
            s0.dma("sync", "ld", m1[:], kb.dram["bandmask"])
        elif causal:
            for d in range(4):
                s0.dma("sync", "ld", masks[:, d * 512:(d + 1) * 512],
                       kb.dram[f"cmaskT_{d}"])
            s0.dma("sync", "ld", m1[:, 0:128], kb.dram["madd"])
        s0.fence("sync", "ld")
        s0.emit()

        groups = []
        cur, cur_kv = [], []
        for h in range(nheads):
            kv = kv_of_head(h)
            nk = cur_kv + ([kv] if kv not in cur_kv else [])
            if len(nk) > 4 and cur:
                groups.append((cur, cur_kv))
                cur, cur_kv = [], []
                nk = [kv]
            cur.append(h)
            cur_kv = nk
        groups.append((cur, cur_kv))

        sc = Sched(kb, tag)
        for heads, kvs in groups:
            kvpos = {kv: j for j, kv in enumerate(kvs)}
            gate = [("pvm", sc.cur("pvm")), ("mmS", sc.cur("mmS")),
                    ("mm1", sc.cur("mm1"))]
            for j, kv in enumerate(kvs):
                sc.dma("sync", "kld", kres[0:64, j * T:(j + 1) * T],
                       kb.dram[kfm][kv * 64:(kv + 1) * 64, :], waits=gate)
                sc.dma("sync", "kld",
                       vres[:, j * NT * 65:(j + 1) * NT * 65].rearrange(
                           "p (i c) -> p i c", c=65)[:, :, 0:64],
                       kb.dram[vtm][:, kv * 64:(kv + 1) * 64].rearrange(
                           "(i p) c -> p i c", p=128),
                       waits=[])
            kld_t = sc.cur("kld")
            def inst_ones(e, nkv=len(kvs)):
                e.memset(kres[64:65, 0:nkv * T], 1.0)
                return e.memset(
                    vres[:, 0:nkv * NT * 65].rearrange("p (i c) -> p i c", c=65)[:, :, 64:65],
                    1.0)
            sc.dve(inst_ones, waits=gate + [("kld", kld_t)], inc="ons")

            for h in heads:
                kj = kvpos[kv_of_head(h)]
                hh = h - heads[0]
                for qt in range(NT):
                    u1 = sc.cur("mx1")
                    qb = (sc.cur("q1l") // 16 % 2) * 128
                    sc.dma("sync", "q1l", q1[0:64, qb:qb + 128],
                           kb.dram[qfm][h * 64:(h + 1) * 64, qt * 128:(qt + 1) * 128],
                           waits=[("mm1", sc.cur("mm1"))])
                    q1l_t = sc.cur("q1l")
                    if local_tiles:
                        ks = max(0, qt - local_tiles)
                        strip = (qt - ks + 1) * 128
                    else:
                        ks = 0
                        strip = ((qt + 1) if causal else NT) * 128
                    nchunk = (strip + 511) // 512
                    for ci in range(nchunk):
                        c0 = ci * 512
                        w = min(512, strip - c0)
                        pb = (sc.cur("mm1") % 2) * 512
                        def s1mm(e, c0=c0, w=w, pb=pb, qb=qb, kj=kj, ks=ks):
                            return e.matmul(
                                psS[:, pb:pb + w], q1[0:64, qb:qb + 128],
                                kres[0:64, kj * T + ks * 128 + c0:
                                     kj * T + ks * 128 + c0 + w],
                                start=True, stop=True)
                        sc.mm(s1mm, waits=[("q1l", q1l_t), ("ons", sc.cur("ons")),
                                           ("dmx", sc.cur("dmx") - 1)], inc="mm1")
                        def d_mr(e, ci=ci, pb=pb, w=w, c0=c0, strip=strip):
                            if local_tiles:
                                off = 640 - strip
                                e.tensor_tensor(psS[:, pb:pb + w], psS[:, pb:pb + w],
                                                m1[:, off + c0: off + c0 + w], ALU.add)
                            elif causal and c0 + w == strip:
                                e.tensor_tensor(psS[:, pb + w - 128: pb + w],
                                                psS[:, pb + w - 128: pb + w],
                                                m1[:, 0:128], ALU.add)
                            return e.tensor_reduce(
                                mx[:, ci:ci + 1], psS[:, pb:pb + w], AX.X, ALU.max)
                        sc.dve(d_mr, waits=[("mm1", sc.cur("mm1"))], inc="dmx")
                    def d_fm(e, nchunk=nchunk, h=h, hh=hh, qt=qt):
                        return e.tensor_reduce(negm[:, hh * NT + qt: hh * NT + qt + 1],
                                               mx[:, 0:nchunk], AX.X, ALU.max,
                                               negate=True)
                    sc.dve(d_fm, waits=[("dmx", sc.cur("dmx")),
                                        ("mmN", sc.cur("mmN"))], inc="ngm")

            ngm_t = sc.cur("ngm")

            for h in heads:
                kj = kvpos[kv_of_head(h)]
                hh = h - heads[0]
                for tqc in range(NTQC):
                    qb = (sc.cur("qld") // 16 % 2) * 512
                    sc.dma("sync", "qld", qa[0:64, qb:qb + 512],
                           kb.dram[qfm][h * 64:(h + 1) * 64, tqc * 512:(tqc + 1) * 512],
                           waits=[("mmS", sc.cur("mmS"))])
                    qld_t = sc.cur("qld")
                    def t_n(e, hh=hh, tqc=tqc):
                        for s in range(4):
                            mm = e.transpose(psR[0:1, s * 128:(s + 1) * 128],
                                             negm[:, hh * NT + tqc * 4 + s:
                                                  hh * NT + tqc * 4 + s + 1],
                                             g["identf"][:])
                        return mm
                    sc.mm(t_n, waits=[("ngm", ngm_t), ("dvq", sc.cur("dvq"))], inc="mmN")
                    sc.dve(lambda e, qb=qb: e.tensor_copy(qa[64:65, qb:qb + 512],
                                                          psR[0:1, :]),
                           waits=[("mmN", sc.cur("mmN")), ("qld", qld_t)], inc="dvq")
                    dvq_t = sc.cur("dvq")
                    if local_tiles:
                        t_lo = max(0, tqc * 4 - local_tiles)
                        t_hi = tqc * 4 + 3
                    else:
                        t_lo = 0
                        t_hi = (tqc * 4 + 3) if causal else NT - 1
                    ntile = t_hi - t_lo + 1
                    first = True
                    for tk in range(t_lo, t_hi + 1):
                        pb = (sc.cur("mmS") % 2) * 512
                        def s2mm(e, tk=tk, pb=pb, qb=qb, kj=kj):
                            return e.matmul(
                                psS[:, pb:pb + 512],
                                kres[0:65, kj * T + tk * 128: kj * T + (tk + 1) * 128],
                                qa[0:65, qb:qb + 512],
                                start=True, stop=True)
                        sc.mm(s2mm, waits=[("dvq", dvq_t),
                                           ("evP", sc.cur("evP") - (0 if HEAVY_A else 1))],
                              inc="mmS")
                        d = tk - tqc * 4
                        if local_tiles:
                            mi = d + 4
                            need_mask = True
                        else:
                            mi = d
                            need_mask = causal and 0 <= d <= 3
                        if need_mask:
                            sc.dve(lambda e, pb=pb, mi=mi: e.tensor_tensor(
                                psS[:, pb:pb + 512], psS[:, pb:pb + 512],
                                masks[:, mi * 512:(mi + 1) * 512], ALU.add),
                                waits=[("mmS", sc.cur("mmS"))], inc="dvm")
                        ptb = (sc.cur("evP") % 2) * 512
                        def e_exp(e, pb=pb, ptb=ptb, scale=scale):
                            return e.activation(Pt[:, ptb:ptb + 512], psS[:, pb:pb + 512],
                                                AF.Exp, scale=scale)
                        sc.act(e_exp, waits=[("dvm", sc.cur("dvm")),
                                             ("mmS", sc.cur("mmS")),
                                             ("pvm", sc.cur("pvm") - (0 if HEAVY_B else 1))],
                               inc="evP")
                        def pvmm(e, tk=tk, ptb=ptb, kj=kj, first=first,
                                 last=(tk == t_hi)):
                            return e.matmul(
                                psO[0:65, :],
                                vres[:, kj * NT * 65 + tk * 65: kj * NT * 65 + (tk + 1) * 65],
                                Pt[:, ptb:ptb + 512],
                                start=first, stop=last)
                        sc.mm(pvmm, waits=[("evP", sc.cur("evP")),
                                           ("evO", sc.cur("evO"))], inc="pvm")
                        first = False
                    sc.act(lambda e: e.copy(osb[0:65, :], psO[0:65, :]),
                           waits=[("pvm", sc.cur("pvm")), ("dvf", sc.cur("dvf"))],
                           inc="evO")
                    def d_rr(e):
                        return e.reciprocal(rrow[:], osb[64:65, :])
                    sc.dve(d_rr, waits=[("evO", sc.cur("evO")), ("bcm", sc.cur("bcm"))],
                           inc="dvr")
                    def b_mm(e):
                        return e.matmul(psB[:, 0:512], g["ones_r"][:], rrow[:],
                                        start=True, stop=True)
                    sc.mm(b_mm, waits=[("dvr", sc.cur("dvr")), ("dvf", sc.cur("dvf"))],
                          inc="bcm")
                    def d_fin(e, tqc=tqc):
                        return e.tensor_tensor(outst[0:64, tqc * 512:(tqc + 1) * 512],
                                               osb[0:64, :], psB[0:64, 0:512], ALU.mult)
                    sc.dve(d_fin, waits=[("bcm", sc.cur("bcm")), ("ost", sc.cur("ost"))],
                           inc="dvf")
                sc.dma("gpsimd", "ost", kb.dram[attout][h * 64:(h + 1) * 64, :],
                       outst[0:64, :], waits=[("dvf", sc.cur("dvf"))])
        sc.fence("gpsimd", "ost")
        sc.emit()
